# version 10
# baseline (speedup 1.0000x reference)
import numpy as np
import ml_dtypes
from contextlib import ExitStack

import concourse.bass as bass
import concourse.mybir as mybir
from concourse.bass_utils import run_bass_kernel_spmd
from concourse.bass_core_rust import IndirectOffsetOnAxis

F32 = mybir.dt.float32
BF16 = mybir.dt.bfloat16
I32 = mybir.dt.int32
U8 = mybir.dt.uint8
AF = mybir.ActivationFunctionType
ALU = mybir.AluOpType
BF = ml_dtypes.bfloat16

P = 128
D = 1024
TPC = 4096
SEQ = 8192
LN_EPS = 1e-5
PG = [[0, 1], [2, 3], [4, 5], [6, 7]]
AG8 = [list(range(8))]


class Sem:
    def __init__(self, ctx, nc, name):
        self.h = ctx.enter_context(nc.semaphore(name))
        self.n = 0

    def inc(self, inst, k=1):
        inst.then_inc(self.h, k)
        self.n += k
        return inst

    def dma(self, inst):
        return self.inc(inst, 16)

    def wait(self, eng, lag=0):
        v = self.n - lag
        if v > 0:
            eng.wait_ge(self.h, v)

    def wait_at(self, eng, v):
        if v > 0:
            eng.wait_ge(self.h, v)


def fm_ap(dram_ap, d_total, t0, nt):
    return dram_ap[:, t0:t0 + nt].rearrange("(kt p) t -> p kt t", p=P)


SEMS = ("ld", "st", "sc", "ve", "te", "vv", "gi", "tp")


def barrier(S, ccs=()):
    blk = S["blk"]
    for engname in ("tensor", "vector", "scalar", "gpsimd", "sync"):
        dec = getattr(blk, engname)

        def _mk(eng):
            for k in SEMS:
                S[k].wait(eng)
            for h in ccs:
                eng.wait_ge(h, 1)
        dec(_mk)


def emit_ln(nc, S, xc, kt_n, nt, ones_col, ones_row, p_sum, p_sq, p_mu, p_rstd,
            xsq, murow, qrow, mu2, varrow, rstdrow, eps_ap, mu_b, rstd_b,
            gates=("ld",), stats_to_psum=False):
    blk = S["blk"]
    dtot = kt_n * P

    @blk.vector
    def _(ve):
        for g in gates:
            S[g].wait(ve)
        for kt in range(kt_n):
            inst = ve.tensor_mul(xsq[:, kt, :], xc[:, kt, :], xc[:, kt, :])
        S["ve"].inc(inst)

    @blk.tensor
    def _(te):
        for g in gates:
            S[g].wait(te)
        for kt in range(kt_n):
            te.matmul(p_sum[:, :], ones_col[:, :], xc[:, kt, :],
                      start=(kt == 0), stop=(kt == kt_n - 1))
        S["ve"].wait(te)
        for kt in range(kt_n):
            inst = te.matmul(p_sq[:, :], ones_col[:, :], xsq[:, kt, :],
                             start=(kt == 0), stop=(kt == kt_n - 1))
        S["te"].inc(inst)

    @blk.scalar
    def _(sc):
        S["te"].wait(sc)
        sc.mul(murow[:, :], p_sum[:, :], 1.0 / dtot)
        inst = sc.mul(qrow[:, :], p_sq[:, :], 1.0 / dtot)
        S["sc"].inc(inst)

    @blk.vector
    def _(ve):
        S["sc"].wait(ve)
        ve.tensor_mul(mu2[:, :], murow[:, :], murow[:, :])
        ve.tensor_sub(varrow[:, :], qrow[:, :], mu2[:, :])
        S["vv"].inc(ve.tensor_scalar_add(varrow[:, :], varrow[:, :], LN_EPS))
        S["vv"].wait(ve)
        inst = ve.reciprocal(rstdrow[:, :], varrow[:, :])
        S["ve"].inc(inst)

    @blk.scalar
    def _(sc):
        S["ve"].wait(sc)
        inst = sc.activation(rstdrow[:, :], rstdrow[:, :], AF.Sqrt)
        S["sc"].inc(inst)

    @blk.tensor
    def _(te):
        S["sc"].wait(te)
        te.matmul(p_mu[:, :], ones_row[:, :], murow[:, :], start=True, stop=True)
        inst = te.matmul(p_rstd[:, :], ones_row[:, :], rstdrow[:, :], start=True, stop=True)
        S["te"].inc(inst)

    if stats_to_psum:
        return

    @blk.vector
    def _(ve):
        S["te"].wait(ve)
        ve.tensor_copy(mu_b[:, :], p_mu[:, :])
        inst = ve.tensor_copy(rstd_b[:, :], p_rstd[:, :])
        S["ve"].inc(inst)


def mk_ln_bufs(nc, E, pfx, nt, slim=False):
    d = dict(
        ones_col=E(nc.sbuf_tensor(pfx + "ones_col", [P, 1], F32)),
        ones_row=E(nc.sbuf_tensor(pfx + "ones_row", [1, P], F32)),
        eps_ap=E(nc.sbuf_tensor(pfx + "eps", [1, 1], F32)),
        murow=E(nc.sbuf_tensor(pfx + "murow", [1, nt], F32)),
        qrow=E(nc.sbuf_tensor(pfx + "qrow", [1, nt], F32)),
        varrow=E(nc.sbuf_tensor(pfx + "varrow", [1, nt], F32)),
        rstdrow=E(nc.sbuf_tensor(pfx + "rstdrow", [1, nt], F32)),
    )
    d["mu2"] = d["rstdrow"]
    if not slim:
        d["mu_b"] = E(nc.sbuf_tensor(pfx + "mu_b", [P, nt], F32))
        d["rstd_b"] = E(nc.sbuf_tensor(pfx + "rstd_b", [P, nt], F32))
    else:
        d["mu_b"] = None
        d["rstd_b"] = None
    return d


def ln_args(L, p_sum, p_sq, p_mu, p_rstd, xsq):
    return (L["ones_col"], L["ones_row"], p_sum, p_sq, p_mu, p_rstd, xsq,
            L["murow"], L["qrow"], L["mu2"], L["varrow"], L["rstdrow"],
            L["eps_ap"], L["mu_b"], L["rstd_b"])


def emit_A(nc, S, x, whgT, bhg, ab_loc, nt=512):
    DI = 1536
    HG = 2 * DI
    nch = TPC // nt
    blk = S["blk"]
    with ExitStack() as ctx:
        E = ctx.enter_context
        w_sb = E(nc.sbuf_tensor("a_w_sb", [P, 8, HG], BF16))
        bhg_sb = E(nc.sbuf_tensor("a_bhg", [P, 24], F32))
        bhg_neg = E(nc.sbuf_tensor("a_bhgn", [P, 24], F32))
        bhg_half = E(nc.sbuf_tensor("a_bhgh", [P, 24], F32))
        xc = E(nc.sbuf_tensor("a_xc", [P, 8, nt], F32))
        xsq = E(nc.sbuf_tensor("a_xsq", [P, 8, nt], F32))
        xnb = E(nc.sbuf_tensor("a_xnb", [P, 8, nt], BF16))
        L = mk_ln_bufs(nc, E, "a_", nt)
        sig_g = E(nc.sbuf_tensor("a_sig_g", [P, nt], F32))
        sig_h = E(nc.sbuf_tensor("a_sig_h", [P, nt], F32))
        lin_h = E(nc.sbuf_tensor("a_lin_h", [P, nt], F32))
        mask_h = E(nc.sbuf_tensor("a_mask_h", [P, nt], U8))
        g_h = E(nc.sbuf_tensor("a_g_h", [P, nt], F32))
        a_st2 = [E(nc.sbuf_tensor("a_ast0", [P, nt], F32)),
                 E(nc.sbuf_tensor("a_ast1", [P, nt], F32))]
        b_st2 = [E(nc.sbuf_tensor("a_bst0", [P, nt], F32)),
                 E(nc.sbuf_tensor("a_bst1", [P, nt], F32))]
        p_sum = E(nc.psum_tensor("a_p_sum", [1, nt], F32))
        p_sq = E(nc.psum_tensor("a_p_sq", [1, nt], F32))
        p_mu = E(nc.psum_tensor("a_p_mu", [P, nt], F32))
        p_rstd = E(nc.psum_tensor("a_p_rstd", [P, nt], F32))
        p_h2 = [E(nc.psum_tensor("a_p_h0", [P, nt], F32)),
                E(nc.psum_tensor("a_p_h1", [P, nt], F32))]
        p_g2 = [E(nc.psum_tensor("a_p_g0", [P, nt], F32)),
                E(nc.psum_tensor("a_p_g1", [P, nt], F32))]

        bank_guard = [(), ()]
        st_guard = [0, 0]

        @blk.gpsimd
        def _(gp):
            S["ld"].dma(gp.dma_start(
                out=w_sb[:, :, :], in_=whgT.rearrange("(kt p) c -> p kt c", p=P)))

        @blk.sync
        def _(sy):
            S["ld"].dma(sy.dma_start(out=bhg_sb[:, :], in_=bhg))

        @blk.vector
        def _(ve):
            ve.memset(L["ones_col"][:, :], 1.0)
            ve.memset(L["ones_row"][:, :], 1.0)
            ve.memset(L["eps_ap"][:, :], LN_EPS)
            S["ld"].wait(ve)
            ve.tensor_scalar_mul(bhg_neg[:, :], bhg_sb[:, :], -1.0)
            inst = ve.tensor_scalar_add(bhg_half[:, :], bhg_sb[:, :], 0.5)
            S["ve"].inc(inst)

        for ci in range(nch):
            t0 = ci * nt

            @blk.sync
            def _(sy):
                S["st"].wait(sy)
                S["ve"].wait(sy)
                S["sc"].wait(sy)
                S["te"].wait(sy)
                S["ld"].dma(sy.dma_start(out=xc[:, :, :], in_=fm_ap(x, D, t0, nt)))

            emit_ln(nc, S, xc, 8, nt, *ln_args(L, p_sum, p_sq, p_mu, p_rstd, xsq))

            @blk.vector
            def _(ve):
                S["ve"].wait(ve)
                for kt in range(8):
                    ve.tensor_sub(xsq[:, kt, :], xc[:, kt, :], L["mu_b"][:, :])
                    inst = ve.tensor_mul(xnb[:, kt, :], xsq[:, kt, :], L["rstd_b"][:, :])
                S["ve"].inc(inst)

            xnb_ready = S["ve"].n
            for j in range(12):
                @blk.tensor
                def _(te, j=j):
                    S["ve"].wait_at(te, xnb_ready)
                    for g0 in bank_guard[j % 2]:
                        S[g0[0]].wait_at(te, g0[1])
                    for kt in range(8):
                        te.matmul(p_h2[j % 2][:, :], w_sb[:, kt, j * P:(j + 1) * P],
                                  xnb[:, kt, :], start=(kt == 0), stop=(kt == 7))
                    for kt in range(8):
                        inst = te.matmul(p_g2[j % 2][:, :], w_sb[:, kt, (12 + j) * P:(13 + j) * P],
                                         xnb[:, kt, :], start=(kt == 0), stop=(kt == 7))
                    S["te"].inc(inst)

                @blk.scalar
                def _(sc, j=j):
                    S["te"].wait(sc)
                    S["ve"].wait(sc)
                    S["st"].wait_at(sc, st_guard[j % 2])
                    sc.activation(sig_g[:, :], p_g2[j % 2][:, :], AF.Sigmoid,
                                  bias=bhg_sb[:, 12 + j:13 + j])
                    sc.activation(a_st2[j % 2][:, :], p_g2[j % 2][:, :], AF.Sigmoid,
                                  bias=bhg_neg[:, 12 + j:13 + j], scale=-1.0)
                    inst = sc.activation(sig_h[:, :], p_h2[j % 2][:, :], AF.Sigmoid,
                                         bias=bhg_sb[:, j:j + 1])
                    S["sc"].inc(inst)

                @blk.vector
                def _(ve, j=j):
                    S["sc"].wait(ve)
                    ve.tensor_scalar_add(lin_h[:, :], p_h2[j % 2][:, :], bhg_half[:, j:j + 1])
                    ve.tensor_scalar(mask_h[:, :], p_h2[j % 2][:, :],
                                     bhg_neg[:, j:j + 1], None, op0=ALU.is_ge)
                    ve.select(g_h[:, :], mask_h[:, :], lin_h[:, :], sig_h[:, :])
                    inst = ve.tensor_mul(b_st2[j % 2][:, :], sig_g[:, :], g_h[:, :])
                    S["ve"].inc(inst)

                bank_guard[j % 2] = (("sc", S["sc"].n), ("ve", S["ve"].n))

                @blk.sync
                def _(sy, j=j, t0=t0):
                    S["ve"].wait(sy)
                    S["st"].dma(sy.dma_start(
                        out=ab_loc[j * P:(j + 1) * P, t0:t0 + nt], in_=a_st2[j % 2][:, :]))
                    S["st"].dma(sy.dma_start(
                        out=ab_loc[1536 + j * P:1536 + (j + 1) * P, t0:t0 + nt],
                        in_=b_st2[j % 2][:, :]))

                st_guard[j % 2] = S["st"].n


def emit_B(nc, S, ab_g, h0p, absel, h_loc):
    NJ = 6
    blk = S["blk"]
    with ExitStack() as ctx:
        E = ctx.enter_context
        s_a = E(nc.sbuf_tensor("b_sa", [P, SEQ], F32))
        s_b = E(nc.sbuf_tensor("b_sb", [P, SEQ], F32))
        s_h = E(nc.sbuf_tensor("b_sh", [P, SEQ], F32))
        s_hb = E(nc.sbuf_tensor("b_shb", [P, SEQ], BF16))
        s_h0 = E(nc.sbuf_tensor("b_h0", [P, NJ], F32))
        s_sig = E(nc.sbuf_tensor("b_sig", [P, NJ], F32))
        s_lin = E(nc.sbuf_tensor("b_lin", [P, NJ], F32))
        s_msk = E(nc.sbuf_tensor("b_msk", [P, NJ], U8))
        s_h0g = E(nc.sbuf_tensor("b_h0g", [P, NJ], F32))
        sel = E(nc.sbuf_tensor("b_sel", [P, 24], I32))

        @blk.sync
        def _(sy):
            S["ld"].dma(sy.dma_start(out=s_h0[:, :], in_=h0p))
            S["ld"].dma(sy.dma_start(out=sel[:, :], in_=absel))

        @blk.scalar
        def _(sc):
            S["ld"].wait(sc)
            inst = sc.activation(s_sig[:, :], s_h0[:, :], AF.Sigmoid)
            S["sc"].inc(inst)

        @blk.vector
        def _(ve):
            S["sc"].wait(ve)
            ve.tensor_scalar_add(s_lin[:, :], s_h0[:, :], 0.5)
            ve.tensor_scalar(s_msk[:, :], s_h0[:, :], 0.0, None, op0=ALU.is_ge)
            inst = ve.select(s_h0g[:, :], s_msk[:, :], s_lin[:, :], s_sig[:, :])
            S["ve"].inc(inst)

        for j in range(NJ):
            @blk.gpsimd
            def _(gp, j=j):
                S["ve"].wait(gp)
                S["st"].wait(gp)
                S["ld"].wait(gp)
                for (dst, col) in ((s_a[:, 0:TPC], 4 * j + 0), (s_a[:, TPC:SEQ], 4 * j + 1),
                                   (s_b[:, 0:TPC], 4 * j + 2), (s_b[:, TPC:SEQ], 4 * j + 3)):
                    S["gi"].dma(gp.indirect_dma_start(
                        out=dst, out_offset=None, in_=ab_g,
                        in_offset=IndirectOffsetOnAxis(ap=sel[:, col:col + 1], axis=0)))

            @blk.vector
            def _(ve, j=j):
                S["gi"].wait(ve)
                S["st"].wait(ve)
                ve.tensor_tensor_scan(
                    out=s_h[:, :], data0=s_a[:, :], data1=s_b[:, :],
                    initial=s_h0g[:, j:j + 1], op0=ALU.mult, op1=ALU.add)
                inst = ve.tensor_copy(s_hb[:, :], s_h[:, :])
                S["ve"].inc(inst)

            @blk.sync
            def _(sy, j=j):
                S["ve"].wait(sy)
                S["st"].dma(sy.dma_start(out=h_loc[j * P:(j + 1) * P, :], in_=s_hb[:, 0:TPC]))
                S["st"].dma(sy.dma_start(out=h_loc[768 + j * P:768 + (j + 1) * P, :],
                                         in_=s_hb[:, TPC:SEQ]))


def emit_C(nc, S, x, h_g, hsel, woutT, bout, lng, lnb, centT, identb,
           x2_o, feat_o, sc_o, normtm, nt=512, cc_gate=None):
    nch = TPC // nt
    blk = S["blk"]
    with ExitStack() as ctx:
        E = ctx.enter_context
        w_sb = E(nc.sbuf_tensor("c_w_sb", [P, 12, D], BF16))
        cent_sb = E(nc.sbuf_tensor("c_cent", [P, 8, 8], F32))
        bout_sb = E(nc.sbuf_tensor("c_bout", [P, 8], F32))
        lng_sb = E(nc.sbuf_tensor("c_lng", [P, 8], F32))
        lnb_sb = E(nc.sbuf_tensor("c_lnb", [P, 8], F32))
        idb = E(nc.sbuf_tensor("c_idb", [P, P], BF16))
        hselb = E(nc.sbuf_tensor("c_hsel", [P, 12], I32))
        hc = E(nc.sbuf_tensor("c_hc", [P, 12, nt], BF16))
        xc = E(nc.sbuf_tensor("c_xc", [P, 8, nt], F32))
        x2c = E(nc.sbuf_tensor("c_x2c", [P, 8, nt], F32))
        featc = E(nc.sbuf_tensor("c_featc", [P, 8, nt], F32))
        normc = E(nc.sbuf_tensor("c_normc", [P, 8, nt], F32))
        normb = E(nc.sbuf_tensor("c_normb", [P, 8, nt], BF16))
        xsq = E(nc.sbuf_tensor("c_xsq", [P, 8, nt], F32))
        L = mk_ln_bufs(nc, E, "c_", nt)
        sc_st = E(nc.sbuf_tensor("c_sc_st", [8, nt], F32))
        stg = [E(nc.sbuf_tensor("c_stg0", [P, D], BF16)),
               E(nc.sbuf_tensor("c_stg1", [P, D], BF16))]

        p_y2 = [E(nc.psum_tensor("c_p_y0", [P, nt], F32)),
                E(nc.psum_tensor("c_p_y1", [P, nt], F32))]
        p_sum = E(nc.psum_tensor("c_p_sum", [1, nt], F32))
        p_sq = E(nc.psum_tensor("c_p_sq", [1, nt], F32))
        p_mu = E(nc.psum_tensor("c_p_mu", [P, nt], F32))
        p_rstd = E(nc.psum_tensor("c_p_rstd", [P, nt], F32))
        p_sc = E(nc.psum_tensor("c_p_sc", [8, nt], F32))

        @blk.gpsimd
        def _(gp):
            S["ld"].dma(gp.dma_start(
                out=w_sb[:, :, :], in_=woutT.rearrange("(ct p) d -> p ct d", p=P)))

        @blk.sync
        def _(sy):
            S["ld"].dma(sy.dma_start(
                out=cent_sb[:, :, :], in_=centT.rearrange("(kt p) e -> p kt e", p=P)))
            S["ld"].dma(sy.dma_start(out=bout_sb[:, :], in_=bout))
            S["ld"].dma(sy.dma_start(out=lng_sb[:, :], in_=lng))
            S["ld"].dma(sy.dma_start(out=lnb_sb[:, :], in_=lnb))
            S["ld"].dma(sy.dma_start(out=idb[:, :], in_=identb))
            S["ld"].dma(sy.dma_start(out=hselb[:, :], in_=hsel))

        @blk.vector
        def _(ve):
            ve.memset(L["ones_col"][:, :], 1.0)
            ve.memset(L["ones_row"][:, :], 1.0)
            inst = ve.memset(L["eps_ap"][:, :], LN_EPS)
            S["ve"].inc(inst)

        tp_guard = [0, 0]
        for ci in range(nch):
            t0 = ci * nt

            @blk.sync
            def _(sy):
                S["st"].wait(sy)
                S["ve"].wait(sy)
                S["sc"].wait(sy)
                S["te"].wait(sy)
                S["ld"].dma(sy.dma_start(out=xc[:, :, :], in_=fm_ap(x, D, t0, nt)))

            @blk.gpsimd
            def _(gp, t0=t0):
                S["st"].wait(gp)
                S["ve"].wait(gp)
                S["ld"].wait(gp)
                if cc_gate is not None:
                    gp.wait_ge(cc_gate, 1)
                for ct in range(12):
                    S["gi"].dma(gp.indirect_dma_start(
                        out=hc[:, ct, :], out_offset=None, in_=h_g, element_offset=t0,
                        in_offset=IndirectOffsetOnAxis(ap=hselb[:, ct:ct + 1], axis=0)))

            yg = [0, 0]
            for dt in range(8):
                @blk.tensor
                def _(te, dt=dt):
                    S["ld"].wait(te)
                    S["gi"].wait(te)
                    S["ve"].wait_at(te, yg[dt % 2])
                    for ctj in range(12):
                        inst = te.matmul(p_y2[dt % 2][:, :], w_sb[:, ctj, dt * P:(dt + 1) * P],
                                         hc[:, ctj, :], start=(ctj == 0), stop=(ctj == 11))
                    S["te"].inc(inst)

                @blk.vector
                def _(ve, dt=dt):
                    S["te"].wait(ve)
                    inst = ve.scalar_tensor_tensor(
                        out=x2c[:, dt, :], in0=p_y2[dt % 2][:, :], scalar=bout_sb[:, dt:dt + 1],
                        in1=xc[:, dt, :], op0=ALU.add, op1=ALU.add)
                    S["ve"].inc(inst)
                yg[dt % 2] = S["ve"].n

            emit_ln(nc, S, x2c, 8, nt, *ln_args(L, p_sum, p_sq, p_mu, p_rstd, xsq),
                    gates=("ve",))

            @blk.vector
            def _(ve):
                S["ve"].wait(ve)
                for kt in range(8):
                    ve.tensor_sub(xsq[:, kt, :], x2c[:, kt, :], L["mu_b"][:, :])
                    inst = ve.tensor_mul(featc[:, kt, :], xsq[:, kt, :], L["rstd_b"][:, :])
                S["ve"].inc(inst)

            emit_ln(nc, S, featc, 8, nt, *ln_args(L, p_sum, p_sq, p_mu, p_rstd, xsq),
                    gates=("ve",))

            @blk.vector
            def _(ve):
                S["ve"].wait(ve)
                for kt in range(8):
                    ve.tensor_sub(xsq[:, kt, :], featc[:, kt, :], L["mu_b"][:, :])
                    ve.tensor_mul(xsq[:, kt, :], xsq[:, kt, :], L["rstd_b"][:, :])
                    ve.tensor_scalar(
                        normc[:, kt, :], xsq[:, kt, :],
                        lng_sb[:, kt:kt + 1], lnb_sb[:, kt:kt + 1],
                        op0=ALU.mult, op1=ALU.add)
                    inst = ve.tensor_copy(normb[:, kt, :], normc[:, kt, :])
                S["ve"].inc(inst)

            @blk.tensor
            def _(te):
                S["ve"].wait(te)
                S["sc"].wait(te)
                for kt in range(8):
                    inst = te.matmul(p_sc[:, :], cent_sb[:, kt, :], normc[:, kt, :],
                                     start=(kt == 0), stop=(kt == 7))
                S["te"].inc(inst)

            @blk.scalar
            def _(sc):
                S["te"].wait(sc)
                inst = sc.copy(sc_st[:, :], p_sc[:, :])
                S["sc"].inc(inst)

            for tt in range(4):
                for half in range(2):
                    @blk.tensor
                    def _(te, tt=tt, half=half):
                        S["ve"].wait(te)
                        for d4 in range(4):
                            dtg = half * 4 + d4
                            inst = te.matmul(p_y2[half][:, d4 * P:(d4 + 1) * P],
                                             normb[:, dtg, tt * P:(tt + 1) * P],
                                             idb[:, :], start=True, stop=True)
                        S["te"].inc(inst)

                    @blk.vector
                    def _(ve, tt=tt, half=half):
                        S["te"].wait(ve)
                        S["st"].wait_at(ve, tp_guard[tt % 2])
                        inst = ve.tensor_copy(stg[tt % 2][:, half * 512:(half + 1) * 512],
                                              p_y2[half][:, 0:512])
                        S["ve"].inc(inst)
                    yg[half] = S["ve"].n

                @blk.sync
                def _(sy, tt=tt, t0=t0):
                    S["ve"].wait(sy)
                    S["st"].dma(sy.dma_start(
                        out=normtm[t0 + tt * P:t0 + (tt + 1) * P, :], in_=stg[tt % 2][:, :]))
                tp_guard[tt % 2] = S["st"].n

            @blk.sync
            def _(sy, t0=t0):
                S["ve"].wait(sy)
                S["sc"].wait(sy)
                S["st"].dma(sy.dma_start(out=fm_ap(x2_o, D, t0, nt), in_=x2c[:, :, :]))
                S["st"].dma(sy.dma_start(out=fm_ap(feat_o, D, t0, nt), in_=featc[:, :, :]))
                S["st"].dma(sy.dma_start(out=sc_o[:, t0:t0 + nt], in_=sc_st[:, :]))


def emit_D(nc, S, scG, triu, iota_f, iota_i, t2s_o, t2sf_o, s2t_o, niter=32):
    T = 32768
    FE = 256
    CAP = 4096.0
    NEG = -3.0e38
    blk = S["blk"]
    with ExitStack() as ctx:
        E = ctx.enter_context
        s_sc = E(nc.sbuf_tensor("d_sc", [P, 8, FE], F32))
        s_m = E(nc.sbuf_tensor("d_m", [P, 8, FE], F32))
        s_cur = E(nc.sbuf_tensor("d_cur", [P, FE], F32))
        s_taken = E(nc.sbuf_tensor("d_taken", [P, FE], U8))
        s_tkf = E(nc.sbuf_tensor("d_tkf", [P, FE], F32))
        s_neg = E(nc.sbuf_tensor("d_neg", [P, FE], F32))
        s_bounds = E(nc.sbuf_tensor("d_bounds", [P, 2], F32))
        junk = E(nc.sbuf_tensor("d_junk", [P, FE], F32))
        tot_s = E(nc.sbuf_tensor("d_tot", [1, 1], F32))
        lo = E(nc.sbuf_tensor("d_lo", [P, 1], F32))
        hi = E(nc.sbuf_tensor("d_hi", [P, 1], F32))
        mid = E(nc.sbuf_tensor("d_mid", [P, 1], F32))
        lo_n = E(nc.sbuf_tensor("d_lo_n", [P, 1], F32))
        hi_n = E(nc.sbuf_tensor("d_hi_n", [P, 1], F32))
        geb = E(nc.sbuf_tensor("d_geb", [P, 1], U8))
        ones_col = E(nc.sbuf_tensor("d_ones_c", [P, 1], F32))
        ones_row = E(nc.sbuf_tensor("d_ones_r", [1, P], F32))
        rowred = E(nc.sbuf_tensor("d_rowred", [P, 2], F32))
        glob = E(nc.sbuf_tensor("d_glob", [1, 2], F32))
        brow = E(nc.sbuf_tensor("d_brow", [1, 2 * P], F32))
        glob2 = E(nc.sbuf_tensor("d_glob2", [1, 1], F32))
        s_triu = E(nc.sbuf_tensor("d_triu", [P, P], F32))
        s_iota = E(nc.sbuf_tensor("d_iota", [P, FE], F32))
        s_iota_i = E(nc.sbuf_tensor("d_iota_i", [P, FE], I32))
        s_one256 = E(nc.sbuf_tensor("d_one256", [P, FE], F32))
        s_cums = E(nc.sbuf_tensor("d_cums", [P, FE], F32))
        s_rowtot = E(nc.sbuf_tensor("d_rowtot", [P, 1], F32))
        s_pre = E(nc.sbuf_tensor("d_pre", [P, 1], F32))
        s_t2s = E(nc.sbuf_tensor("d_t2s", [P, FE], F32))
        s_t2si = E(nc.sbuf_tensor("d_t2si", [P, FE], I32))
        s_tmp = E(nc.sbuf_tensor("d_tmp", [P, FE], F32))
        p_pre = E(nc.psum_tensor("d_p_pre", [P, 1], F32))
        p_row = E(nc.psum_tensor("d_p_row", [1, 256], F32))
        p_bc = E(nc.psum_tensor("d_p_bc", [P, 2], F32))
        p_cntb = E(nc.psum_tensor("d_p_cntb", [P, 1], F32))
        bmax_d = nc.dram_tensor("d_bmax", [P, 1], F32, kind="Internal").ap()
        bmin_d = nc.dram_tensor("d_bmin", [P, 1], F32, kind="Internal").ap()

        @blk.sync
        def _(sy):
            for r in range(8):
                for e in range(8):
                    S["ld"].dma(sy.dma_start(
                        out=s_sc[16 * r:16 * (r + 1), e, :],
                        in_=scG[8 * r + e:8 * r + e + 1, :].rearrange(
                            "one (pp f) -> (one pp) f", pp=16)))
            S["ld"].dma(sy.dma_start(out=s_triu[:, :], in_=triu))
            S["ld"].dma(sy.dma_start(out=s_iota[:, :], in_=iota_f))
            S["ld"].dma(sy.dma_start(out=s_iota_i[:, :], in_=iota_i))

        @blk.vector
        def _(ve):
            ve.memset(ones_col[:, :], 1.0)
            ve.memset(ones_row[:, :], 1.0)
            ve.memset(s_taken[:, :], 0)
            ve.memset(s_tkf[:, :], 0.0)
            ve.memset(s_neg[:, :], NEG)
            S["ld"].wait(ve)
            flat = s_sc[:, :, :].rearrange("p e f -> p (e f)")
            ve.tensor_reduce(rowred[:, 0:1], flat, mybir.AxisListType.X, ALU.max)
            inst = ve.tensor_reduce(rowred[:, 1:2], flat, mybir.AxisListType.X, ALU.min)
            S["ve"].inc(inst)

        @blk.vector
        def _(ve):
            S["ve"].wait(ve)
            inst = ve.tensor_scalar_mul(rowred[:, 1:2], rowred[:, 1:2], -1.0)
            S["ve"].inc(inst)

        @blk.sync
        def _(sy):
            S["ve"].wait(sy)
            S["st"].dma(sy.dma_start(out=bmax_d, in_=rowred[:, 0:1]))
            S["st"].dma(sy.dma_start(out=bmin_d, in_=rowred[:, 1:2]))

        @blk.sync
        def _(sy):
            S["st"].wait(sy)
            S["ld"].dma(sy.dma_start(out=brow[0:1, 0:P], in_=bmax_d.rearrange("p one -> (p one)")[None, :]))
            S["ld"].dma(sy.dma_start(out=brow[0:1, P:2 * P], in_=bmin_d.rearrange("p one -> (p one)")[None, :]))

        @blk.vector
        def _(ve):
            S["ld"].wait(ve)
            ve.tensor_reduce(glob[0:1, 0:1], brow[0:1, 0:P], mybir.AxisListType.X, ALU.max)
            inst = ve.tensor_reduce(glob2[0:1, 0:1], brow[0:1, P:2 * P], mybir.AxisListType.X, ALU.max)
            S["ve"].inc(inst)

        @blk.tensor
        def _(te):
            S["ve"].wait(te)
            te.matmul(p_bc[:, 0:1], ones_row[:, :], glob[0:1, 0:1], start=True, stop=True)
            inst = te.matmul(p_bc[:, 1:2], ones_row[:, :], glob2[0:1, 0:1], start=True, stop=True)
            S["te"].inc(inst)

        @blk.vector
        def _(ve):
            S["te"].wait(ve)
            ve.tensor_scalar_add(s_bounds[:, 0:1], p_bc[:, 0:1], 1.0)
            inst = ve.tensor_scalar(s_bounds[:, 1:2], p_bc[:, 1:2], -1.0, -1.0,
                                    op0=ALU.mult, op1=ALU.add)
            S["ve"].inc(inst)

        for e in range(8):
            @blk.vector
            def _(ve, e=e):
                S["ve"].wait(ve)
                ve.select(s_cur[:, :], s_taken[:, :], s_neg[:, :], s_sc[:, e, :])
                ve.tensor_copy(hi[:, :], s_bounds[:, 0:1])
                inst = ve.tensor_copy(lo[:, :], s_bounds[:, 1:2])
                S["ve"].inc(inst)

            for it in range(niter):
                @blk.vector
                def _(ve):
                    S["ve"].wait(ve)
                    S["te"].wait(ve)
                    S["vv"].inc(ve.tensor_add(mid[:, :], lo[:, :], hi[:, :]))
                    S["vv"].wait(ve)
                    S["vv"].inc(ve.tensor_scalar_mul(mid[:, :], mid[:, :], 0.5))
                    S["vv"].wait(ve)
                    inst = ve.tensor_scalar(junk[:, :], s_cur[:, :], mid[:, :], None,
                                            op0=ALU.is_gt)
                    S["ve"].inc(inst)

                @blk.tensor
                def _(te):
                    S["ve"].wait(te)
                    inst = te.matmul(p_row[:, :], ones_col[:, :], junk[:, :],
                                     start=True, stop=True)
                    S["te"].inc(inst)

                @blk.vector
                def _(ve):
                    S["te"].wait(ve)
                    inst = ve.tensor_reduce(tot_s[:, :], p_row[:, :],
                                            mybir.AxisListType.X, ALU.add)
                    S["ve"].inc(inst)

                @blk.tensor
                def _(te):
                    S["ve"].wait(te)
                    inst = te.matmul(p_cntb[:, :], ones_row[:, :], tot_s[:, :],
                                     start=True, stop=True)
                    S["te"].inc(inst)

                @blk.vector
                def _(ve):
                    S["te"].wait(ve)
                    S["vv"].inc(ve.tensor_scalar(geb[:, :], p_cntb[:, :], CAP, None,
                                                 op0=ALU.is_ge))
                    S["vv"].wait(ve)
                    ve.select(lo_n[:, :], geb[:, :], mid[:, :], lo[:, :])
                    S["vv"].inc(ve.select(hi_n[:, :], geb[:, :], hi[:, :], mid[:, :]))
                    S["vv"].wait(ve)
                    ve.tensor_copy(lo[:, :], lo_n[:, :])
                    inst = ve.tensor_copy(hi[:, :], hi_n[:, :])
                    S["ve"].inc(inst)

            @blk.vector
            def _(ve, e=e):
                S["ve"].wait(ve)
                ve.tensor_scalar(s_m[:, e, :], s_cur[:, :], lo[:, :], None, op0=ALU.is_gt)
                ve.tensor_add(s_tkf[:, :], s_tkf[:, :], s_m[:, e, :])
                inst = ve.tensor_scalar(s_taken[:, :], s_tkf[:, :], 0.5, None, op0=ALU.is_gt)
                S["ve"].inc(inst)

        @blk.vector
        def _(ve):
            S["ve"].wait(ve)
            ve.memset(s_one256[:, :], 1.0)
            inst = ve.memset(s_t2s[:, :], 0.0)
            S["ve"].inc(inst)

        for e in range(8):
            @blk.vector
            def _(ve, e=e):
                S["ve"].wait(ve)
                S["te"].wait(ve)
                S["vv"].inc(ve.tensor_tensor_scan(
                    out=s_cums[:, :], data0=s_one256[:, :], data1=s_m[:, e, :],
                    initial=0.0, op0=ALU.mult, op1=ALU.add))
                S["vv"].wait(ve)
                inst = ve.tensor_copy(s_rowtot[:, :], s_cums[:, FE - 1:FE])
                S["ve"].inc(inst)

            @blk.tensor
            def _(te):
                S["ve"].wait(te)
                inst = te.matmul(p_pre[:, :], s_triu[:, :], s_rowtot[:, :],
                                 start=True, stop=True)
                S["te"].inc(inst)

            @blk.vector
            def _(ve, e=e):
                S["te"].wait(ve)
                S["vv"].inc(ve.tensor_copy(s_pre[:, :], p_pre[:, :]))
                S["vv"].wait(ve)
                ve.tensor_scalar_add(s_tmp[:, :], s_cums[:, :], float(4096 * e - 1))
                S["vv"].inc(ve.tensor_scalar_add(s_tmp[:, :], s_tmp[:, :], s_pre[:, :]))
                S["vv"].wait(ve)
                ve.tensor_mul(s_tmp[:, :], s_tmp[:, :], s_m[:, e, :])
                inst = ve.tensor_add(s_t2s[:, :], s_t2s[:, :], s_tmp[:, :])
                S["ve"].inc(inst)

        @blk.vector
        def _(ve):
            S["ve"].wait(ve)
            inst = ve.tensor_copy(s_t2si[:, :], s_t2s[:, :])
            S["ve"].inc(inst)

        @blk.sync
        def _(sy):
            S["ve"].wait(sy)
            S["st"].dma(sy.dma_start(out=t2s_o, in_=s_t2si[:, :]))
            S["st"].dma(sy.dma_start(
                out=t2sf_o.rearrange("(p f) one -> p (f one)", p=P), in_=s_t2si[:, :]))

        @blk.gpsimd
        def _(gp):
            S["ve"].wait(gp)
            for f in range(FE):
                S["st"].dma(gp.indirect_dma_start(
                    out=s2t_o,
                    out_offset=IndirectOffsetOnAxis(ap=s_t2si[:, f:f + 1], axis=0),
                    in_=s_iota_i[:, f:f + 1], in_offset=None))


def emit_E_weights(nc, S, ectx, w1e, b1p, w2e, b2p):
    blk = S["blk"]
    E = ectx.enter_context
    w1_sb = E(nc.sbuf_tensor("e_w1", [P, 8, 4096], BF16))
    w2_sb = E(nc.sbuf_tensor("e_w2", [P, 32, D], BF16))
    b1_sb = E(nc.sbuf_tensor("e_b1", [P, 32], F32))
    b2_sb = E(nc.sbuf_tensor("e_b2", [P, 8], F32))

    @blk.gpsimd
    def _(gp):
        S["ld"].dma(gp.dma_start(
            out=w1_sb[:, :, :], in_=w1e.rearrange("(kt p) f -> p kt f", p=P)))
        S["ld"].dma(gp.dma_start(
            out=w2_sb[:, :, :], in_=w2e.rearrange("(ft p) d -> p ft d", p=P)))

    @blk.sync
    def _(sy):
        S["ld"].dma(sy.dma_start(out=b1_sb[:, :], in_=b1p))
        S["ld"].dma(sy.dma_start(out=b2_sb[:, :], in_=b2p))
    return w1_sb, w2_sb, b1_sb, b2_sb


def emit_E(nc, S, normG, s2t_d, tksel, wbufs, identb, stag, slabtm, nt=512):
    F = 4096
    nch = TPC // nt
    blk = S["blk"]
    w1_sb, w2_sb, b1_sb, b2_sb = wbufs
    with ExitStack() as ctx:
        E = ctx.enter_context
        selb = E(nc.sbuf_tensor("e_sel", [P, 32], I32))
        idx_sb = E(nc.sbuf_tensor("e_idx", [P, 32], I32))
        idb = E(nc.sbuf_tensor("e_idb", [P, P], BF16))
        g_sb2 = [E(nc.sbuf_tensor("e_g0", [P, D], BF16)),
                 E(nc.sbuf_tensor("e_g1", [P, D], BF16))]
        nrm_c = E(nc.sbuf_tensor("e_nrm", [P, 8, nt], BF16))
        h_bf = E(nc.sbuf_tensor("e_h", [P, 32, nt], BF16))
        o_bf = E(nc.sbuf_tensor("e_o", [P, 8, nt], BF16))
        stg = [E(nc.sbuf_tensor("e_stg0", [P, D], BF16)),
               E(nc.sbuf_tensor("e_stg1", [P, D], BF16))]
        p_h2 = [E(nc.psum_tensor("e_p_h0", [P, nt], F32)),
                E(nc.psum_tensor("e_p_h1", [P, nt], F32))]
        p_o2 = [E(nc.psum_tensor("e_p_o0", [P, nt], F32)),
                E(nc.psum_tensor("e_p_o1", [P, nt], F32))]

        @blk.sync
        def _(sy):
            S["ld"].dma(sy.dma_start(out=selb[:, :], in_=tksel))
            S["ld"].dma(sy.dma_start(out=idb[:, :], in_=identb))

        @blk.gpsimd
        def _(gp):
            S["ld"].wait(gp)
            for r in range(32):
                S["gi"].dma(gp.indirect_dma_start(
                    out=idx_sb[:, r:r + 1], out_offset=None, in_=s2t_d,
                    in_offset=IndirectOffsetOnAxis(ap=selb[:, r:r + 1], axis=0)))

        gst_guard = [0, 0]
        for r in range(32):
            @blk.gpsimd
            def _(gp, r=r):
                if r < 2:
                    S["gi"].wait(gp)
                S["st"].wait_at(gp, gst_guard[r % 2])
                S["gi"].dma(gp.indirect_dma_start(
                    out=g_sb2[r % 2][:, :], out_offset=None,
                    in_=normG, in_offset=IndirectOffsetOnAxis(ap=idx_sb[:, r:r + 1], axis=0)))

            @blk.sync
            def _(sy, r=r):
                S["gi"].wait(sy)
                S["st"].dma(sy.dma_start(out=stag[r * P:(r + 1) * P, :], in_=g_sb2[r % 2][:, :]))
            gst_guard[r % 2] = S["st"].n

        tp_guard = [0, 0]
        for ci in range(nch):
            t0 = ci * nt

            @blk.scalar
            def _(sc, t0=t0):
                S["gi"].wait(sc)
                S["st"].wait(sc)
                S["te"].wait(sc)
                S["ve"].wait(sc)
                for dt in range(8):
                    S["tp"].dma(sc.dma_start_transpose(
                        out=nrm_c[:, dt, :], in_=stag[t0:t0 + nt, dt * P:(dt + 1) * P]))

            for ft in range(32):
                @blk.tensor
                def _(te, ft=ft):
                    S["tp"].wait(te)
                    S["sc"].wait(te, lag=1)
                    S["ve"].wait(te)
                    for kt in range(8):
                        inst = te.matmul(p_h2[ft % 2][:, :], w1_sb[:, kt, ft * P:(ft + 1) * P],
                                         nrm_c[:, kt, :], start=(kt == 0), stop=(kt == 7))
                    S["te"].inc(inst)

                @blk.scalar
                def _(sc, ft=ft):
                    S["te"].wait(sc)
                    inst = sc.activation(h_bf[:, ft, :], p_h2[ft % 2][:, :], AF.Gelu,
                                         bias=b1_sb[:, ft:ft + 1])
                    S["sc"].inc(inst)

            for dt in range(8):
                @blk.tensor
                def _(te, dt=dt):
                    S["sc"].wait(te)
                    S["ve"].wait(te, lag=1)
                    for ft in range(32):
                        inst = te.matmul(p_o2[dt % 2][:, :], w2_sb[:, ft, dt * P:(dt + 1) * P],
                                         h_bf[:, ft, :], start=(ft == 0), stop=(ft == 31))
                    S["te"].inc(inst)

                @blk.vector
                def _(ve, dt=dt):
                    S["te"].wait(ve)
                    inst = ve.tensor_scalar_add(o_bf[:, dt, :], p_o2[dt % 2][:, :],
                                                b2_sb[:, dt:dt + 1])
                    S["ve"].inc(inst)

            for tt in range(4):
                for half in range(2):
                    @blk.tensor
                    def _(te, tt=tt, half=half):
                        S["ve"].wait(te)
                        for d4 in range(4):
                            dtg = half * 4 + d4
                            inst = te.matmul(p_h2[half][:, d4 * P:(d4 + 1) * P],
                                             o_bf[:, dtg, tt * P:(tt + 1) * P],
                                             idb[:, :], start=True, stop=True)
                        S["te"].inc(inst)

                    @blk.vector
                    def _(ve, tt=tt, half=half):
                        S["te"].wait(ve)
                        S["st"].wait_at(ve, tp_guard[tt % 2])
                        inst = ve.tensor_copy(stg[tt % 2][:, half * 512:(half + 1) * 512],
                                              p_h2[half][:, 0:512])
                        S["ve"].inc(inst)

                @blk.sync
                def _(sy, tt=tt, t0=t0):
                    S["ve"].wait(sy)
                    S["st"].dma(sy.dma_start(
                        out=slabtm[t0 + tt * P:t0 + (tt + 1) * P, :], in_=stg[tt % 2][:, :]))
                tp_guard[tt % 2] = S["st"].n


def emit_G1(nc, S, x2, feat, ffnG, t2sf_d, tksel, wcatT, bcat, ng, nb, stag2,
            x3_o, z_o, cp_o, gv_o, ubf_loc, tb_loc, nt=256, cc_gate=None):
    NCH = 37
    nch = TPC // nt
    blk = S["blk"]
    with ExitStack() as ctx:
        E = ctx.enter_context
        w_sb = E(nc.sbuf_tensor("g_w", [P, 8, NCH * P], BF16))
        bcat_sb = E(nc.sbuf_tensor("g_bcat", [P, NCH], F32))
        ng_sb = E(nc.sbuf_tensor("g_ng", [P, 8], F32))
        nb_sb = E(nc.sbuf_tensor("g_nb", [P, 8], F32))
        selb = E(nc.sbuf_tensor("g_sel", [P, 32], I32))
        idx_sb = E(nc.sbuf_tensor("g_idx", [P, 32], I32))
        g_sb = E(nc.sbuf_tensor("g_g", [P, D], BF16))
        x2c = E(nc.sbuf_tensor("g_x2c", [P, 8, nt], F32))
        featc = E(nc.sbuf_tensor("g_featc", [P, 8, nt], F32))
        ffn_c = E(nc.sbuf_tensor("g_ffn_c", [P, 8, nt], BF16))
        x3c = E(nc.sbuf_tensor("g_x3c", [P, 8, nt], F32))
        zc = E(nc.sbuf_tensor("g_zc", [P, 8, nt], F32))
        xsq = E(nc.sbuf_tensor("g_xsq", [P, 8, nt], F32))
        xnb = E(nc.sbuf_tensor("g_xnb", [P, 8, nt], BF16))
        sig_cg = E(nc.sbuf_tensor("g_sig_cg", [P, 8, nt], F32))
        sp_st = E(nc.sbuf_tensor("g_sp", [P, 8, nt], F32))
        fg_st = E(nc.sbuf_tensor("g_fg", [P, 4, nt], F32))
        misc_st = E(nc.sbuf_tensor("g_misc", [P, nt], F32))
        cp_st = E(nc.sbuf_tensor("g_cp", [P, nt], F32))
        gv_st = E(nc.sbuf_tensor("g_gv", [P, nt], F32))
        u_st = E(nc.sbuf_tensor("g_u", [P, nt], BF16))
        L = mk_ln_bufs(nc, E, "g_", nt)
        p_sum = E(nc.psum_tensor("g_p_sum", [1, nt], F32))
        p_sq = E(nc.psum_tensor("g_p_sq", [1, nt], F32))
        p_mu = E(nc.psum_tensor("g_p_mu", [P, nt], F32))
        p_rstd = E(nc.psum_tensor("g_p_rstd", [P, nt], F32))
        p_pr2 = [E(nc.psum_tensor("g_p_pr0", [P, nt], F32)),
                 E(nc.psum_tensor("g_p_pr1", [P, nt], F32))]
        pr_guard = [(), ()]
        cp_guard = [0]
        gv_guard = [0]

        @blk.gpsimd
        def _(gp):
            S["ld"].dma(gp.dma_start(
                out=w_sb[:, :, :], in_=wcatT.rearrange("(kt p) c -> p kt c", p=P)))

        @blk.sync
        def _(sy):
            S["ld"].dma(sy.dma_start(out=bcat_sb[:, :], in_=bcat))
            S["ld"].dma(sy.dma_start(out=ng_sb[:, :], in_=ng))
            S["ld"].dma(sy.dma_start(out=nb_sb[:, :], in_=nb))
            S["ld"].dma(sy.dma_start(out=selb[:, :], in_=tksel))

        @blk.vector
        def _(ve):
            ve.memset(L["ones_col"][:, :], 1.0)
            ve.memset(L["ones_row"][:, :], 1.0)
            inst = ve.memset(L["eps_ap"][:, :], LN_EPS)
            S["ve"].inc(inst)

        @blk.gpsimd
        def _(gp):
            S["ld"].wait(gp)
            if cc_gate is not None:
                gp.wait_ge(cc_gate, 1)
            for r in range(32):
                S["gi"].dma(gp.indirect_dma_start(
                    out=idx_sb[:, r:r + 1], out_offset=None, in_=t2sf_d,
                    in_offset=IndirectOffsetOnAxis(ap=selb[:, r:r + 1], axis=0)))

        for r in range(32):
            @blk.gpsimd
            def _(gp, r=r):
                S["gi"].wait(gp)
                S["st"].wait(gp)
                S["gi"].dma(gp.indirect_dma_start(
                    out=g_sb[:, :], out_offset=None,
                    in_=ffnG, in_offset=IndirectOffsetOnAxis(ap=idx_sb[:, r:r + 1], axis=0)))

            @blk.sync
            def _(sy, r=r):
                S["gi"].wait(sy)
                S["st"].dma(sy.dma_start(out=stag2[r * P:(r + 1) * P, :], in_=g_sb[:, :]))

        for ci in range(nch):
            t0 = ci * nt

            @blk.sync
            def _(sy, t0=t0):
                S["st"].wait(sy)
                S["ve"].wait(sy)
                S["sc"].wait(sy)
                S["te"].wait(sy)
                S["ld"].dma(sy.dma_start(out=x2c[:, :, :], in_=fm_ap(x2, D, t0, nt)))
                S["ld"].dma(sy.dma_start(out=featc[:, :, :], in_=fm_ap(feat, D, t0, nt)))

            @blk.scalar
            def _(sc, t0=t0):
                S["gi"].wait(sc)
                S["st"].wait(sc)
                S["ve"].wait(sc)
                for dt in range(8):
                    S["tp"].dma(sc.dma_start_transpose(
                        out=ffn_c[:, dt, :], in_=stag2[t0:t0 + nt, dt * P:(dt + 1) * P]))

            @blk.vector
            def _(ve):
                S["ld"].wait(ve)
                S["tp"].wait(ve)
                S["st"].wait(ve)
                for kt in range(8):
                    ve.tensor_add(x3c[:, kt, :], x2c[:, kt, :], featc[:, kt, :])
                    inst = ve.tensor_add(x3c[:, kt, :], x3c[:, kt, :], ffn_c[:, kt, :])
                S["ve"].inc(inst)

            emit_ln(nc, S, x3c, 8, nt, *ln_args(L, p_sum, p_sq, p_mu, p_rstd, xsq),
                    gates=("ve",))

            @blk.vector
            def _(ve):
                S["ve"].wait(ve)
                for kt in range(8):
                    ve.tensor_sub(xsq[:, kt, :], x3c[:, kt, :], L["mu_b"][:, :])
                    inst = ve.tensor_mul(zc[:, kt, :], xsq[:, kt, :], L["rstd_b"][:, :])
                S["ve"].inc(inst)

            emit_ln(nc, S, zc, 8, nt, *ln_args(L, p_sum, p_sq, p_mu, p_rstd, xsq),
                    gates=("ve",))

            @blk.vector
            def _(ve):
                S["ve"].wait(ve)
                for kt in range(8):
                    ve.tensor_sub(xsq[:, kt, :], zc[:, kt, :], L["mu_b"][:, :])
                    ve.tensor_mul(xsq[:, kt, :], xsq[:, kt, :], L["rstd_b"][:, :])
                    inst = ve.tensor_scalar(
                        xnb[:, kt, :], xsq[:, kt, :],
                        ng_sb[:, kt:kt + 1], nb_sb[:, kt:kt + 1],
                        op0=ALU.mult, op1=ALU.add)
                S["ve"].inc(inst)

            order = list(range(8, 16)) + list(range(0, 8)) + list(range(16, NCH))
            xnb_ready = S["ve"].n
            sc_ready = S["sc"].n
            for idx, ct in enumerate(order):
                @blk.tensor
                def _(te, ct=ct, idx=idx):
                    S["ve"].wait_at(te, xnb_ready)
                    S["sc"].wait_at(te, sc_ready)
                    for g0 in pr_guard[idx % 2]:
                        S[g0[0]].wait_at(te, g0[1])
                    for kt in range(8):
                        inst = te.matmul(p_pr2[idx % 2][:, :], w_sb[:, kt, ct * P:(ct + 1) * P],
                                         xnb[:, kt, :], start=(kt == 0), stop=(kt == 7))
                    S["te"].inc(inst)

                if 8 <= ct < 16:
                    @blk.scalar
                    def _(sc, ct=ct, idx=idx):
                        S["te"].wait(sc)
                        inst = sc.activation(sig_cg[:, ct - 8, :], p_pr2[idx % 2][:, :], AF.Sigmoid,
                                             bias=bcat_sb[:, ct:ct + 1])
                        S["sc"].inc(inst)
                    pr_guard[idx % 2] = (("sc", S["sc"].n),)
                elif ct < 8:
                    @blk.vector
                    def _(ve, ct=ct, idx=idx):
                        S["te"].wait(ve)
                        S["st"].wait_at(ve, cp_guard[0])
                        inst = ve.scalar_tensor_tensor(
                            out=cp_st[:, :], in0=p_pr2[idx % 2][:, :], scalar=bcat_sb[:, ct:ct + 1],
                            in1=sig_cg[:, ct, :], op0=ALU.add, op1=ALU.mult)
                        S["ve"].inc(inst)
                    pr_guard[idx % 2] = (("ve", S["ve"].n),)

                    @blk.sync
                    def _(sy, ct=ct, t0=t0):
                        S["ve"].wait(sy)
                        S["st"].dma(sy.dma_start(
                            out=cp_o[ct * P:(ct + 1) * P, t0:t0 + nt], in_=cp_st[:, :]))
                    cp_guard[0] = S["st"].n
                elif 16 <= ct < 24:
                    @blk.vector
                    def _(ve, ct=ct, idx=idx):
                        S["te"].wait(ve)
                        inst = ve.tensor_copy(sp_st[:, ct - 16, :], p_pr2[idx % 2][:, :])
                        S["ve"].inc(inst)
                    pr_guard[idx % 2] = (("ve", S["ve"].n),)
                elif 24 <= ct < 32:
                    @blk.scalar
                    def _(sc, ct=ct, idx=idx):
                        S["te"].wait(sc)
                        S["st"].wait_at(sc, gv_guard[0])
                        inst = sc.activation(gv_st[:, :], p_pr2[idx % 2][:, :], AF.Sigmoid,
                                             bias=bcat_sb[:, ct:ct + 1])
                        S["sc"].inc(inst)
                    pr_guard[idx % 2] = (("sc", S["sc"].n),)

                    @blk.sync
                    def _(sy, ct=ct, t0=t0):
                        S["sc"].wait(sy)
                        S["st"].dma(sy.dma_start(
                            out=gv_o[(ct - 24) * P:(ct - 23) * P, t0:t0 + nt], in_=gv_st[:, :]))
                    gv_guard[0] = S["st"].n
                elif 32 <= ct < 36:
                    @blk.scalar
                    def _(sc, ct=ct, idx=idx):
                        S["te"].wait(sc)
                        inst = sc.activation(fg_st[:, ct - 32, :], p_pr2[idx % 2][:, :], AF.Sigmoid,
                                             bias=bcat_sb[:, ct:ct + 1])
                        S["sc"].inc(inst)
                    pr_guard[idx % 2] = (("sc", S["sc"].n),)
                else:
                    @blk.scalar
                    def _(sc, ct=ct, idx=idx):
                        S["te"].wait(sc)
                        inst = sc.activation(misc_st[:, :], p_pr2[idx % 2][:, :], AF.Sigmoid,
                                             bias=bcat_sb[:, ct:ct + 1])
                        S["sc"].inc(inst)
                    pr_guard[idx % 2] = (("sc", S["sc"].n),)

            for j in range(4):
                @blk.vector
                def _(ve, j=j):
                    S["ve"].wait(ve)
                    S["sc"].wait(ve)
                    S["st"].wait(ve)
                    inst = ve.tensor_mul(u_st[:, :], sp_st[:, j, :], fg_st[:, j, :])
                    S["ve"].inc(inst)

                @blk.sync
                def _(sy, j=j, t0=t0):
                    S["ve"].wait(sy)
                    S["st"].dma(sy.dma_start(
                        out=ubf_loc[j * P:(j + 1) * P, t0:t0 + nt], in_=u_st[:, :]))

                @blk.vector
                def _(ve, j=j):
                    S["st"].wait(ve)
                    inst = ve.tensor_mul(u_st[:, :], sp_st[:, 4 + j, :], fg_st[:, j, :])
                    S["ve"].inc(inst)

                @blk.sync
                def _(sy, j=j, t0=t0):
                    S["ve"].wait(sy)
                    S["st"].dma(sy.dma_start(
                        out=ubf_loc[512 + j * P:512 + (j + 1) * P, t0:t0 + nt], in_=u_st[:, :]))

            @blk.sync
            def _(sy, t0=t0):
                S["ve"].wait(sy)
                S["sc"].wait(sy)
                S["st"].dma(sy.dma_start(out=fm_ap(x3_o, D, t0, nt), in_=x3c[:, :, :]))
                S["st"].dma(sy.dma_start(out=fm_ap(z_o, D, t0, nt), in_=zc[:, :, :]))
                S["st"].dma(sy.dma_start(out=tb_loc[0:4, t0:t0 + nt], in_=misc_st[0:4, :]))
                S["st"].dma(sy.dma_start(out=tb_loc[4:5, t0:t0 + nt], in_=misc_st[4:5, :]))


N_FFT = 16384
R_F = 128


def fft_host_consts():
    k = np.arange(R_F)
    aa, bb = np.meshgrid(k, k, indexing="ij")
    w128 = np.exp(-2j * np.pi * aa * bb / R_F)
    bgrid, kgrid = np.meshgrid(k, k, indexing="ij")
    tw = np.exp(-2j * np.pi * bgrid * kgrid / N_FFT)
    twc = np.conj(tw) / N_FFT
    w128c = np.conj(w128)

    def bf(x):
        return np.ascontiguousarray(x.astype(np.float32)).astype(BF)

    t4 = np.tile(tw, (1, 4))
    tc4 = np.tile(twc, (1, 4))
    return {
        "f_re": bf(w128.real), "f_im": bf(w128.imag), "f_imN": bf(-w128.imag),
        "fc_re": bf(w128c.real), "fc_im": bf(w128c.imag), "fc_imN": bf(-w128c.imag),
        "t_re": bf(t4.real), "t_im": bf(t4.imag),
        "tc_re": bf(tc4.real), "tc_im": bf(tc4.imag),
    }


def emit_fft4(S, te_c, xre, xim, C, ps, sb, out_re, out_im, zero_pad=True):
    blk = S["blk"]
    KA = 64 if zero_pad else 128

    @blk.tensor
    def _(te):
        te_c(te)
        for c in range(4):
            sl = slice(128 * c, 128 * (c + 1))
            te.matmul(ps["pA_re"][:, sl], xre[:, c, :], C["f_re"][0:KA, :], start=True, stop=False)
            te.matmul(ps["pA_re"][:, sl], xim[:, c, :], C["f_imN"][0:KA, :], start=False, stop=True)
            te.matmul(ps["pA_im"][:, sl], xre[:, c, :], C["f_im"][0:KA, :], start=True, stop=False)
            inst = te.matmul(ps["pA_im"][:, sl], xim[:, c, :], C["f_re"][0:KA, :], start=False, stop=True)
        S["te"].inc(inst)

    @blk.vector
    def _(ve):
        S["te"].wait(ve)
        ve.tensor_copy(sb["d_re"][:, :], ps["pA_re"][:, :])
        ve.tensor_copy(sb["d_im"][:, :], ps["pA_im"][:, :])
        ve.tensor_mul(sb["e_re"][:, :], sb["d_re"][:, :], C["t_re"][:, :])
        ve.tensor_mul(sb["scr"][:, :], sb["d_im"][:, :], C["t_im"][:, :])
        ve.tensor_sub(sb["e_re"][:, :], sb["e_re"][:, :], sb["scr"][:, :])
        ve.tensor_mul(sb["e_im"][:, :], sb["d_re"][:, :], C["t_im"][:, :])
        ve.tensor_mul(sb["scr"][:, :], sb["d_im"][:, :], C["t_re"][:, :])
        inst = ve.tensor_add(sb["e_im"][:, :], sb["e_im"][:, :], sb["scr"][:, :])
        S["ve"].inc(inst)

    @blk.tensor
    def _(te):
        S["ve"].wait(te)
        te.matmul(ps["pB_re"][:, :], C["f_re"][:, :], sb["e_re"][:, :], start=True, stop=False)
        te.matmul(ps["pB_re"][:, :], C["f_imN"][:, :], sb["e_im"][:, :], start=False, stop=True)
        te.matmul(ps["pB_im"][:, :], C["f_im"][:, :], sb["e_re"][:, :], start=True, stop=False)
        inst = te.matmul(ps["pB_im"][:, :], C["f_re"][:, :], sb["e_im"][:, :], start=False, stop=True)
        S["te"].inc(inst)

    @blk.vector
    def _(ve):
        S["te"].wait(ve)
        ve.tensor_copy(out_re.rearrange("p c k -> p (c k)"), ps["pB_re"][:, :])
        inst = ve.tensor_copy(out_im.rearrange("p c k -> p (c k)"), ps["pB_im"][:, :])
        S["ve"].inc(inst)


def emit_ifft4(S, pr_re, pr_im, C, ps, sb, y_re, y_im):
    blk = S["blk"]

    @blk.tensor
    def _(te):
        S["ve"].wait(te)
        for c in range(4):
            sl = slice(128 * c, 128 * (c + 1))
            te.matmul(ps["pA_re"][:, sl], pr_re[:, c, :], C["fc_re"][:, :], start=True, stop=False)
            te.matmul(ps["pA_re"][:, sl], pr_im[:, c, :], C["fc_imN"][:, :], start=False, stop=True)
            te.matmul(ps["pA_im"][:, sl], pr_re[:, c, :], C["fc_im"][:, :], start=True, stop=False)
            inst = te.matmul(ps["pA_im"][:, sl], pr_im[:, c, :], C["fc_re"][:, :], start=False, stop=True)
        S["te"].inc(inst)

    @blk.vector
    def _(ve):
        S["te"].wait(ve)
        ve.tensor_copy(sb["d_re"][:, :], ps["pA_re"][:, :])
        ve.tensor_copy(sb["d_im"][:, :], ps["pA_im"][:, :])
        ve.tensor_mul(sb["e_re"][:, :], sb["d_re"][:, :], C["tc_re"][:, :])
        ve.tensor_mul(sb["scr"][:, :], sb["d_im"][:, :], C["tc_im"][:, :])
        ve.tensor_sub(sb["e_re"][:, :], sb["e_re"][:, :], sb["scr"][:, :])
        ve.tensor_mul(sb["e_im"][:, :], sb["d_re"][:, :], C["tc_im"][:, :])
        ve.tensor_mul(sb["scr"][:, :], sb["d_im"][:, :], C["tc_re"][:, :])
        inst = ve.tensor_add(sb["e_im"][:, :], sb["e_im"][:, :], sb["scr"][:, :])
        S["ve"].inc(inst)

    @blk.tensor
    def _(te):
        S["ve"].wait(te)
        te.matmul(ps["pB_re"][:, :], C["fc_re"][:, :], sb["e_re"][:, :], start=True, stop=False)
        te.matmul(ps["pB_re"][:, :], C["fc_imN"][:, :], sb["e_im"][:, :], start=False, stop=True)
        te.matmul(ps["pB_im"][:, :], C["fc_im"][:, :], sb["e_re"][:, :], start=True, stop=False)
        inst = te.matmul(ps["pB_im"][:, :], C["fc_re"][:, :], sb["e_im"][:, :], start=False, stop=True)
        S["te"].inc(inst)

    @blk.vector
    def _(ve):
        S["te"].wait(ve)
        ve.tensor_copy(y_re.rearrange("p c k -> p (c k)"), ps["pB_re"][0:64, :])
        inst = ve.tensor_copy(y_im.rearrange("p c k -> p (c k)"), ps["pB_im"][0:64, :])
        S["ve"].inc(inst)


def emit_cmul4(S, a_re, a_im, b_re, b_im, o_re, o_im, scr):
    blk = S["blk"]

    @blk.vector
    def _(ve):
        S["ve"].wait(ve)
        ar = a_re.rearrange("p c k -> p (c k)"); ai = a_im.rearrange("p c k -> p (c k)")
        br = b_re.rearrange("p c k -> p (c k)"); bi = b_im.rearrange("p c k -> p (c k)")
        orr = o_re.rearrange("p c k -> p (c k)"); oi = o_im.rearrange("p c k -> p (c k)")
        ve.tensor_mul(orr, ar, br)
        ve.tensor_mul(scr[:, :], ai, bi)
        ve.tensor_sub(orr, orr, scr[:, :])
        ve.tensor_mul(oi, ar, bi)
        ve.tensor_mul(scr[:, :], ai, br)
        inst = ve.tensor_add(oi, oi, scr[:, :])
        S["ve"].inc(inst)


def emit_F(nc, S, ubf_g, tb_g, ufsel, tbsel, ld_loc, fr_loc, dt_loc, fi_roll,
           iota1, cst, u_re_s, u_im_s, y_loc, nt=512, cc_uf=None, cc_tb=None):
    CH = 256
    NJ = 2
    nch = SEQ // nt
    PI = float(np.pi)
    blk = S["blk"]
    k_re_d = nc.dram_tensor("f_k_re", [CH, SEQ], BF16, kind="Internal").ap()
    k_im_d = nc.dram_tensor("f_k_im", [CH, SEQ], BF16, kind="Internal").ap()
    dec_d0 = nc.dram_tensor("f_dec0", [P, 1], F32, kind="Internal").ap()
    dec_d1 = nc.dram_tensor("f_dec1", [P, 1], F32, kind="Internal").ap()
    bnd_d = nc.dram_tensor("f_bnd_d", [4, TPC], F32, kind="Internal").ap()
    with ExitStack() as ctx:
        E = ctx.enter_context
        C = {n: E(nc.sbuf_tensor("f_" + n, [128, 512 if n.startswith("t") else 128], BF16))
             for n in cst}
        ldr = E(nc.sbuf_tensor("f_ldr", [1, CH], F32))
        frr = E(nc.sbuf_tensor("f_frr", [1, CH], F32))
        dtr = E(nc.sbuf_tensor("f_dtr", [1, CH], F32))
        fir = E(nc.sbuf_tensor("f_fir", [1, 512], F32))
        fie = E(nc.sbuf_tensor("f_fie", [1, 512], F32))
        smx = E(nc.sbuf_tensor("f_smx", [1, 4], F32))
        decr = E(nc.sbuf_tensor("f_decr", [1, CH], F32))
        omr = E(nc.sbuf_tensor("f_omr", [1, CH], F32))
        iot_c = E(nc.sbuf_tensor("f_iot", [1, 512], F32))
        dec_col = E(nc.sbuf_tensor("f_decc", [P, NJ], F32))
        amag = E(nc.sbuf_tensor("f_amag", [P, SEQ], F32))
        rscan = E(nc.sbuf_tensor("f_rscan", [P, SEQ + 1], F32))
        ones_row = E(nc.sbuf_tensor("f_ones_r", [1, P], F32))
        ones_cl = E(nc.sbuf_tensor("f_ones_c", [P, 1], F32))
        phs = E(nc.sbuf_tensor("f_phs", [P, nt], F32))
        ki = E(nc.sbuf_tensor("f_ki", [P, nt], I32))
        kf = E(nc.sbuf_tensor("f_kf", [P, nt], F32))
        cu8 = E(nc.sbuf_tensor("f_cu8", [P, nt], U8))
        sinv = E(nc.sbuf_tensor("f_sinv", [P, nt], F32))
        cosv = E(nc.sbuf_tensor("f_cosv", [P, nt], F32))
        ktmp = E(nc.sbuf_tensor("f_ktmp", [P, nt], F32))
        kre_st = E(nc.sbuf_tensor("f_kre", [P, nt], BF16))
        kim_st = E(nc.sbuf_tensor("f_kim", [P, nt], BF16))
        selu = E(nc.sbuf_tensor("f_selu", [P, 8], I32))
        selt = E(nc.sbuf_tensor("f_selt", [4, 1], I32))
        g2 = E(nc.sbuf_tensor("f_g2", [P, TPC], BF16))
        rowbuf = E(nc.sbuf_tensor("f_rowbuf", [4, TPC], F32))
        bnd_c = E(nc.sbuf_tensor("f_bnd_c", [1, 512], F32))
        ts_c = E(nc.sbuf_tensor("f_ts_c", [1, 512], F32))
        xre = E(nc.sbuf_tensor("f_xre", [64, 4, 128], BF16))
        xim = E(nc.sbuf_tensor("f_xim", [64, 4, 128], BF16))
        kre = E(nc.sbuf_tensor("f_kre2", [64, 4, 128], BF16))
        kim = E(nc.sbuf_tensor("f_kim2", [64, 4, 128], BF16))
        su_re = E(nc.sbuf_tensor("f_su_re", [128, 4, 128], BF16))
        su_im = E(nc.sbuf_tensor("f_su_im", [128, 4, 128], BF16))
        sk_re = E(nc.sbuf_tensor("f_sk_re", [128, 4, 128], BF16))
        sk_im = E(nc.sbuf_tensor("f_sk_im", [128, 4, 128], BF16))
        pr_re = E(nc.sbuf_tensor("f_pr_re", [128, 4, 128], BF16))
        pr_im = E(nc.sbuf_tensor("f_pr_im", [128, 4, 128], BF16))
        yt_re = E(nc.sbuf_tensor("f_yt_re", [64, 4, 128], BF16))
        yt_im = E(nc.sbuf_tensor("f_yt_im", [64, 4, 128], BF16))
        sb = {n: E(nc.sbuf_tensor("f_" + n, [128, 512], BF16))
              for n in ["e_re", "e_im", "e_re2", "e_im2", "scr"]}
        ps = {n: E(nc.psum_tensor("f_" + n, [128, 512], F32))
              for n in ["pA_re", "pA_im", "pB_re", "pB_im"]}
        bpctx = ExitStack()
        p_b = bpctx.enter_context(nc.psum_tensor("f_p_b", [P, nt], F32))
        p_t = bpctx.enter_context(nc.psum_tensor("f_p_t", [P, nt], F32))
        p_ph = bpctx.enter_context(nc.psum_tensor("f_p_ph", [P, nt], F32))

        @blk.sync
        def _(sy):
            for n, ap in cst.items():
                S["ld"].dma(sy.dma_start(out=C[n][:, :], in_=ap))
            for t, ap in ((ldr, ld_loc), (frr, fr_loc), (dtr, dt_loc), (fir, fi_roll)):
                S["ld"].dma(sy.dma_start(out=t[:, :], in_=ap))
            S["ld"].dma(sy.dma_start(out=selu[:, :], in_=ufsel))
            S["ld"].dma(sy.dma_start(out=selt[:, :], in_=tbsel))

        @blk.gpsimd
        def _(gp):
            S["ld"].wait(gp)
            if cc_tb is not None:
                gp.wait_ge(cc_tb, 1)
            if cc_uf is not None:
                gp.wait_ge(cc_uf, 1)
            S["gi"].dma(gp.indirect_dma_start(
                out=rowbuf[:, :], out_offset=None, in_=tb_g,
                in_offset=IndirectOffsetOnAxis(ap=selt[:, 0:1], axis=0)))

        @blk.sync
        def _(sy):
            S["gi"].wait(sy)
            S["st"].dma(sy.dma_start(out=bnd_d, in_=rowbuf[:, :]))

        for t in range(8):
            plane, hh, j2 = t // 4, (t // 2) % 2, t % 2
            dst = (u_re_s if plane == 0 else u_im_s)

            @blk.gpsimd
            def _(gp, t=t):
                S["ld"].wait(gp)
                S["st"].wait(gp)
                S["gi"].dma(gp.indirect_dma_start(
                    out=g2[:, :], out_offset=None, in_=ubf_g,
                    in_offset=IndirectOffsetOnAxis(ap=selu[:, t:t + 1], axis=0)))

            @blk.sync
            def _(sy, dst=dst, hh=hh, j2=j2):
                S["gi"].wait(sy)
                S["st"].dma(sy.dma_start(
                    out=dst[j2 * P:(j2 + 1) * P, hh * TPC:(hh + 1) * TPC], in_=g2[:, :]))

        @blk.vector
        def _(ve):
            ve.memset(ones_row[:, :], 1.0)
            ve.memset(ones_cl[:, :], 1.0)
            ve.memset(rscan[:, 0:1], 1.0)
            S["ld"].wait(ve)
            inst = ve.tensor_reduce(smx[0:1, 0:1], fir[:, :], mybir.AxisListType.X, ALU.max)
            S["ve"].inc(inst)

        @blk.vector
        def _(ve):
            S["ve"].wait(ve)
            S["vv"].inc(ve.tensor_scalar_mul(smx[0:1, 1:2], smx[0:1, 0:1], -1.0))
            S["vv"].wait(ve)
            inst = ve.tensor_scalar_add(fie[:, :], fir[:, :], smx[0:1, 1:2])
            S["ve"].inc(inst)

        @blk.scalar
        def _(sc):
            S["ve"].wait(sc)
            inst = sc.activation(fie[:, :], fie[:, :], AF.Exp)
            S["sc"].inc(inst)

        @blk.vector
        def _(ve):
            S["sc"].wait(ve)
            S["vv"].inc(ve.tensor_reduce(smx[0:1, 2:3], fie[:, :], mybir.AxisListType.X, ALU.add))
            S["vv"].wait(ve)
            S["vv"].inc(ve.reciprocal(smx[0:1, 3:4], smx[0:1, 2:3]))
            S["vv"].wait(ve)
            S["vv"].inc(ve.tensor_scalar_mul(decr[:, :], fie[0:1, 0:CH], smx[0:1, 3:4]))
            S["vv"].wait(ve)
            inst = ve.tensor_scalar_mul(decr[:, :], decr[:, :], 512.0)
            S["ve"].inc(inst)

        @blk.scalar
        def _(sc):
            S["ve"].wait(sc)
            sc.activation(ldr[:, :], ldr[:, :], AF.Sigmoid)
            inst = sc.activation(dtr[:, :], dtr[:, :], AF.Sigmoid)
            S["sc"].inc(inst)

        @blk.vector
        def _(ve):
            S["sc"].wait(ve)
            ve.tensor_mul(decr[:, :], decr[:, :], ldr[:, :])
            S["vv"].inc(ve.tensor_mul(omr[:, :], frr[:, :], dtr[:, :]))
            S["vv"].wait(ve)
            inst = ve.tensor_scalar_mul(omr[:, :], omr[:, :], 0.2)
            S["ve"].inc(inst)

        @blk.sync
        def _(sy):
            S["ve"].wait(sy)
            S["st"].dma(sy.dma_start(out=dec_d0, in_=decr[0:1, 0:P]))
            S["st"].dma(sy.dma_start(out=dec_d1, in_=decr[0:1, P:2 * P]))

        @blk.sync
        def _(sy):
            S["st"].wait(sy)
            S["ld"].dma(sy.dma_start(out=dec_col[:, 0:1], in_=dec_d0))
            S["ld"].dma(sy.dma_start(out=dec_col[:, 1:2], in_=dec_d1))

        for j in range(NJ):
            @blk.vector
            def _(ve, j=j):
                S["ld"].wait(ve)
                S["te"].wait(ve)
                inst = ve.engine_nop()
                S["ve"].inc(inst)

            for ci in range(nch):
                t0 = ci * nt
                hh = t0 // TPC
                c0s = t0 % TPC

                @blk.sync
                def _(sy, j=j, hh=hh, c0s=c0s):
                    S["te"].wait(sy)
                    S["st"].wait(sy)
                    S["ld"].dma(sy.dma_start(
                        out=bnd_c[:, 0:nt], in_=bnd_d[2 * hh + j:2 * hh + j + 1, c0s:c0s + nt]))

                @blk.tensor
                def _(te, j=j, hh=hh, c0s=c0s):
                    S["ve"].wait(te)
                    S["ld"].wait(te)
                    inst = te.matmul(p_b[:, :], ones_row[:, :], bnd_c[0:1, 0:nt],
                                     start=True, stop=True)
                    S["te"].inc(inst)

                @blk.vector
                def _(ve, j=j, t0=t0):
                    S["te"].wait(ve)
                    inst = ve.tensor_single_scalar(amag[:, t0:t0 + nt], p_b[:, :],
                                                   dec_col[:, j:j + 1], op=ALU.mult)
                    S["ve"].inc(inst)

            @blk.vector
            def _(ve, j=j):
                S["ve"].wait(ve)
                inst = ve.tensor_tensor_scan(
                    out=rscan[:, 1:SEQ + 1], data0=amag[:, :],
                    data1=ones_cl[:, 0:1].to_broadcast([P, SEQ]),
                    initial=1.0, op0=ALU.mult, op1=ALU.mult)
                S["ve"].inc(inst)

            for ci in range(nch):
                t0 = ci * nt
                hh = t0 // TPC
                c0s = t0 % TPC

                @blk.sync
                def _(sy, t0=t0, hh=hh, c0s=c0s):
                    S["te"].wait(sy)
                    if cc_tb is not None:
                        sy.wait_ge(cc_tb, 1)
                    S["ld"].dma(sy.dma_start(out=iot_c[:, 0:nt], in_=iota1[0:1, t0:t0 + nt]))
                    S["ld"].dma(sy.dma_start(
                        out=ts_c[:, 0:nt], in_=tb_g[4 + 5 * hh:5 + 5 * hh, c0s:c0s + nt]))

                @blk.tensor
                def _(te, j=j, hh=hh, c0s=c0s):
                    S["ve"].wait(te)
                    S["sc"].wait(te)
                    S["ld"].wait(te)
                    te.matmul(p_ph[:, :], omr[0:1, j * P:(j + 1) * P], iot_c[0:1, 0:nt],
                              start=True, stop=True)
                    inst = te.matmul(p_t[:, :], ones_row[:, :], ts_c[0:1, 0:nt],
                                     start=True, stop=True)
                    S["te"].inc(inst)

                @blk.vector
                def _(ve):
                    S["te"].wait(ve)
                    S["vv"].inc(ve.tensor_single_scalar(kf[:, :], p_ph[:, :], 1.0 / (2 * PI), op=ALU.mult))
                    S["vv"].wait(ve)
                    S["vv"].inc(ve.tensor_copy(ki[:, :], kf[:, :]))
                    S["vv"].wait(ve)
                    S["vv"].inc(ve.tensor_copy(kf[:, :], ki[:, :]))
                    S["vv"].wait(ve)
                    S["vv"].inc(ve.scalar_tensor_tensor(out=phs[:, :], in0=kf[:, :],
                                                        scalar=-2 * PI, in1=p_ph[:, :],
                                                        op0=ALU.mult, op1=ALU.add))
                    S["vv"].wait(ve)
                    S["vv"].inc(ve.tensor_scalar(cu8[:, :], phs[:, :], PI, None, op0=ALU.is_gt))
                    S["vv"].wait(ve)
                    S["vv"].inc(ve.scalar_tensor_tensor(out=phs[:, :], in0=cu8[:, :],
                                                        scalar=-2 * PI, in1=phs[:, :],
                                                        op0=ALU.mult, op1=ALU.add))
                    S["vv"].wait(ve)
                    S["vv"].inc(ve.tensor_scalar(cu8[:, :], phs[:, :], -PI, None, op0=ALU.is_lt))
                    S["vv"].wait(ve)
                    inst = ve.scalar_tensor_tensor(out=phs[:, :], in0=cu8[:, :],
                                                   scalar=2 * PI, in1=phs[:, :],
                                                   op0=ALU.mult, op1=ALU.add)
                    S["ve"].inc(inst)

                @blk.scalar
                def _(sc):
                    S["ve"].wait(sc)
                    inst = sc.activation(sinv[:, :], phs[:, :], AF.Sin)
                    S["sc"].inc(inst)

                @blk.vector
                def _(ve):
                    S["sc"].wait(ve)
                    S["vv"].inc(ve.tensor_scalar_add(phs[:, :], phs[:, :], PI / 2))
                    S["vv"].wait(ve)
                    S["vv"].inc(ve.tensor_scalar(cu8[:, :], phs[:, :], PI, None, op0=ALU.is_gt))
                    S["vv"].wait(ve)
                    inst = ve.scalar_tensor_tensor(out=phs[:, :], in0=cu8[:, :],
                                                   scalar=-2 * PI, in1=phs[:, :],
                                                   op0=ALU.mult, op1=ALU.add)
                    S["ve"].inc(inst)

                @blk.scalar
                def _(sc):
                    S["ve"].wait(sc)
                    inst = sc.activation(cosv[:, :], phs[:, :], AF.Sin)
                    S["sc"].inc(inst)

                @blk.vector
                def _(ve, t0=t0):
                    S["sc"].wait(ve)
                    S["st"].wait(ve)
                    S["vv"].inc(ve.tensor_mul(ktmp[:, :], rscan[:, t0:t0 + nt], p_t[:, :]))
                    S["vv"].wait(ve)
                    ve.tensor_mul(kre_st[:, :], ktmp[:, :], cosv[:, :])
                    inst = ve.tensor_mul(kim_st[:, :], ktmp[:, :], sinv[:, :])
                    S["ve"].inc(inst)

                @blk.sync
                def _(sy, j=j, t0=t0):
                    S["ve"].wait(sy)
                    S["st"].dma(sy.dma_start(
                        out=k_re_d[j * P:(j + 1) * P, t0:t0 + nt], in_=kre_st[:, :]))
                    S["st"].dma(sy.dma_start(
                        out=k_im_d[j * P:(j + 1) * P, t0:t0 + nt], in_=kim_st[:, :]))

        bpctx.close()
        psK = {n: E(nc.psum_tensor("f_k" + n, [128, 512], F32))
               for n in ["pA_re", "pA_im", "pB_re", "pB_im"]}

        def stageA(te, xr, xi, p, KA):
            for c in range(4):
                sl = slice(128 * c, 128 * (c + 1))
                te.matmul(p["pA_re"][:, sl], xr[:, c, :], C["f_re"][0:KA, :], start=True, stop=False)
                te.matmul(p["pA_re"][:, sl], xi[:, c, :], C["f_imN"][0:KA, :], start=False, stop=True)
                te.matmul(p["pA_im"][:, sl], xr[:, c, :], C["f_im"][0:KA, :], start=True, stop=False)
                inst = te.matmul(p["pA_im"][:, sl], xi[:, c, :], C["f_re"][0:KA, :], start=False, stop=True)
            return inst

        def twiddle(ve, p, er, ei, tre, tim):
            ve.tensor_mul(er[:, :], p["pA_re"][:, :], tre[:, :])
            ve.tensor_mul(sb["scr"][:, :], p["pA_im"][:, :], tim[:, :])
            ve.tensor_sub(er[:, :], er[:, :], sb["scr"][:, :])
            ve.tensor_mul(ei[:, :], p["pA_re"][:, :], tim[:, :])
            ve.tensor_mul(sb["scr"][:, :], p["pA_im"][:, :], tre[:, :])
            return ve.tensor_add(ei[:, :], ei[:, :], sb["scr"][:, :])

        def stageB(te, er, ei, p, fre, fim, fimN):
            te.matmul(p["pB_re"][:, :], fre[:, :], er[:, :], start=True, stop=False)
            te.matmul(p["pB_re"][:, :], fimN[:, :], ei[:, :], start=False, stop=True)
            te.matmul(p["pB_im"][:, :], fim[:, :], er[:, :], start=True, stop=False)
            return te.matmul(p["pB_im"][:, :], fre[:, :], ei[:, :], start=False, stop=True)

        for g in range(CH // 4):
            c0 = g * 4

            @blk.gpsimd
            def _(gp, c0=c0):
                S["ve"].wait(gp)
                S["st"].wait(gp)
                S["ld"].dma(gp.dma_start(
                    out=xre[:, :, :], in_=u_re_s[c0:c0 + 4, :].rearrange("c (a b) -> a c b", b=128)))
                S["ld"].dma(gp.dma_start(
                    out=xim[:, :, :], in_=u_im_s[c0:c0 + 4, :].rearrange("c (a b) -> a c b", b=128)))
                S["ld"].dma(gp.dma_start(
                    out=kre[:, :, :], in_=k_re_d[c0:c0 + 4, :].rearrange("c (a b) -> a c b", b=128)))
                S["ld"].dma(gp.dma_start(
                    out=kim[:, :, :], in_=k_im_d[c0:c0 + 4, :].rearrange("c (a b) -> a c b", b=128)))

            @blk.tensor
            def _(te):
                S["ld"].wait(te)
                S["ve"].wait(te)
                stageA(te, xre, xim, ps, 64)
                inst = stageA(te, kre, kim, psK, 64)
                S["te"].inc(inst)

            @blk.vector
            def _(ve):
                S["te"].wait(ve)
                inst = twiddle(ve, ps, sb["e_re"], sb["e_im"], C["t_re"], C["t_im"])
                S["ve"].inc(inst)

            @blk.tensor
            def _(te):
                S["ve"].wait(te)
                inst = stageB(te, sb["e_re"], sb["e_im"], ps, C["f_re"], C["f_im"], C["f_imN"])
                S["te"].inc(inst)

            @blk.vector
            def _(ve):
                inst = twiddle(ve, psK, sb["e_re2"], sb["e_im2"], C["t_re"], C["t_im"])
                S["ve"].inc(inst)

            @blk.tensor
            def _(te):
                S["ve"].wait(te)
                inst = stageB(te, sb["e_re2"], sb["e_im2"], psK, C["f_re"], C["f_im"], C["f_imN"])
                S["te"].inc(inst)

            @blk.vector
            def _(ve):
                S["te"].wait(ve)
                S["st"].wait(ve)
                ur = su_re.rearrange("p c k -> p (c k)"); ui = su_im.rearrange("p c k -> p (c k)")
                kr = sk_re.rearrange("p c k -> p (c k)"); ki2 = sk_im.rearrange("p c k -> p (c k)")
                prr = pr_re.rearrange("p c k -> p (c k)"); pri = pr_im.rearrange("p c k -> p (c k)")
                ve.tensor_copy(ur, ps["pB_re"][:, :])
                ve.tensor_copy(ui, ps["pB_im"][:, :])
                ve.tensor_copy(kr, psK["pB_re"][:, :])
                ve.tensor_copy(ki2, psK["pB_im"][:, :])
                ve.tensor_mul(prr, ur, kr)
                ve.tensor_mul(sb["scr"][:, :], ui, ki2)
                ve.tensor_sub(prr, prr, sb["scr"][:, :])
                ve.tensor_mul(pri, ur, ki2)
                ve.tensor_mul(sb["scr"][:, :], ui, kr)
                inst = ve.tensor_add(pri, pri, sb["scr"][:, :])
                S["ve"].inc(inst)

            @blk.tensor
            def _(te):
                S["ve"].wait(te)
                for c in range(4):
                    sl = slice(128 * c, 128 * (c + 1))
                    te.matmul(ps["pA_re"][:, sl], pr_re[:, c, :], C["fc_re"][:, :], start=True, stop=False)
                    te.matmul(ps["pA_re"][:, sl], pr_im[:, c, :], C["fc_imN"][:, :], start=False, stop=True)
                    te.matmul(ps["pA_im"][:, sl], pr_re[:, c, :], C["fc_im"][:, :], start=True, stop=False)
                    inst = te.matmul(ps["pA_im"][:, sl], pr_im[:, c, :], C["fc_re"][:, :], start=False, stop=True)
                S["te"].inc(inst)

            @blk.vector
            def _(ve):
                S["te"].wait(ve)
                inst = twiddle(ve, ps, sb["e_re"], sb["e_im"], C["tc_re"], C["tc_im"])
                S["ve"].inc(inst)

            @blk.tensor
            def _(te):
                S["ve"].wait(te)
                inst = stageB(te, sb["e_re"], sb["e_im"], ps, C["fc_re"], C["fc_im"], C["fc_imN"])
                S["te"].inc(inst)

            @blk.vector
            def _(ve):
                S["te"].wait(ve)
                ve.tensor_copy(yt_re.rearrange("p c k -> p (c k)"), ps["pB_re"][0:64, :])
                inst = ve.tensor_copy(yt_im.rearrange("p c k -> p (c k)"), ps["pB_im"][0:64, :])
                S["ve"].inc(inst)

            @blk.sync
            def _(sy, c0=c0):
                S["ve"].wait(sy)
                S["st"].dma(sy.dma_start(
                    out=y_loc[c0:c0 + 4, :].rearrange("c (a b) -> a c b", b=128),
                    in_=yt_re[0:32, :, :]))
                S["st"].dma(sy.dma_start(
                    out=y_loc[512 + c0:512 + c0 + 4, :].rearrange("c (a b) -> a c b", b=128),
                    in_=yt_re[32:64, :, :]))
                S["st"].dma(sy.dma_start(
                    out=y_loc[256 + c0:256 + c0 + 4, :].rearrange("c (a b) -> a c b", b=128),
                    in_=yt_im[0:32, :, :]))
                S["st"].dma(sy.dma_start(
                    out=y_loc[768 + c0:768 + c0 + 4, :].rearrange("c (a b) -> a c b", b=128),
                    in_=yt_im[32:64, :, :]))


def emit_G2a(nc, S, z_in, y_g, ysel, cp, gv, wfsT, osc, x4_o, nt=512, cc_gate=None):
    nch = TPC // nt
    blk = S["blk"]
    with ExitStack() as ctx:
        E = ctx.enter_context
        wfs_sb = E(nc.sbuf_tensor("h_wfs", [P, 8, D], BF16))
        osc_sb = E(nc.sbuf_tensor("h_osc", [1, 1], F32))
        osc_col = E(nc.sbuf_tensor("h_oscc", [P, 1], F32))
        ones_row = E(nc.sbuf_tensor("h_ones_r", [1, P], F32))
        sely = E(nc.sbuf_tensor("h_sely", [P, 8], I32))
        specc = E(nc.sbuf_tensor("h_spec", [P, 8, nt], BF16))
        zc = E(nc.sbuf_tensor("h_zc", [P, 8, nt], F32))
        cpc = E(nc.sbuf_tensor("h_cpc", [P, 8, nt], F32))
        gvc = E(nc.sbuf_tensor("h_gvc", [P, 8, nt], F32))
        x4c = E(nc.sbuf_tensor("h_x4c", [P, 8, nt], F32))
        scr = E(nc.sbuf_tensor("h_scr", [P, nt], F32))
        p_y = E(nc.psum_tensor("h_p_y", [P, nt], F32))
        p_o = E(nc.psum_tensor("h_p_o", [P, 1], F32))

        @blk.gpsimd
        def _(gp):
            S["ld"].dma(gp.dma_start(
                out=wfs_sb[:, :, :], in_=wfsT.rearrange("(kt p) d -> p kt d", p=P)))

        @blk.sync
        def _(sy):
            S["ld"].dma(sy.dma_start(out=osc_sb[:, :], in_=osc))
            S["ld"].dma(sy.dma_start(out=sely[:, :], in_=ysel))

        @blk.vector
        def _(ve):
            inst = ve.memset(ones_row[:, :], 1.0)
            S["ve"].inc(inst)

        @blk.tensor
        def _(te):
            S["ld"].wait(te)
            S["ve"].wait(te)
            inst = te.matmul(p_o[:, :], ones_row[:, :], osc_sb[:, :], start=True, stop=True)
            S["te"].inc(inst)

        @blk.vector
        def _(ve):
            S["te"].wait(ve)
            inst = ve.tensor_copy(osc_col[:, :], p_o[:, :])
            S["ve"].inc(inst)

        for ci in range(nch):
            t0 = ci * nt

            @blk.sync
            def _(sy, t0=t0):
                S["st"].wait(sy)
                S["ve"].wait(sy)
                S["te"].wait(sy)
                S["ld"].dma(sy.dma_start(out=zc[:, :, :], in_=fm_ap(z_in, D, t0, nt)))
                S["ld"].dma(sy.dma_start(out=cpc[:, :, :], in_=fm_ap(cp, D, t0, nt)))
                S["ld"].dma(sy.dma_start(out=gvc[:, :, :], in_=fm_ap(gv, D, t0, nt)))

            @blk.gpsimd
            def _(gp, t0=t0):
                S["ve"].wait(gp)
                S["te"].wait(gp)
                S["ld"].wait(gp)
                if cc_gate is not None:
                    gp.wait_ge(cc_gate, 1)
                for kt in range(8):
                    S["gi"].dma(gp.indirect_dma_start(
                        out=specc[:, kt, :], out_offset=None, in_=y_g, element_offset=t0,
                        in_offset=IndirectOffsetOnAxis(ap=sely[:, kt:kt + 1], axis=0)))

            for dt in range(8):
                @blk.tensor
                def _(te, dt=dt):
                    S["ld"].wait(te)
                    S["gi"].wait(te)
                    S["ve"].wait(te)
                    for kt in range(8):
                        inst = te.matmul(p_y[:, :], wfs_sb[:, kt, dt * P:(dt + 1) * P],
                                         specc[:, kt, :], start=(kt == 0), stop=(kt == 7))
                    S["te"].inc(inst)

                @blk.vector
                def _(ve, dt=dt):
                    S["te"].wait(ve)
                    ve.tensor_single_scalar(scr[:, :], p_y[:, :], osc_col[:, 0:1], op=ALU.mult)
                    ve.tensor_mul(scr[:, :], scr[:, :], gvc[:, dt, :])
                    ve.tensor_add(scr[:, :], scr[:, :], cpc[:, dt, :])
                    inst = ve.scalar_tensor_tensor(
                        out=x4c[:, dt, :], in0=scr[:, :], scalar=0.5,
                        in1=zc[:, dt, :], op0=ALU.mult, op1=ALU.add)
                    S["ve"].inc(inst)

            @blk.sync
            def _(sy, t0=t0):
                S["ve"].wait(sy)
                S["st"].dma(sy.dma_start(out=fm_ap(x4_o, D, t0, nt), in_=x4c[:, :, :]))


def emit_G2f(nc, S, pfx, x4, w1T, b1p, w2T, b2p, x5_o,
             x3=None, n2g=None, n2b=None, plain_ln=False, nt=256, share_scratch=False):
    F = 4096
    nch = TPC // nt
    blk = S["blk"]
    with ExitStack() as ctx:
        E = ctx.enter_context
        w1_sb = E(nc.sbuf_tensor(pfx + "w1", [P, 8, F], BF16))
        w2_sb = E(nc.sbuf_tensor(pfx + "w2", [P, 32, D], BF16))
        b1_sb = E(nc.sbuf_tensor(pfx + "b1", [P, 32], F32))
        b2_sb = E(nc.sbuf_tensor(pfx + "b2", [P, 8], F32))
        if not plain_ln:
            n2g_sb = E(nc.sbuf_tensor(pfx + "n2g", [P, 8], F32))
            n2b_sb = E(nc.sbuf_tensor(pfx + "n2b", [P, 8], F32))
        if x3 is not None:
            x3c = E(nc.sbuf_tensor(pfx + "x3c", [P, 8, nt], F32))
        x4c = E(nc.sbuf_tensor(pfx + "x4c", [P, 8, nt], F32))
        xnb = E(nc.sbuf_tensor(pfx + "xnb", [P, 8, nt], BF16))
        h_bf = E(nc.sbuf_tensor(pfx + "h", [P, 32, nt], BF16))
        x5c = E(nc.sbuf_tensor(pfx + "x5c", [P, 8, nt], F32))
        xsq = x5c if share_scratch else E(nc.sbuf_tensor(pfx + "xsq", [P, 8, nt], F32))
        L = mk_ln_bufs(nc, E, pfx, nt, slim=share_scratch)
        p_sum = E(nc.psum_tensor(pfx + "p_sum", [1, nt], F32))
        p_sq = E(nc.psum_tensor(pfx + "p_sq", [1, nt], F32))
        p_mu = E(nc.psum_tensor(pfx + "p_mu", [P, nt], F32))
        p_rstd = E(nc.psum_tensor(pfx + "p_rstd", [P, nt], F32))
        p_h2 = [E(nc.psum_tensor(pfx + "p_h0", [P, nt], F32)),
                E(nc.psum_tensor(pfx + "p_h1", [P, nt], F32))]
        p_o2 = [E(nc.psum_tensor(pfx + "p_o0", [P, nt], F32)),
                E(nc.psum_tensor(pfx + "p_o1", [P, nt], F32))]

        @blk.gpsimd
        def _(gp):
            S["ld"].dma(gp.dma_start(
                out=w1_sb[:, :, :], in_=w1T.rearrange("(kt p) f -> p kt f", p=P)))
            S["ld"].dma(gp.dma_start(
                out=w2_sb[:, :, :], in_=w2T.rearrange("(ft p) d -> p ft d", p=P)))

        @blk.sync
        def _(sy):
            S["ld"].dma(sy.dma_start(out=b1_sb[:, :], in_=b1p))
            S["ld"].dma(sy.dma_start(out=b2_sb[:, :], in_=b2p))
            if not plain_ln:
                S["ld"].dma(sy.dma_start(out=n2g_sb[:, :], in_=n2g))
                S["ld"].dma(sy.dma_start(out=n2b_sb[:, :], in_=n2b))

        @blk.vector
        def _(ve):
            ve.memset(L["ones_col"][:, :], 1.0)
            ve.memset(L["ones_row"][:, :], 1.0)
            inst = ve.memset(L["eps_ap"][:, :], LN_EPS)
            S["ve"].inc(inst)

        for ci in range(nch):
            t0 = ci * nt

            @blk.sync
            def _(sy, t0=t0):
                S["st"].wait(sy)
                S["ve"].wait(sy)
                S["sc"].wait(sy)
                S["te"].wait(sy)
                if x3 is not None:
                    S["ld"].dma(sy.dma_start(out=x3c[:, :, :], in_=fm_ap(x3, D, t0, nt)))
                S["ld"].dma(sy.dma_start(out=x4c[:, :, :], in_=fm_ap(x4, D, t0, nt)))

            emit_ln(nc, S, x4c, 8, nt, *ln_args(L, p_sum, p_sq, p_mu, p_rstd, xsq),
                    gates=("ld",), stats_to_psum=share_scratch)
            mub = p_mu if share_scratch else L["mu_b"]
            rsb = p_rstd if share_scratch else L["rstd_b"]

            @blk.vector
            def _(ve):
                S["ve"].wait(ve)
                if share_scratch:
                    S["te"].wait(ve)
                for kt in range(8):
                    ve.tensor_sub(xsq[:, kt, :], x4c[:, kt, :], mub[:, :])
                    if plain_ln:
                        ve.tensor_mul(xsq[:, kt, :], xsq[:, kt, :], rsb[:, :])
                        inst = ve.tensor_copy(xnb[:, kt, :], xsq[:, kt, :])
                    else:
                        ve.tensor_mul(xsq[:, kt, :], xsq[:, kt, :], rsb[:, :])
                        inst = ve.tensor_scalar(
                            xnb[:, kt, :], xsq[:, kt, :],
                            n2g_sb[:, kt:kt + 1], n2b_sb[:, kt:kt + 1],
                            op0=ALU.mult, op1=ALU.add)
                S["ve"].inc(inst)

            for ft in range(32):
                @blk.tensor
                def _(te, ft=ft):
                    S["ve"].wait(te)
                    S["sc"].wait(te, lag=1)
                    for kt in range(8):
                        inst = te.matmul(p_h2[ft % 2][:, :], w1_sb[:, kt, ft * P:(ft + 1) * P],
                                         xnb[:, kt, :], start=(kt == 0), stop=(kt == 7))
                    S["te"].inc(inst)

                @blk.scalar
                def _(sc, ft=ft):
                    S["te"].wait(sc)
                    inst = sc.activation(h_bf[:, ft, :], p_h2[ft % 2][:, :], AF.Gelu,
                                         bias=b1_sb[:, ft:ft + 1])
                    S["sc"].inc(inst)

            for dt in range(8):
                @blk.tensor
                def _(te, dt=dt):
                    S["sc"].wait(te)
                    S["ve"].wait(te, lag=1)
                    for ft in range(32):
                        inst = te.matmul(p_o2[dt % 2][:, :], w2_sb[:, ft, dt * P:(dt + 1) * P],
                                         h_bf[:, ft, :], start=(ft == 0), stop=(ft == 31))
                    S["te"].inc(inst)

                @blk.vector
                def _(ve, dt=dt):
                    S["te"].wait(ve)
                    inst = ve.scalar_tensor_tensor(
                        out=x5c[:, dt, :], in0=p_o2[dt % 2][:, :], scalar=b2_sb[:, dt:dt + 1],
                        in1=x4c[:, dt, :], op0=ALU.add, op1=ALU.add)
                    if x3 is not None:
                        inst = ve.tensor_add(x5c[:, dt, :], x5c[:, dt, :], x3c[:, dt, :])
                    S["ve"].inc(inst)

            @blk.sync
            def _(sy, t0=t0):
                S["ve"].wait(sy)
                S["st"].dma(sy.dma_start(out=fm_ap(x5_o, D, t0, nt), in_=x5c[:, :, :]))


def emit_BIN(nc, S, fw1T, fw2T, bw1T_o, bw2T_o):
    F = 4096
    blk = S["blk"]
    with ExitStack() as ctx:
        E = ctx.enter_context
        wtile = E(nc.sbuf_tensor("n_wtile", [P, F], F32))
        scr = E(nc.sbuf_tensor("n_scr", [P, F], F32))
        alpha = E(nc.sbuf_tensor("n_alpha", [1, F], F32))
        alpha_b = E(nc.sbuf_tensor("n_alpha_b", [P, F], F32))
        bw_st = E(nc.sbuf_tensor("n_bw", [P, F], BF16))
        ones_col = E(nc.sbuf_tensor("n_ones_c", [P, 1], F32))
        ones_row = E(nc.sbuf_tensor("n_ones_r", [1, P], F32))
        p_al = E(nc.psum_tensor("n_p_al", [1, 512], F32))
        p_alb = E(nc.psum_tensor("n_p_alb", [P, 512], F32))

        @blk.vector
        def _(ve):
            ve.memset(ones_col[:, :], 1.0)
            inst = ve.memset(ones_row[:, :], 1.0)
            S["ve"].inc(inst)

        def binarize(wT_ap, bw_ap, nkt, nfree, dtot):
            for f0 in range(0, nfree, 512):
                for kt in range(nkt):
                    @blk.sync
                    def _(sy, kt=kt, f0=f0):
                        S["sc"].wait(sy)
                        S["te"].wait(sy)
                        S["ld"].dma(sy.dma_start(
                            out=wtile[:, 0:512],
                            in_=wT_ap.rearrange("(kt p) f -> p kt f", p=P)[:, kt, f0:f0 + 512]))

                    @blk.scalar
                    def _(sc, kt=kt):
                        S["ld"].wait(sc)
                        inst = sc.activation(scr[:, 0:512], wtile[:, 0:512], AF.Abs)
                        S["sc"].inc(inst)

                    @blk.tensor
                    def _(te, kt=kt, f0=f0):
                        S["sc"].wait(te)
                        inst = te.matmul(p_al[:, 0:512], ones_col[:, :], scr[:, 0:512],
                                         start=(kt == 0), stop=(kt == nkt - 1))
                        S["te"].inc(inst)

                @blk.scalar
                def _(sc, f0=f0):
                    S["te"].wait(sc)
                    inst = sc.mul(alpha[:, f0:f0 + 512], p_al[:, 0:512], 1.0 / dtot)
                    S["sc"].inc(inst)

            for f0 in range(0, nfree, 512):
                @blk.tensor
                def _(te, f0=f0):
                    S["sc"].wait(te)
                    S["ve"].wait(te)
                    inst = te.matmul(p_alb[:, 0:512], ones_row[:, :], alpha[0:1, f0:f0 + 512],
                                     start=True, stop=True)
                    S["te"].inc(inst)

                @blk.vector
                def _(ve, f0=f0):
                    S["te"].wait(ve)
                    inst = ve.tensor_copy(alpha_b[:, f0:f0 + 512], p_alb[:, 0:512])
                    S["ve"].inc(inst)

            for kt in range(nkt):
                @blk.sync
                def _(sy, kt=kt):
                    S["ve"].wait(sy)
                    S["sc"].wait(sy)
                    S["st"].wait(sy)
                    S["ld"].dma(sy.dma_start(
                        out=wtile[:, 0:nfree],
                        in_=wT_ap.rearrange("(kt p) f -> p kt f", p=P)[:, kt, :]))

                @blk.scalar
                def _(sc, kt=kt):
                    S["ld"].wait(sc)
                    inst = sc.activation(scr[:, 0:nfree], wtile[:, 0:nfree], AF.Sign)
                    S["sc"].inc(inst)

                @blk.vector
                def _(ve, kt=kt):
                    S["sc"].wait(ve)
                    inst = ve.tensor_mul(bw_st[:, 0:nfree], scr[:, 0:nfree],
                                         alpha_b[:, 0:nfree])
                    S["ve"].inc(inst)

                @blk.sync
                def _(sy, kt=kt):
                    S["ve"].wait(sy)
                    S["st"].dma(sy.dma_start(
                        out=bw_ap.rearrange("(kt p) f -> p kt f", p=P)[:, kt, :],
                        in_=bw_st[:, 0:nfree]))

        binarize(fw1T, bw1T_o, 8, 4096, D)
        binarize(fw2T, bw2T_o, 32, D, 4096)


def build_fused():
    nc = bass.Bass("TRN2", target_bir_lowering=False)

    def inp(name, shape, dt=F32):
        return nc.dram_tensor(name, shape, dt, kind="ExternalInput").ap()

    def intl(name, shape, dt=F32):
        return nc.dram_tensor(name, shape, dt, kind="Internal").ap()

    x = inp("x", [D, TPC])
    whgT = inp("whgT", [D, 3072])
    bhg = inp("bhg", [P, 24])
    h0p = inp("h0p", [P, 6])
    absel = inp("absel", [P, 24], I32)
    woutT = inp("woutT", [1536, D])
    bout = inp("bout", [P, 8])
    lng = inp("lng", [P, 8])
    lnb = inp("lnb", [P, 8])
    centT = inp("centT", [D, 8])
    hsel = inp("hsel", [P, 12], I32)
    identb = inp("identb", [P, P], BF16)
    triu = inp("triu", [P, P])
    iota_f = inp("iota_f", [P, 256])
    iota_i = inp("iota_i", [P, 256], I32)
    tksel = inp("tksel", [P, 32], I32)
    w1e = inp("w1e", [D, 4096])
    b1p = inp("b1p", [P, 32])
    w2e = inp("w2e", [4096, D])
    b2p = inp("b2p", [P, 8])
    wcatT = inp("wcatT", [D, 37 * P])
    bcat = inp("bcat", [P, 37])
    ng = inp("ng", [P, 8])
    nb = inp("nb", [P, 8])
    ufsel = inp("ufsel", [P, 8], I32)
    tbsel = inp("tbsel", [4, 1], I32)
    ld_loc = inp("ld_loc", [1, 256])
    fr_loc = inp("fr_loc", [1, 256])
    dt_loc = inp("dt_loc", [1, 256])
    fi_roll = inp("fi_roll", [1, 512])
    iota1 = inp("iota1", [1, SEQ])
    cst = {n: inp(n, [128, 512 if n.startswith("t") else 128], BF16)
           for n in ["f_re", "f_im", "f_imN", "fc_re", "fc_im", "fc_imN",
                     "t_re", "t_im", "tc_re", "tc_im"]}
    ysel = inp("ysel", [P, 8], I32)
    wfsT = inp("wfsT", [D, D])
    osc = inp("osc", [1, 1])
    n2g = inp("n2g", [P, 8])
    n2b = inp("n2b", [P, 8])
    w1T = inp("w1T", [D, 4096])
    b1p2 = inp("b1p2", [P, 32])
    w2T = inp("w2T", [4096, D])
    b2p2 = inp("b2p2", [P, 8])
    fw1T = inp("fw1T", [D, 4096])
    fw2T = inp("fw2T", [4096, D])
    fb1p = inp("fb1p", [P, 32])
    fb2p = inp("fb2p", [P, 8])

    ab_loc = intl("ab_loc", [3072, TPC])
    ab_g = intl("ab_g", [6144, TPC])
    h_loc = intl("h_loc", [1536, TPC], BF16)
    h_g = intl("h_g", [3072, TPC], BF16)
    x2_d = intl("x2_d", [D, TPC])
    feat_d = intl("feat_d", [D, TPC])
    sc_d = intl("sc_d", [8, TPC])
    scG = nc.dram_tensor("scG", [64, TPC], F32, kind="Internal", addr_space="Shared").ap()
    normtm = intl("normtm", [TPC, D], BF16)
    normG = nc.dram_tensor("normG", [8 * TPC, D], BF16, kind="Internal", addr_space="Shared").ap()
    t2s_d = intl("t2s_d", [P, 256], I32)
    t2sf_d = intl("t2sf_d", [32768, 1], I32)
    s2t_d = intl("s2t_d", [32768, 1], I32)
    stag = intl("stag", [TPC, D], BF16)
    slabtm = intl("slabtm", [TPC, D], BF16)
    ffnG = nc.dram_tensor("ffnG", [8 * TPC, D], BF16, kind="Internal", addr_space="Shared").ap()
    stag2 = intl("stag2", [TPC, D], BF16)
    x3_d = intl("x3_d", [D, TPC])
    z_d = intl("z_d", [D, TPC])
    cp_d = intl("cp_d", [D, TPC])
    gv_d = intl("gv_d", [D, TPC])
    ubf_loc = intl("ubf_loc", [1024, TPC], BF16)
    ubf_g = intl("ubf_g", [2048, TPC], BF16)
    tb_loc = intl("tb_loc", [5, TPC])
    tb_g = intl("tb_g", [10, TPC])
    u_re_s = intl("u_re_s", [256, SEQ], BF16)
    u_im_s = intl("u_im_s", [256, SEQ], BF16)
    y_loc = intl("y_loc", [1024, TPC], BF16)
    y_g = intl("y_g", [2048, TPC], BF16)
    x4_d = intl("x4_d", [D, TPC])
    x5_d = intl("x5_d", [D, TPC])
    bw1T_d = intl("bw1T_d", [D, 4096], BF16)
    bw2T_d = intl("bw2T_d", [4096, D], BF16)
    xout = nc.dram_tensor("xout", [D, TPC], F32, kind="ExternalOutput").ap()

    with ExitStack() as ctx:
        E = ctx.enter_context
        blk = E(nc.Block())
        S = {"blk": blk}
        for k in SEMS:
            S[k] = Sem(ctx, nc, "s_" + k)
        cch = {}
        for k in ("ab", "h", "nrm", "sco", "slab", "uf", "tb", "y"):
            cch[k] = E(nc.semaphore("cc_" + k))

        def cc(name, groups, i, o):
            @blk.gpsimd
            def _(gp):
                S["st"].wait(gp)
                gp.collective_compute(
                    "AllGather", ALU.bypass, replica_groups=groups,
                    ins=[i], outs=[o]).then_inc(cch[name], 1)

        emit_A(nc, S, x, whgT, bhg, ab_loc)
        cc("ab", PG, ab_loc, ab_g)
        barrier(S)

        emit_BIN(nc, S, fw1T, fw2T, bw1T_d, bw2T_d)
        barrier(S, [cch["ab"]])

        emit_B(nc, S, ab_g, h0p, absel, h_loc)
        cc("h", PG, h_loc, h_g)
        barrier(S)

        emit_C(nc, S, x, h_g, hsel, woutT, bout, lng, lnb, centT, identb,
               x2_d, feat_d, sc_d, normtm, cc_gate=cch["h"])
        cc("sco", AG8, sc_d, scG)
        cc("nrm", AG8, normtm, normG)
        barrier(S, [cch["sco"]])

        ectx = ExitStack()
        wbufs = emit_E_weights(nc, S, ectx, w1e, b1p, w2e, b2p)
        emit_D(nc, S, scG, triu, iota_f, iota_i, t2s_d, t2sf_d, s2t_d)
        barrier(S, [cch["nrm"]])

        emit_E(nc, S, normG, s2t_d, tksel, wbufs, identb, stag, slabtm)
        ectx.close()
        cc("slab", AG8, slabtm, ffnG)
        barrier(S)

        emit_G1(nc, S, x2_d, feat_d, ffnG, t2sf_d, tksel, wcatT, bcat, ng, nb, stag2,
                x3_d, z_d, cp_d, gv_d, ubf_loc, tb_loc, cc_gate=cch["slab"])
        cc("uf", PG, ubf_loc, ubf_g)
        cc("tb", PG, tb_loc, tb_g)
        barrier(S)

        emit_F(nc, S, ubf_g, tb_g, ufsel, tbsel, ld_loc, fr_loc, dt_loc, fi_roll,
               iota1, cst, u_re_s, u_im_s, y_loc, cc_uf=cch["uf"], cc_tb=cch["tb"])
        cc("y", PG, y_loc, y_g)
        barrier(S)

        emit_G2a(nc, S, z_d, y_g, ysel, cp_d, gv_d, wfsT, osc, x4_d, cc_gate=cch["y"])
        barrier(S)

        emit_G2f(nc, S, "i_", x4_d, w1T, b1p2, w2T, b2p2, x5_d,
                 x3=x3_d, n2g=n2g, n2b=n2b)
        barrier(S)

        emit_G2f(nc, S, "j_", x5_d, bw1T_d, fb1p, bw2T_d, fb2p, xout,
                 x3=None, plain_ln=True, nt=256, share_scratch=True)
        barrier(S)
    return nc


def _t(a):
    return np.ascontiguousarray(a.T)


def _pp(v, ntile):
    return np.ascontiguousarray(np.asarray(v).reshape(ntile, 128).T)


_NC_CACHE = []


def kernel(**ins):
    cores = list(range(8))
    if not _NC_CACHE:
        _NC_CACHE.append(build_fused())
    nc = _NC_CACHE[0]

    x = np.asarray(ins["L0_attn_x"])
    x_fm = [_t(x[b]) for b in range(4)]

    pcol = np.arange(P).reshape(P, 1)
    whgT = _t(ins["L0_attn_W_hg"])
    bhgp = _pp(ins["L0_attn_b_hg"], 24)
    h0 = np.asarray(ins["L0_attn_h0"])
    woutT = _t(ins["L0_attn_W_out"])
    boutp = _pp(ins["L0_attn_b_out"], 8)
    lngp = _pp(ins["L0_ffn_ln_g"], 8)
    lnbp = _pp(ins["L0_ffn_ln_b"], 8)
    centT = _t(ins["L0_ffn_centroids"])
    identb = np.eye(P, dtype=np.float32).astype(BF)
    triu = np.ascontiguousarray(np.triu(np.ones((P, P), np.float32), 1))
    pg, fg = np.meshgrid(np.arange(P), np.arange(256), indexing="ij")
    iota = (pg * 256 + fg)
    w1 = np.asarray(ins["L0_ffn_w1"]); b1 = np.asarray(ins["L0_ffn_b1"])
    w2 = np.asarray(ins["L0_ffn_w2"]); b2 = np.asarray(ins["L0_ffn_b2"])
    wcatT = np.concatenate([
        _t(ins["L1_attn_Wconcept"]), _t(ins["L1_attn_Wcg"]), _t(ins["L1_attn_Wsp"]),
        _t(ins["L1_attn_Wgate"]), _t(ins["L1_attn_Wfg"]), _t(ins["L1_attn_Wband"]),
        _t(ins["L1_attn_Wtg"]), np.zeros((D, 123), np.float32)], axis=1)
    bcatv = np.concatenate([
        np.asarray(ins["L1_attn_bconcept"]), np.asarray(ins["L1_attn_bcg"]),
        np.zeros(D, np.float32), np.asarray(ins["L1_attn_bgate"]),
        np.asarray(ins["L1_attn_bfg"]), np.asarray(ins["L1_attn_bband"]),
        np.asarray(ins["L1_attn_btg"]), np.zeros(123, np.float32)])
    bcatp = _pp(bcatv, 37)
    ngp = _pp(ins["L1_attn_norm_g"], 8)
    nbp = _pp(ins["L1_attn_norm_b"], 8)
    consts = fft_host_consts()
    ld_ = np.asarray(ins["L1_attn_log_decay"]); fi_ = np.asarray(ins["L1_attn_freq_importance"])
    fr_ = np.asarray(ins["L1_attn_frequencies"]); dt_ = np.asarray(ins["L1_attn_dt"])
    iota1 = np.arange(1, SEQ + 1, dtype=np.float32).reshape(1, SEQ)
    wfsT = _t(ins["L1_attn_Wfs"])
    oscv = np.ascontiguousarray(np.asarray(ins["L1_attn_output_scale"]).reshape(1, 1))
    n2gp = _pp(ins["L1_attn_norm2_g"], 8)
    n2bp = _pp(ins["L1_attn_norm2_b"], 8)
    w1Tm = _t(ins["L1_attn_W1"]); b1p2 = _pp(ins["L1_attn_b1"], 32)
    w2Tm = _t(ins["L1_attn_W2"]); b2p2 = _pp(ins["L1_attn_b2"], 8)
    fw1T = _t(ins["L1_ffn_w1"]); fb1p = _pp(ins["L1_ffn_b1"], 32)
    fw2T = _t(ins["L1_ffn_w2"]); fb2p = _pp(ins["L1_ffn_b2"], 8)

    im = []
    for c in cores:
        b, h = c // 2, c % 2
        fh = h
        absel = np.empty((P, 24), np.int32)
        for j in range(6):
            base = 768 * h + j * P
            absel[:, 4 * j + 0] = base + pcol[:, 0]
            absel[:, 4 * j + 1] = 3072 + base + pcol[:, 0]
            absel[:, 4 * j + 2] = 1536 + base + pcol[:, 0]
            absel[:, 4 * j + 3] = 4608 + base + pcol[:, 0]
        hsel = np.empty((P, 12), np.int32)
        for ct in range(12):
            hsel[:, ct] = 1536 * (ct // 6) + 768 * h + (ct % 6) * P + pcol[:, 0]
        tksel = np.empty((P, 32), np.int32)
        for r in range(32):
            tksel[:, r] = c * TPC + r * P + pcol[:, 0]
        ufsel = np.empty((P, 8), np.int32)
        for t in range(8):
            plane, hh, j2 = t // 4, (t // 2) % 2, t % 2
            ufsel[:, t] = hh * 1024 + plane * 512 + 256 * fh + j2 * P + pcol[:, 0]
        tbsel = np.array([[2 * fh], [2 * fh + 1], [5 + 2 * fh], [6 + 2 * fh]], np.int32)
        ysel = np.empty((P, 8), np.int32)
        ybase = [h * 512 + 0, h * 512 + 128, 1024 + h * 512 + 0, 1024 + h * 512 + 128,
                 h * 512 + 256, h * 512 + 384, 1024 + h * 512 + 256, 1024 + h * 512 + 384]
        for kt in range(8):
            ysel[:, kt] = ybase[kt] + pcol[:, 0]
        fs = slice(256 * fh, 256 * (fh + 1))
        d = {
            "x": np.ascontiguousarray(x_fm[b][:, h * TPC:(h + 1) * TPC]),
            "whgT": whgT, "bhg": bhgp,
            "h0p": np.ascontiguousarray(h0[768 * h:768 * (h + 1)].reshape(6, P).T),
            "absel": absel, "woutT": woutT, "bout": boutp, "lng": lngp, "lnb": lnbp,
            "centT": centT, "hsel": hsel, "identb": identb,
            "triu": triu, "iota_f": iota.astype(np.float32), "iota_i": iota.astype(np.int32),
            "tksel": tksel,
            "w1e": np.ascontiguousarray(w1[c]), "b1p": _pp(b1[c], 32),
            "w2e": np.ascontiguousarray(w2[c]), "b2p": _pp(b2[c], 8),
            "wcatT": np.ascontiguousarray(wcatT), "bcat": bcatp, "ng": ngp, "nb": nbp,
            "ufsel": ufsel, "tbsel": tbsel,
            "ld_loc": np.ascontiguousarray(ld_[fs].reshape(1, 256)),
            "fr_loc": np.ascontiguousarray(fr_[fs].reshape(1, 256)),
            "dt_loc": np.ascontiguousarray(dt_[fs].reshape(1, 256)),
            "fi_roll": np.ascontiguousarray(np.roll(fi_, -256 * fh).reshape(1, 512)),
            "iota1": iota1,
            "ysel": ysel, "wfsT": wfsT, "osc": oscv,
            "n2g": n2gp, "n2b": n2bp, "w1T": w1Tm, "b1p2": b1p2, "w2T": w2Tm, "b2p2": b2p2,
            "fw1T": fw1T, "fw2T": fw2T, "fb1p": fb1p, "fb2p": fb2p,
        }
        d.update(consts)
        im.append(d)

    r = run_bass_kernel_spmd(nc, im, core_ids=cores).results
    out = np.empty((4, SEQ, D), np.float32)
    for c in cores:
        b, h = c // 2, c % 2
        out[b, h * TPC:(h + 1) * TPC, :] = np.asarray(r[c]["xout"], np.float32).T
    return out


# version 11 (final)
# speedup vs baseline: 1.1377x; 1.1377x over previous
import numpy as np
import ml_dtypes
from contextlib import ExitStack

import concourse.bass as bass
import concourse.mybir as mybir
from concourse.bass_utils import run_bass_kernel_spmd
from concourse.bass_core_rust import IndirectOffsetOnAxis

F32 = mybir.dt.float32
BF16 = mybir.dt.bfloat16
I32 = mybir.dt.int32
U8 = mybir.dt.uint8
AF = mybir.ActivationFunctionType
ALU = mybir.AluOpType
BF = ml_dtypes.bfloat16

P = 128
D = 1024
TPC = 4096
SEQ = 8192
LN_EPS = 1e-5
PG = [[0, 1], [2, 3], [4, 5], [6, 7]]
AG8 = [list(range(8))]


class Sem:
    def __init__(self, ctx, nc, name):
        self.h = ctx.enter_context(nc.semaphore(name))
        self.n = 0

    def inc(self, inst, k=1):
        inst.then_inc(self.h, k)
        self.n += k
        return inst

    def dma(self, inst):
        return self.inc(inst, 16)

    def wait(self, eng, lag=0):
        v = self.n - lag
        if v > 0:
            eng.wait_ge(self.h, v)

    def wait_at(self, eng, v):
        if v > 0:
            eng.wait_ge(self.h, v)


def fm_ap(dram_ap, d_total, t0, nt):
    return dram_ap[:, t0:t0 + nt].rearrange("(kt p) t -> p kt t", p=P)


SEMS = ("ld", "st", "sc", "ve", "te", "vv", "gi", "tp")


def barrier(S, ccs=()):
    blk = S["blk"]
    for engname in ("tensor", "vector", "scalar", "gpsimd", "sync"):
        dec = getattr(blk, engname)

        def _mk(eng):
            for k in SEMS:
                S[k].wait(eng)
            for h in ccs:
                eng.wait_ge(h, 1)
        dec(_mk)


def emit_ln(nc, S, xc, kt_n, nt, ones_col, ones_row, p_sum, p_sq, p_mu, p_rstd,
            xsq, murow, qrow, mu2, varrow, rstdrow, eps_ap, mu_b, rstd_b,
            gates=("ld",), stats_to_psum=False):
    blk = S["blk"]
    dtot = kt_n * P

    @blk.vector
    def _(ve):
        for g in gates:
            S[g].wait(ve)
        for kt in range(kt_n):
            inst = ve.tensor_mul(xsq[:, kt, :], xc[:, kt, :], xc[:, kt, :])
        S["ve"].inc(inst)

    @blk.tensor
    def _(te):
        for g in gates:
            S[g].wait(te)
        for kt in range(kt_n):
            te.matmul(p_sum[:, :], ones_col[:, :], xc[:, kt, :],
                      start=(kt == 0), stop=(kt == kt_n - 1))
        S["ve"].wait(te)
        for kt in range(kt_n):
            inst = te.matmul(p_sq[:, :], ones_col[:, :], xsq[:, kt, :],
                             start=(kt == 0), stop=(kt == kt_n - 1))
        S["te"].inc(inst)

    @blk.scalar
    def _(sc):
        S["te"].wait(sc)
        sc.mul(murow[:, :], p_sum[:, :], 1.0 / dtot)
        inst = sc.mul(qrow[:, :], p_sq[:, :], 1.0 / dtot)
        S["sc"].inc(inst)

    @blk.vector
    def _(ve):
        S["sc"].wait(ve)
        ve.tensor_mul(mu2[:, :], murow[:, :], murow[:, :])
        ve.tensor_sub(varrow[:, :], qrow[:, :], mu2[:, :])
        S["vv"].inc(ve.tensor_scalar_add(varrow[:, :], varrow[:, :], LN_EPS))
        S["vv"].wait(ve)
        inst = ve.reciprocal(rstdrow[:, :], varrow[:, :])
        S["ve"].inc(inst)

    @blk.scalar
    def _(sc):
        S["ve"].wait(sc)
        inst = sc.activation(rstdrow[:, :], rstdrow[:, :], AF.Sqrt)
        S["sc"].inc(inst)

    @blk.tensor
    def _(te):
        S["sc"].wait(te)
        te.matmul(p_mu[:, :], ones_row[:, :], murow[:, :], start=True, stop=True)
        inst = te.matmul(p_rstd[:, :], ones_row[:, :], rstdrow[:, :], start=True, stop=True)
        S["te"].inc(inst)

    if stats_to_psum:
        return

    @blk.vector
    def _(ve):
        S["te"].wait(ve)
        ve.tensor_copy(mu_b[:, :], p_mu[:, :])
        inst = ve.tensor_copy(rstd_b[:, :], p_rstd[:, :])
        S["ve"].inc(inst)


def mk_ln_bufs(nc, E, pfx, nt, slim=False):
    d = dict(
        ones_col=E(nc.sbuf_tensor(pfx + "ones_col", [P, 1], F32)),
        ones_row=E(nc.sbuf_tensor(pfx + "ones_row", [1, P], F32)),
        eps_ap=E(nc.sbuf_tensor(pfx + "eps", [1, 1], F32)),
        murow=E(nc.sbuf_tensor(pfx + "murow", [1, nt], F32)),
        qrow=E(nc.sbuf_tensor(pfx + "qrow", [1, nt], F32)),
        varrow=E(nc.sbuf_tensor(pfx + "varrow", [1, nt], F32)),
        rstdrow=E(nc.sbuf_tensor(pfx + "rstdrow", [1, nt], F32)),
    )
    d["mu2"] = d["rstdrow"]
    if not slim:
        d["mu_b"] = E(nc.sbuf_tensor(pfx + "mu_b", [P, nt], F32))
        d["rstd_b"] = E(nc.sbuf_tensor(pfx + "rstd_b", [P, nt], F32))
    else:
        d["mu_b"] = None
        d["rstd_b"] = None
    return d


def ln_args(L, p_sum, p_sq, p_mu, p_rstd, xsq):
    return (L["ones_col"], L["ones_row"], p_sum, p_sq, p_mu, p_rstd, xsq,
            L["murow"], L["qrow"], L["mu2"], L["varrow"], L["rstdrow"],
            L["eps_ap"], L["mu_b"], L["rstd_b"])


def emit_A(nc, S, x, whgT, bhg, ab_loc, nt=512):
    DI = 1536
    HG = 2 * DI
    nch = TPC // nt
    blk = S["blk"]
    with ExitStack() as ctx:
        E = ctx.enter_context
        w_sb = E(nc.sbuf_tensor("a_w_sb", [P, 8, HG], BF16))
        bhg_sb = E(nc.sbuf_tensor("a_bhg", [P, 24], F32))
        bhg_neg = E(nc.sbuf_tensor("a_bhgn", [P, 24], F32))
        bhg_half = E(nc.sbuf_tensor("a_bhgh", [P, 24], F32))
        xc = E(nc.sbuf_tensor("a_xc", [P, 8, nt], F32))
        xsq = E(nc.sbuf_tensor("a_xsq", [P, 8, nt], F32))
        xnb = E(nc.sbuf_tensor("a_xnb", [P, 8, nt], BF16))
        L = mk_ln_bufs(nc, E, "a_", nt)
        sig_g = E(nc.sbuf_tensor("a_sig_g", [P, nt], F32))
        sig_h = E(nc.sbuf_tensor("a_sig_h", [P, nt], F32))
        lin_h = E(nc.sbuf_tensor("a_lin_h", [P, nt], F32))
        mask_h = E(nc.sbuf_tensor("a_mask_h", [P, nt], U8))
        g_h = E(nc.sbuf_tensor("a_g_h", [P, nt], F32))
        a_st2 = [E(nc.sbuf_tensor("a_ast0", [P, nt], F32)),
                 E(nc.sbuf_tensor("a_ast1", [P, nt], F32))]
        b_st2 = [E(nc.sbuf_tensor("a_bst0", [P, nt], F32)),
                 E(nc.sbuf_tensor("a_bst1", [P, nt], F32))]
        p_sum = E(nc.psum_tensor("a_p_sum", [1, nt], F32))
        p_sq = E(nc.psum_tensor("a_p_sq", [1, nt], F32))
        p_mu = E(nc.psum_tensor("a_p_mu", [P, nt], F32))
        p_rstd = E(nc.psum_tensor("a_p_rstd", [P, nt], F32))
        p_h2 = [E(nc.psum_tensor("a_p_h0", [P, nt], F32)),
                E(nc.psum_tensor("a_p_h1", [P, nt], F32))]
        p_g2 = [E(nc.psum_tensor("a_p_g0", [P, nt], F32)),
                E(nc.psum_tensor("a_p_g1", [P, nt], F32))]

        bank_guard = [(), ()]
        st_guard = [0, 0]

        @blk.gpsimd
        def _(gp):
            S["ld"].dma(gp.dma_start(
                out=w_sb[:, :, :], in_=whgT.rearrange("(kt p) c -> p kt c", p=P)))

        @blk.sync
        def _(sy):
            S["ld"].dma(sy.dma_start(out=bhg_sb[:, :], in_=bhg))

        @blk.vector
        def _(ve):
            ve.memset(L["ones_col"][:, :], 1.0)
            ve.memset(L["ones_row"][:, :], 1.0)
            ve.memset(L["eps_ap"][:, :], LN_EPS)
            S["ld"].wait(ve)
            ve.tensor_scalar_mul(bhg_neg[:, :], bhg_sb[:, :], -1.0)
            inst = ve.tensor_scalar_add(bhg_half[:, :], bhg_sb[:, :], 0.5)
            S["ve"].inc(inst)

        for ci in range(nch):
            t0 = ci * nt

            @blk.sync
            def _(sy):
                S["st"].wait(sy)
                S["ve"].wait(sy)
                S["sc"].wait(sy)
                S["te"].wait(sy)
                S["ld"].dma(sy.dma_start(out=xc[:, :, :], in_=fm_ap(x, D, t0, nt)))

            emit_ln(nc, S, xc, 8, nt, *ln_args(L, p_sum, p_sq, p_mu, p_rstd, xsq))

            @blk.vector
            def _(ve):
                S["ve"].wait(ve)
                for kt in range(8):
                    ve.tensor_sub(xsq[:, kt, :], xc[:, kt, :], L["mu_b"][:, :])
                    inst = ve.tensor_mul(xnb[:, kt, :], xsq[:, kt, :], L["rstd_b"][:, :])
                S["ve"].inc(inst)

            xnb_ready = S["ve"].n
            for j in range(12):
                @blk.tensor
                def _(te, j=j):
                    S["ve"].wait_at(te, xnb_ready)
                    for g0 in bank_guard[j % 2]:
                        S[g0[0]].wait_at(te, g0[1])
                    for kt in range(8):
                        te.matmul(p_h2[j % 2][:, :], w_sb[:, kt, j * P:(j + 1) * P],
                                  xnb[:, kt, :], start=(kt == 0), stop=(kt == 7))
                    for kt in range(8):
                        inst = te.matmul(p_g2[j % 2][:, :], w_sb[:, kt, (12 + j) * P:(13 + j) * P],
                                         xnb[:, kt, :], start=(kt == 0), stop=(kt == 7))
                    S["te"].inc(inst)

                @blk.scalar
                def _(sc, j=j):
                    S["te"].wait(sc)
                    S["ve"].wait(sc)
                    S["st"].wait_at(sc, st_guard[j % 2])
                    sc.activation(sig_g[:, :], p_g2[j % 2][:, :], AF.Sigmoid,
                                  bias=bhg_sb[:, 12 + j:13 + j])
                    sc.activation(a_st2[j % 2][:, :], p_g2[j % 2][:, :], AF.Sigmoid,
                                  bias=bhg_neg[:, 12 + j:13 + j], scale=-1.0)
                    inst = sc.activation(sig_h[:, :], p_h2[j % 2][:, :], AF.Sigmoid,
                                         bias=bhg_sb[:, j:j + 1])
                    S["sc"].inc(inst)

                @blk.vector
                def _(ve, j=j):
                    S["sc"].wait(ve)
                    ve.tensor_scalar_add(lin_h[:, :], p_h2[j % 2][:, :], bhg_half[:, j:j + 1])
                    ve.tensor_scalar(mask_h[:, :], p_h2[j % 2][:, :],
                                     bhg_neg[:, j:j + 1], None, op0=ALU.is_ge)
                    ve.select(g_h[:, :], mask_h[:, :], lin_h[:, :], sig_h[:, :])
                    inst = ve.tensor_mul(b_st2[j % 2][:, :], sig_g[:, :], g_h[:, :])
                    S["ve"].inc(inst)

                bank_guard[j % 2] = (("sc", S["sc"].n), ("ve", S["ve"].n))

                @blk.sync
                def _(sy, j=j, t0=t0):
                    S["ve"].wait(sy)
                    S["st"].dma(sy.dma_start(
                        out=ab_loc[j * P:(j + 1) * P, t0:t0 + nt], in_=a_st2[j % 2][:, :]))
                    S["st"].dma(sy.dma_start(
                        out=ab_loc[1536 + j * P:1536 + (j + 1) * P, t0:t0 + nt],
                        in_=b_st2[j % 2][:, :]))

                st_guard[j % 2] = S["st"].n


def emit_B(nc, S, ab_g, h0p, absel, h_loc):
    NJ = 6
    blk = S["blk"]
    with ExitStack() as ctx:
        E = ctx.enter_context
        s_a2 = [E(nc.sbuf_tensor("b_sa0", [P, SEQ], F32)),
                E(nc.sbuf_tensor("b_sa1", [P, SEQ], F32))]
        s_b2 = [E(nc.sbuf_tensor("b_sb0", [P, SEQ], F32)),
                E(nc.sbuf_tensor("b_sb1", [P, SEQ], F32))]
        s_h = E(nc.sbuf_tensor("b_sh", [P, SEQ], F32))
        s_hb = E(nc.sbuf_tensor("b_shb", [P, SEQ], BF16))
        s_h0 = E(nc.sbuf_tensor("b_h0", [P, NJ], F32))
        s_sig = E(nc.sbuf_tensor("b_sig", [P, NJ], F32))
        s_lin = E(nc.sbuf_tensor("b_lin", [P, NJ], F32))
        s_msk = E(nc.sbuf_tensor("b_msk", [P, NJ], U8))
        s_h0g = E(nc.sbuf_tensor("b_h0g", [P, NJ], F32))
        sel = E(nc.sbuf_tensor("b_sel", [P, 24], I32))

        @blk.sync
        def _(sy):
            S["ld"].dma(sy.dma_start(out=s_h0[:, :], in_=h0p))
            S["ld"].dma(sy.dma_start(out=sel[:, :], in_=absel))

        @blk.scalar
        def _(sc):
            S["ld"].wait(sc)
            inst = sc.activation(s_sig[:, :], s_h0[:, :], AF.Sigmoid)
            S["sc"].inc(inst)

        @blk.vector
        def _(ve):
            S["sc"].wait(ve)
            ve.tensor_scalar_add(s_lin[:, :], s_h0[:, :], 0.5)
            ve.tensor_scalar(s_msk[:, :], s_h0[:, :], 0.0, None, op0=ALU.is_ge)
            inst = ve.select(s_h0g[:, :], s_msk[:, :], s_lin[:, :], s_sig[:, :])
            S["ve"].inc(inst)

        bguard = [0, 0]
        for j in range(NJ):
            @blk.gpsimd
            def _(gp, j=j):
                S["ve"].wait_at(gp, bguard[j % 2])
                S["ld"].wait(gp)
                sa = s_a2[j % 2]
                sbb = s_b2[j % 2]
                for (dst, col) in ((sa[:, 0:TPC], 4 * j + 0), (sa[:, TPC:SEQ], 4 * j + 1),
                                   (sbb[:, 0:TPC], 4 * j + 2), (sbb[:, TPC:SEQ], 4 * j + 3)):
                    S["gi"].dma(gp.indirect_dma_start(
                        out=dst, out_offset=None, in_=ab_g,
                        in_offset=IndirectOffsetOnAxis(ap=sel[:, col:col + 1], axis=0)))

            @blk.vector
            def _(ve, j=j):
                S["gi"].wait(ve)
                S["st"].wait(ve)
                ve.tensor_tensor_scan(
                    out=s_h[:, :], data0=s_a2[j % 2][:, :], data1=s_b2[j % 2][:, :],
                    initial=s_h0g[:, j:j + 1], op0=ALU.mult, op1=ALU.add)
                inst = ve.tensor_copy(s_hb[:, :], s_h[:, :])
                S["ve"].inc(inst)
            bguard[j % 2] = S["ve"].n

            @blk.sync
            def _(sy, j=j):
                S["ve"].wait(sy)
                S["st"].dma(sy.dma_start(out=h_loc[j * P:(j + 1) * P, :], in_=s_hb[:, 0:TPC]))
                S["st"].dma(sy.dma_start(out=h_loc[768 + j * P:768 + (j + 1) * P, :],
                                         in_=s_hb[:, TPC:SEQ]))


def emit_C(nc, S, x, h_g, hsel, woutT, bout, lng, lnb, centT, identb,
           x2_o, feat_o, sc_o, normtm, nt=512, cc_gate=None):
    nch = TPC // nt
    blk = S["blk"]
    with ExitStack() as ctx:
        E = ctx.enter_context
        w_sb = E(nc.sbuf_tensor("c_w_sb", [P, 12, D], BF16))
        cent_sb = E(nc.sbuf_tensor("c_cent", [P, 8, 8], F32))
        bout_sb = E(nc.sbuf_tensor("c_bout", [P, 8], F32))
        lng_sb = E(nc.sbuf_tensor("c_lng", [P, 8], F32))
        lnb_sb = E(nc.sbuf_tensor("c_lnb", [P, 8], F32))
        idb = E(nc.sbuf_tensor("c_idb", [P, P], BF16))
        hselb = E(nc.sbuf_tensor("c_hsel", [P, 12], I32))
        hc = E(nc.sbuf_tensor("c_hc", [P, 12, nt], BF16))
        xc = E(nc.sbuf_tensor("c_xc", [P, 8, nt], F32))
        x2c = E(nc.sbuf_tensor("c_x2c", [P, 8, nt], F32))
        featc = E(nc.sbuf_tensor("c_featc", [P, 8, nt], F32))
        normc = E(nc.sbuf_tensor("c_normc", [P, 8, nt], F32))
        normb = E(nc.sbuf_tensor("c_normb", [P, 8, nt], BF16))
        xsq = E(nc.sbuf_tensor("c_xsq", [P, 8, nt], F32))
        L = mk_ln_bufs(nc, E, "c_", nt)
        sc_st = E(nc.sbuf_tensor("c_sc_st", [8, nt], F32))
        stg = [E(nc.sbuf_tensor("c_stg0", [P, D], BF16)),
               E(nc.sbuf_tensor("c_stg1", [P, D], BF16))]

        p_y2 = [E(nc.psum_tensor("c_p_y0", [P, nt], F32)),
                E(nc.psum_tensor("c_p_y1", [P, nt], F32))]
        p_sum = E(nc.psum_tensor("c_p_sum", [1, nt], F32))
        p_sq = E(nc.psum_tensor("c_p_sq", [1, nt], F32))
        p_mu = E(nc.psum_tensor("c_p_mu", [P, nt], F32))
        p_rstd = E(nc.psum_tensor("c_p_rstd", [P, nt], F32))
        p_sc = E(nc.psum_tensor("c_p_sc", [8, nt], F32))

        @blk.gpsimd
        def _(gp):
            S["ld"].dma(gp.dma_start(
                out=w_sb[:, :, :], in_=woutT.rearrange("(ct p) d -> p ct d", p=P)))

        @blk.sync
        def _(sy):
            S["ld"].dma(sy.dma_start(
                out=cent_sb[:, :, :], in_=centT.rearrange("(kt p) e -> p kt e", p=P)))
            S["ld"].dma(sy.dma_start(out=bout_sb[:, :], in_=bout))
            S["ld"].dma(sy.dma_start(out=lng_sb[:, :], in_=lng))
            S["ld"].dma(sy.dma_start(out=lnb_sb[:, :], in_=lnb))
            S["ld"].dma(sy.dma_start(out=idb[:, :], in_=identb))
            S["ld"].dma(sy.dma_start(out=hselb[:, :], in_=hsel))

        @blk.vector
        def _(ve):
            ve.memset(L["ones_col"][:, :], 1.0)
            ve.memset(L["ones_row"][:, :], 1.0)
            inst = ve.memset(L["eps_ap"][:, :], LN_EPS)
            S["ve"].inc(inst)

        tp_guard = [0, 0]
        for ci in range(nch):
            t0 = ci * nt

            @blk.sync
            def _(sy):
                S["st"].wait(sy)
                S["ve"].wait(sy)
                S["sc"].wait(sy)
                S["te"].wait(sy)
                S["ld"].dma(sy.dma_start(out=xc[:, :, :], in_=fm_ap(x, D, t0, nt)))

            @blk.gpsimd
            def _(gp, t0=t0):
                S["st"].wait(gp)
                S["ve"].wait(gp)
                S["ld"].wait(gp)
                if cc_gate is not None:
                    gp.wait_ge(cc_gate, 1)
                for ct in range(12):
                    S["gi"].dma(gp.indirect_dma_start(
                        out=hc[:, ct, :], out_offset=None, in_=h_g, element_offset=t0,
                        in_offset=IndirectOffsetOnAxis(ap=hselb[:, ct:ct + 1], axis=0)))

            yg = [0, 0]
            for dt in range(8):
                @blk.tensor
                def _(te, dt=dt):
                    S["ld"].wait(te)
                    S["gi"].wait(te)
                    S["ve"].wait_at(te, yg[dt % 2])
                    for ctj in range(12):
                        inst = te.matmul(p_y2[dt % 2][:, :], w_sb[:, ctj, dt * P:(dt + 1) * P],
                                         hc[:, ctj, :], start=(ctj == 0), stop=(ctj == 11))
                    S["te"].inc(inst)

                @blk.vector
                def _(ve, dt=dt):
                    S["te"].wait(ve)
                    inst = ve.scalar_tensor_tensor(
                        out=x2c[:, dt, :], in0=p_y2[dt % 2][:, :], scalar=bout_sb[:, dt:dt + 1],
                        in1=xc[:, dt, :], op0=ALU.add, op1=ALU.add)
                    S["ve"].inc(inst)
                yg[dt % 2] = S["ve"].n

            emit_ln(nc, S, x2c, 8, nt, *ln_args(L, p_sum, p_sq, p_mu, p_rstd, xsq),
                    gates=("ve",))

            @blk.vector
            def _(ve):
                S["ve"].wait(ve)
                for kt in range(8):
                    ve.tensor_sub(xsq[:, kt, :], x2c[:, kt, :], L["mu_b"][:, :])
                    inst = ve.tensor_mul(featc[:, kt, :], xsq[:, kt, :], L["rstd_b"][:, :])
                S["ve"].inc(inst)

            emit_ln(nc, S, featc, 8, nt, *ln_args(L, p_sum, p_sq, p_mu, p_rstd, xsq),
                    gates=("ve",))

            @blk.vector
            def _(ve):
                S["ve"].wait(ve)
                for kt in range(8):
                    ve.tensor_sub(xsq[:, kt, :], featc[:, kt, :], L["mu_b"][:, :])
                    ve.tensor_mul(xsq[:, kt, :], xsq[:, kt, :], L["rstd_b"][:, :])
                    ve.tensor_scalar(
                        normc[:, kt, :], xsq[:, kt, :],
                        lng_sb[:, kt:kt + 1], lnb_sb[:, kt:kt + 1],
                        op0=ALU.mult, op1=ALU.add)
                    inst = ve.tensor_copy(normb[:, kt, :], normc[:, kt, :])
                S["ve"].inc(inst)

            @blk.tensor
            def _(te):
                S["ve"].wait(te)
                S["sc"].wait(te)
                for kt in range(8):
                    inst = te.matmul(p_sc[:, :], cent_sb[:, kt, :], normc[:, kt, :],
                                     start=(kt == 0), stop=(kt == 7))
                S["te"].inc(inst)

            @blk.scalar
            def _(sc):
                S["te"].wait(sc)
                inst = sc.copy(sc_st[:, :], p_sc[:, :])
                S["sc"].inc(inst)

            for tt in range(4):
                for half in range(2):
                    @blk.tensor
                    def _(te, tt=tt, half=half):
                        S["ve"].wait(te)
                        for d4 in range(4):
                            dtg = half * 4 + d4
                            inst = te.matmul(p_y2[half][:, d4 * P:(d4 + 1) * P],
                                             normb[:, dtg, tt * P:(tt + 1) * P],
                                             idb[:, :], start=True, stop=True)
                        S["te"].inc(inst)

                    @blk.vector
                    def _(ve, tt=tt, half=half):
                        S["te"].wait(ve)
                        S["st"].wait_at(ve, tp_guard[tt % 2])
                        inst = ve.tensor_copy(stg[tt % 2][:, half * 512:(half + 1) * 512],
                                              p_y2[half][:, 0:512])
                        S["ve"].inc(inst)
                    yg[half] = S["ve"].n

                @blk.sync
                def _(sy, tt=tt, t0=t0):
                    S["ve"].wait(sy)
                    S["st"].dma(sy.dma_start(
                        out=normtm[t0 + tt * P:t0 + (tt + 1) * P, :], in_=stg[tt % 2][:, :]))
                tp_guard[tt % 2] = S["st"].n

            @blk.sync
            def _(sy, t0=t0):
                S["ve"].wait(sy)
                S["sc"].wait(sy)
                S["st"].dma(sy.dma_start(out=fm_ap(x2_o, D, t0, nt), in_=x2c[:, :, :]))
                S["st"].dma(sy.dma_start(out=fm_ap(feat_o, D, t0, nt), in_=featc[:, :, :]))
                S["st"].dma(sy.dma_start(out=sc_o[:, t0:t0 + nt], in_=sc_st[:, :]))


def emit_D(nc, S, scG, triu, iota_f, iota_i, t2s_o, t2sf_o, s2t_o, niter=32):
    T = 32768
    FE = 256
    CAP = 4096.0
    NEG = -3.0e38
    blk = S["blk"]
    with ExitStack() as ctx:
        E = ctx.enter_context
        s_sc = E(nc.sbuf_tensor("d_sc", [P, 8, FE], F32))
        s_m = E(nc.sbuf_tensor("d_m", [P, 8, FE], F32))
        s_cur = E(nc.sbuf_tensor("d_cur", [P, FE], F32))
        s_taken = E(nc.sbuf_tensor("d_taken", [P, FE], U8))
        s_tkf = E(nc.sbuf_tensor("d_tkf", [P, FE], F32))
        s_neg = E(nc.sbuf_tensor("d_neg", [P, FE], F32))
        s_bounds = E(nc.sbuf_tensor("d_bounds", [P, 2], F32))
        junk = E(nc.sbuf_tensor("d_junk", [P, FE], F32))
        tot_s = E(nc.sbuf_tensor("d_tot", [1, 1], F32))
        lo = E(nc.sbuf_tensor("d_lo", [P, 1], F32))
        hi = E(nc.sbuf_tensor("d_hi", [P, 1], F32))
        mid = E(nc.sbuf_tensor("d_mid", [P, 1], F32))
        lo_n = E(nc.sbuf_tensor("d_lo_n", [P, 1], F32))
        hi_n = E(nc.sbuf_tensor("d_hi_n", [P, 1], F32))
        geb = E(nc.sbuf_tensor("d_geb", [P, 1], U8))
        ones_col = E(nc.sbuf_tensor("d_ones_c", [P, 1], F32))
        ones_row = E(nc.sbuf_tensor("d_ones_r", [1, P], F32))
        rowred = E(nc.sbuf_tensor("d_rowred", [P, 2], F32))
        glob = E(nc.sbuf_tensor("d_glob", [1, 2], F32))
        brow = E(nc.sbuf_tensor("d_brow", [1, 2 * P], F32))
        glob2 = E(nc.sbuf_tensor("d_glob2", [1, 1], F32))
        s_triu = E(nc.sbuf_tensor("d_triu", [P, P], F32))
        s_iota = E(nc.sbuf_tensor("d_iota", [P, FE], F32))
        s_iota_i = E(nc.sbuf_tensor("d_iota_i", [P, FE], I32))
        s_one256 = E(nc.sbuf_tensor("d_one256", [P, FE], F32))
        s_cums = E(nc.sbuf_tensor("d_cums", [P, FE], F32))
        s_rowtot = E(nc.sbuf_tensor("d_rowtot", [P, 1], F32))
        s_pre = E(nc.sbuf_tensor("d_pre", [P, 1], F32))
        s_t2s = E(nc.sbuf_tensor("d_t2s", [P, FE], F32))
        s_t2si = E(nc.sbuf_tensor("d_t2si", [P, FE], I32))
        s_tmp = E(nc.sbuf_tensor("d_tmp", [P, FE], F32))
        p_pre = E(nc.psum_tensor("d_p_pre", [P, 1], F32))
        p_row = E(nc.psum_tensor("d_p_row", [1, 256], F32))
        p_bc = E(nc.psum_tensor("d_p_bc", [P, 2], F32))
        p_cntb = E(nc.psum_tensor("d_p_cntb", [P, 1], F32))
        bmax_d = nc.dram_tensor("d_bmax", [P, 1], F32, kind="Internal").ap()
        bmin_d = nc.dram_tensor("d_bmin", [P, 1], F32, kind="Internal").ap()

        @blk.sync
        def _(sy):
            for r in range(8):
                for e in range(8):
                    S["ld"].dma(sy.dma_start(
                        out=s_sc[16 * r:16 * (r + 1), e, :],
                        in_=scG[8 * r + e:8 * r + e + 1, :].rearrange(
                            "one (pp f) -> (one pp) f", pp=16)))
            S["ld"].dma(sy.dma_start(out=s_triu[:, :], in_=triu))
            S["ld"].dma(sy.dma_start(out=s_iota[:, :], in_=iota_f))
            S["ld"].dma(sy.dma_start(out=s_iota_i[:, :], in_=iota_i))

        @blk.vector
        def _(ve):
            ve.memset(ones_col[:, :], 1.0)
            ve.memset(ones_row[:, :], 1.0)
            ve.memset(s_taken[:, :], 0)
            ve.memset(s_tkf[:, :], 0.0)
            ve.memset(s_neg[:, :], NEG)
            S["ld"].wait(ve)
            flat = s_sc[:, :, :].rearrange("p e f -> p (e f)")
            ve.tensor_reduce(rowred[:, 0:1], flat, mybir.AxisListType.X, ALU.max)
            inst = ve.tensor_reduce(rowred[:, 1:2], flat, mybir.AxisListType.X, ALU.min)
            S["ve"].inc(inst)

        @blk.vector
        def _(ve):
            S["ve"].wait(ve)
            inst = ve.tensor_scalar_mul(rowred[:, 1:2], rowred[:, 1:2], -1.0)
            S["ve"].inc(inst)

        @blk.sync
        def _(sy):
            S["ve"].wait(sy)
            S["st"].dma(sy.dma_start(out=bmax_d, in_=rowred[:, 0:1]))
            S["st"].dma(sy.dma_start(out=bmin_d, in_=rowred[:, 1:2]))

        @blk.sync
        def _(sy):
            S["st"].wait(sy)
            S["ld"].dma(sy.dma_start(out=brow[0:1, 0:P], in_=bmax_d.rearrange("p one -> (p one)")[None, :]))
            S["ld"].dma(sy.dma_start(out=brow[0:1, P:2 * P], in_=bmin_d.rearrange("p one -> (p one)")[None, :]))

        @blk.vector
        def _(ve):
            S["ld"].wait(ve)
            ve.tensor_reduce(glob[0:1, 0:1], brow[0:1, 0:P], mybir.AxisListType.X, ALU.max)
            inst = ve.tensor_reduce(glob2[0:1, 0:1], brow[0:1, P:2 * P], mybir.AxisListType.X, ALU.max)
            S["ve"].inc(inst)

        @blk.tensor
        def _(te):
            S["ve"].wait(te)
            te.matmul(p_bc[:, 0:1], ones_row[:, :], glob[0:1, 0:1], start=True, stop=True)
            inst = te.matmul(p_bc[:, 1:2], ones_row[:, :], glob2[0:1, 0:1], start=True, stop=True)
            S["te"].inc(inst)

        @blk.vector
        def _(ve):
            S["te"].wait(ve)
            ve.tensor_scalar_add(s_bounds[:, 0:1], p_bc[:, 0:1], 1.0)
            inst = ve.tensor_scalar(s_bounds[:, 1:2], p_bc[:, 1:2], -1.0, -1.0,
                                    op0=ALU.mult, op1=ALU.add)
            S["ve"].inc(inst)

        for e in range(8):
            @blk.vector
            def _(ve, e=e):
                S["ve"].wait(ve)
                ve.select(s_cur[:, :], s_taken[:, :], s_neg[:, :], s_sc[:, e, :])
                ve.tensor_copy(hi[:, :], s_bounds[:, 0:1])
                inst = ve.tensor_copy(lo[:, :], s_bounds[:, 1:2])
                S["ve"].inc(inst)

            for it in range(niter):
                @blk.vector
                def _(ve):
                    S["ve"].wait(ve)
                    S["te"].wait(ve)
                    S["vv"].inc(ve.tensor_add(mid[:, :], lo[:, :], hi[:, :]))
                    S["vv"].wait(ve)
                    S["vv"].inc(ve.tensor_scalar_mul(mid[:, :], mid[:, :], 0.5))
                    S["vv"].wait(ve)
                    inst = ve.tensor_scalar(junk[:, :], s_cur[:, :], mid[:, :], None,
                                            op0=ALU.is_gt)
                    S["ve"].inc(inst)

                @blk.tensor
                def _(te):
                    S["ve"].wait(te)
                    inst = te.matmul(p_row[:, :], ones_col[:, :], junk[:, :],
                                     start=True, stop=True)
                    S["te"].inc(inst)

                @blk.vector
                def _(ve):
                    S["te"].wait(ve)
                    inst = ve.tensor_reduce(tot_s[:, :], p_row[:, :],
                                            mybir.AxisListType.X, ALU.add)
                    S["ve"].inc(inst)

                @blk.tensor
                def _(te):
                    S["ve"].wait(te)
                    inst = te.matmul(p_cntb[:, :], ones_row[:, :], tot_s[:, :],
                                     start=True, stop=True)
                    S["te"].inc(inst)

                @blk.vector
                def _(ve):
                    S["te"].wait(ve)
                    S["vv"].inc(ve.tensor_scalar(geb[:, :], p_cntb[:, :], CAP, None,
                                                 op0=ALU.is_ge))
                    S["vv"].wait(ve)
                    ve.select(lo_n[:, :], geb[:, :], mid[:, :], lo[:, :])
                    S["vv"].inc(ve.select(hi_n[:, :], geb[:, :], hi[:, :], mid[:, :]))
                    S["vv"].wait(ve)
                    ve.tensor_copy(lo[:, :], lo_n[:, :])
                    inst = ve.tensor_copy(hi[:, :], hi_n[:, :])
                    S["ve"].inc(inst)

            @blk.vector
            def _(ve, e=e):
                S["ve"].wait(ve)
                ve.tensor_scalar(s_m[:, e, :], s_cur[:, :], lo[:, :], None, op0=ALU.is_gt)
                ve.tensor_add(s_tkf[:, :], s_tkf[:, :], s_m[:, e, :])
                inst = ve.tensor_scalar(s_taken[:, :], s_tkf[:, :], 0.5, None, op0=ALU.is_gt)
                S["ve"].inc(inst)

        @blk.vector
        def _(ve):
            S["ve"].wait(ve)
            ve.memset(s_one256[:, :], 1.0)
            inst = ve.memset(s_t2s[:, :], 0.0)
            S["ve"].inc(inst)

        for e in range(8):
            @blk.vector
            def _(ve, e=e):
                S["ve"].wait(ve)
                S["te"].wait(ve)
                S["vv"].inc(ve.tensor_tensor_scan(
                    out=s_cums[:, :], data0=s_one256[:, :], data1=s_m[:, e, :],
                    initial=0.0, op0=ALU.mult, op1=ALU.add))
                S["vv"].wait(ve)
                inst = ve.tensor_copy(s_rowtot[:, :], s_cums[:, FE - 1:FE])
                S["ve"].inc(inst)

            @blk.tensor
            def _(te):
                S["ve"].wait(te)
                inst = te.matmul(p_pre[:, :], s_triu[:, :], s_rowtot[:, :],
                                 start=True, stop=True)
                S["te"].inc(inst)

            @blk.vector
            def _(ve, e=e):
                S["te"].wait(ve)
                S["vv"].inc(ve.tensor_copy(s_pre[:, :], p_pre[:, :]))
                S["vv"].wait(ve)
                ve.tensor_scalar_add(s_tmp[:, :], s_cums[:, :], float(4096 * e - 1))
                S["vv"].inc(ve.tensor_scalar_add(s_tmp[:, :], s_tmp[:, :], s_pre[:, :]))
                S["vv"].wait(ve)
                ve.tensor_mul(s_tmp[:, :], s_tmp[:, :], s_m[:, e, :])
                inst = ve.tensor_add(s_t2s[:, :], s_t2s[:, :], s_tmp[:, :])
                S["ve"].inc(inst)

        @blk.vector
        def _(ve):
            S["ve"].wait(ve)
            inst = ve.tensor_copy(s_t2si[:, :], s_t2s[:, :])
            S["ve"].inc(inst)

        @blk.sync
        def _(sy):
            S["ve"].wait(sy)
            S["st"].dma(sy.dma_start(out=t2s_o, in_=s_t2si[:, :]))
            S["st"].dma(sy.dma_start(
                out=t2sf_o.rearrange("(p f) one -> p (f one)", p=P), in_=s_t2si[:, :]))

        @blk.gpsimd
        def _(gp):
            S["ve"].wait(gp)
            for f in range(FE):
                S["st"].dma(gp.indirect_dma_start(
                    out=s2t_o,
                    out_offset=IndirectOffsetOnAxis(ap=s_t2si[:, f:f + 1], axis=0),
                    in_=s_iota_i[:, f:f + 1], in_offset=None))


def emit_E_weights(nc, S, ectx, w1e, b1p, w2e, b2p):
    blk = S["blk"]
    E = ectx.enter_context
    w1_sb = E(nc.sbuf_tensor("e_w1", [P, 8, 4096], BF16))
    w2_sb = E(nc.sbuf_tensor("e_w2", [P, 32, D], BF16))
    b1_sb = E(nc.sbuf_tensor("e_b1", [P, 32], F32))
    b2_sb = E(nc.sbuf_tensor("e_b2", [P, 8], F32))

    @blk.gpsimd
    def _(gp):
        S["ld"].dma(gp.dma_start(
            out=w1_sb[:, :, :], in_=w1e.rearrange("(kt p) f -> p kt f", p=P)))
        S["ld"].dma(gp.dma_start(
            out=w2_sb[:, :, :], in_=w2e.rearrange("(ft p) d -> p ft d", p=P)))

    @blk.sync
    def _(sy):
        S["ld"].dma(sy.dma_start(out=b1_sb[:, :], in_=b1p))
        S["ld"].dma(sy.dma_start(out=b2_sb[:, :], in_=b2p))
    return w1_sb, w2_sb, b1_sb, b2_sb


def emit_E(nc, S, normG, s2t_d, tksel, wbufs, identb, stag, slabtm, nt=512):
    F = 4096
    nch = TPC // nt
    blk = S["blk"]
    w1_sb, w2_sb, b1_sb, b2_sb = wbufs
    with ExitStack() as ctx:
        E = ctx.enter_context
        selb = E(nc.sbuf_tensor("e_sel", [P, 32], I32))
        idx_sb = E(nc.sbuf_tensor("e_idx", [P, 32], I32))
        idb = E(nc.sbuf_tensor("e_idb", [P, P], BF16))
        g_sb2 = [E(nc.sbuf_tensor("e_g0", [P, D], BF16)),
                 E(nc.sbuf_tensor("e_g1", [P, D], BF16))]
        nrm_c = E(nc.sbuf_tensor("e_nrm", [P, 8, nt], BF16))
        h_bf = E(nc.sbuf_tensor("e_h", [P, 32, nt], BF16))
        o_bf = E(nc.sbuf_tensor("e_o", [P, 8, nt], BF16))
        stg = [E(nc.sbuf_tensor("e_stg0", [P, D], BF16)),
               E(nc.sbuf_tensor("e_stg1", [P, D], BF16))]
        p_h2 = [E(nc.psum_tensor("e_p_h0", [P, nt], F32)),
                E(nc.psum_tensor("e_p_h1", [P, nt], F32))]
        p_o2 = [E(nc.psum_tensor("e_p_o0", [P, nt], F32)),
                E(nc.psum_tensor("e_p_o1", [P, nt], F32))]

        @blk.sync
        def _(sy):
            S["ld"].dma(sy.dma_start(out=selb[:, :], in_=tksel))
            S["ld"].dma(sy.dma_start(out=idb[:, :], in_=identb))

        @blk.gpsimd
        def _(gp):
            S["ld"].wait(gp)
            for r in range(32):
                S["gi"].dma(gp.indirect_dma_start(
                    out=idx_sb[:, r:r + 1], out_offset=None, in_=s2t_d,
                    in_offset=IndirectOffsetOnAxis(ap=selb[:, r:r + 1], axis=0)))

        gst_guard = [0, 0]
        for r in range(32):
            @blk.gpsimd
            def _(gp, r=r):
                if r < 2:
                    S["gi"].wait(gp)
                S["st"].wait_at(gp, gst_guard[r % 2])
                S["gi"].dma(gp.indirect_dma_start(
                    out=g_sb2[r % 2][:, :], out_offset=None,
                    in_=normG, in_offset=IndirectOffsetOnAxis(ap=idx_sb[:, r:r + 1], axis=0)))

            @blk.sync
            def _(sy, r=r):
                S["gi"].wait(sy)
                S["st"].dma(sy.dma_start(out=stag[r * P:(r + 1) * P, :], in_=g_sb2[r % 2][:, :]))
            gst_guard[r % 2] = S["st"].n

        tp_guard = [0, 0]
        for ci in range(nch):
            t0 = ci * nt

            @blk.scalar
            def _(sc, t0=t0):
                S["gi"].wait(sc)
                S["st"].wait(sc)
                S["te"].wait(sc)
                S["ve"].wait(sc)
                for dt in range(8):
                    S["tp"].dma(sc.dma_start_transpose(
                        out=nrm_c[:, dt, :], in_=stag[t0:t0 + nt, dt * P:(dt + 1) * P]))

            for ft in range(32):
                @blk.tensor
                def _(te, ft=ft):
                    S["tp"].wait(te)
                    S["sc"].wait(te, lag=1)
                    S["ve"].wait(te)
                    for kt in range(8):
                        inst = te.matmul(p_h2[ft % 2][:, :], w1_sb[:, kt, ft * P:(ft + 1) * P],
                                         nrm_c[:, kt, :], start=(kt == 0), stop=(kt == 7))
                    S["te"].inc(inst)

                @blk.scalar
                def _(sc, ft=ft):
                    S["te"].wait(sc)
                    inst = sc.activation(h_bf[:, ft, :], p_h2[ft % 2][:, :], AF.Gelu,
                                         bias=b1_sb[:, ft:ft + 1])
                    S["sc"].inc(inst)

            for dt in range(8):
                @blk.tensor
                def _(te, dt=dt):
                    S["sc"].wait(te)
                    S["ve"].wait(te, lag=1)
                    for ft in range(32):
                        inst = te.matmul(p_o2[dt % 2][:, :], w2_sb[:, ft, dt * P:(dt + 1) * P],
                                         h_bf[:, ft, :], start=(ft == 0), stop=(ft == 31))
                    S["te"].inc(inst)

                @blk.vector
                def _(ve, dt=dt):
                    S["te"].wait(ve)
                    inst = ve.tensor_scalar_add(o_bf[:, dt, :], p_o2[dt % 2][:, :],
                                                b2_sb[:, dt:dt + 1])
                    S["ve"].inc(inst)

            for tt in range(4):
                for half in range(2):
                    @blk.tensor
                    def _(te, tt=tt, half=half):
                        S["ve"].wait(te)
                        for d4 in range(4):
                            dtg = half * 4 + d4
                            inst = te.matmul(p_h2[half][:, d4 * P:(d4 + 1) * P],
                                             o_bf[:, dtg, tt * P:(tt + 1) * P],
                                             idb[:, :], start=True, stop=True)
                        S["te"].inc(inst)

                    @blk.vector
                    def _(ve, tt=tt, half=half):
                        S["te"].wait(ve)
                        S["st"].wait_at(ve, tp_guard[tt % 2])
                        inst = ve.tensor_copy(stg[tt % 2][:, half * 512:(half + 1) * 512],
                                              p_h2[half][:, 0:512])
                        S["ve"].inc(inst)

                @blk.sync
                def _(sy, tt=tt, t0=t0):
                    S["ve"].wait(sy)
                    S["st"].dma(sy.dma_start(
                        out=slabtm[t0 + tt * P:t0 + (tt + 1) * P, :], in_=stg[tt % 2][:, :]))
                tp_guard[tt % 2] = S["st"].n


def emit_G1(nc, S, x2, feat, ffnG, t2sf_d, tksel, wcatT, bcat, ng, nb, stag2,
            x3_o, z_o, cp_o, gv_o, ubf_loc, tb_loc, nt=256, cc_gate=None):
    NCH = 37
    nch = TPC // nt
    blk = S["blk"]
    with ExitStack() as ctx:
        E = ctx.enter_context
        w_sb = E(nc.sbuf_tensor("g_w", [P, 8, NCH * P], BF16))
        bcat_sb = E(nc.sbuf_tensor("g_bcat", [P, NCH], F32))
        ng_sb = E(nc.sbuf_tensor("g_ng", [P, 8], F32))
        nb_sb = E(nc.sbuf_tensor("g_nb", [P, 8], F32))
        selb = E(nc.sbuf_tensor("g_sel", [P, 32], I32))
        idx_sb = E(nc.sbuf_tensor("g_idx", [P, 32], I32))
        g_sb = E(nc.sbuf_tensor("g_g", [P, D], BF16))
        x2c = E(nc.sbuf_tensor("g_x2c", [P, 8, nt], F32))
        featc = E(nc.sbuf_tensor("g_featc", [P, 8, nt], F32))
        ffn_c = E(nc.sbuf_tensor("g_ffn_c", [P, 8, nt], BF16))
        x3c = E(nc.sbuf_tensor("g_x3c", [P, 8, nt], F32))
        zc = E(nc.sbuf_tensor("g_zc", [P, 8, nt], F32))
        xsq = E(nc.sbuf_tensor("g_xsq", [P, 8, nt], F32))
        xnb = E(nc.sbuf_tensor("g_xnb", [P, 8, nt], BF16))
        sig_cg = E(nc.sbuf_tensor("g_sig_cg", [P, 8, nt], F32))
        sp_st = E(nc.sbuf_tensor("g_sp", [P, 8, nt], F32))
        fg_st = E(nc.sbuf_tensor("g_fg", [P, 4, nt], F32))
        misc_st = E(nc.sbuf_tensor("g_misc", [P, nt], F32))
        cp_st = E(nc.sbuf_tensor("g_cp", [P, nt], F32))
        gv_st = E(nc.sbuf_tensor("g_gv", [P, nt], F32))
        u_st = E(nc.sbuf_tensor("g_u", [P, nt], BF16))
        L = mk_ln_bufs(nc, E, "g_", nt)
        p_sum = E(nc.psum_tensor("g_p_sum", [1, nt], F32))
        p_sq = E(nc.psum_tensor("g_p_sq", [1, nt], F32))
        p_mu = E(nc.psum_tensor("g_p_mu", [P, nt], F32))
        p_rstd = E(nc.psum_tensor("g_p_rstd", [P, nt], F32))
        p_pr2 = [E(nc.psum_tensor("g_p_pr0", [P, nt], F32)),
                 E(nc.psum_tensor("g_p_pr1", [P, nt], F32))]
        pr_guard = [(), ()]
        cp_guard = [0]
        gv_guard = [0]

        @blk.gpsimd
        def _(gp):
            S["ld"].dma(gp.dma_start(
                out=w_sb[:, :, :], in_=wcatT.rearrange("(kt p) c -> p kt c", p=P)))

        @blk.sync
        def _(sy):
            S["ld"].dma(sy.dma_start(out=bcat_sb[:, :], in_=bcat))
            S["ld"].dma(sy.dma_start(out=ng_sb[:, :], in_=ng))
            S["ld"].dma(sy.dma_start(out=nb_sb[:, :], in_=nb))
            S["ld"].dma(sy.dma_start(out=selb[:, :], in_=tksel))

        @blk.vector
        def _(ve):
            ve.memset(L["ones_col"][:, :], 1.0)
            ve.memset(L["ones_row"][:, :], 1.0)
            inst = ve.memset(L["eps_ap"][:, :], LN_EPS)
            S["ve"].inc(inst)

        @blk.gpsimd
        def _(gp):
            S["ld"].wait(gp)
            if cc_gate is not None:
                gp.wait_ge(cc_gate, 1)
            for r in range(32):
                S["gi"].dma(gp.indirect_dma_start(
                    out=idx_sb[:, r:r + 1], out_offset=None, in_=t2sf_d,
                    in_offset=IndirectOffsetOnAxis(ap=selb[:, r:r + 1], axis=0)))

        for r in range(32):
            @blk.gpsimd
            def _(gp, r=r):
                S["gi"].wait(gp)
                S["st"].wait(gp)
                S["gi"].dma(gp.indirect_dma_start(
                    out=g_sb[:, :], out_offset=None,
                    in_=ffnG, in_offset=IndirectOffsetOnAxis(ap=idx_sb[:, r:r + 1], axis=0)))

            @blk.sync
            def _(sy, r=r):
                S["gi"].wait(sy)
                S["st"].dma(sy.dma_start(out=stag2[r * P:(r + 1) * P, :], in_=g_sb[:, :]))

        for ci in range(nch):
            t0 = ci * nt

            @blk.sync
            def _(sy, t0=t0):
                S["st"].wait(sy)
                S["ve"].wait(sy)
                S["sc"].wait(sy)
                S["te"].wait(sy)
                S["ld"].dma(sy.dma_start(out=x2c[:, :, :], in_=fm_ap(x2, D, t0, nt)))
                S["ld"].dma(sy.dma_start(out=featc[:, :, :], in_=fm_ap(feat, D, t0, nt)))

            @blk.scalar
            def _(sc, t0=t0):
                S["gi"].wait(sc)
                S["st"].wait(sc)
                S["ve"].wait(sc)
                for dt in range(8):
                    S["tp"].dma(sc.dma_start_transpose(
                        out=ffn_c[:, dt, :], in_=stag2[t0:t0 + nt, dt * P:(dt + 1) * P]))

            @blk.vector
            def _(ve):
                S["ld"].wait(ve)
                S["tp"].wait(ve)
                S["st"].wait(ve)
                for kt in range(8):
                    ve.tensor_add(x3c[:, kt, :], x2c[:, kt, :], featc[:, kt, :])
                    inst = ve.tensor_add(x3c[:, kt, :], x3c[:, kt, :], ffn_c[:, kt, :])
                S["ve"].inc(inst)

            emit_ln(nc, S, x3c, 8, nt, *ln_args(L, p_sum, p_sq, p_mu, p_rstd, xsq),
                    gates=("ve",))

            @blk.vector
            def _(ve):
                S["ve"].wait(ve)
                for kt in range(8):
                    ve.tensor_sub(xsq[:, kt, :], x3c[:, kt, :], L["mu_b"][:, :])
                    inst = ve.tensor_mul(zc[:, kt, :], xsq[:, kt, :], L["rstd_b"][:, :])
                S["ve"].inc(inst)

            emit_ln(nc, S, zc, 8, nt, *ln_args(L, p_sum, p_sq, p_mu, p_rstd, xsq),
                    gates=("ve",))

            @blk.vector
            def _(ve):
                S["ve"].wait(ve)
                for kt in range(8):
                    ve.tensor_sub(xsq[:, kt, :], zc[:, kt, :], L["mu_b"][:, :])
                    ve.tensor_mul(xsq[:, kt, :], xsq[:, kt, :], L["rstd_b"][:, :])
                    inst = ve.tensor_scalar(
                        xnb[:, kt, :], xsq[:, kt, :],
                        ng_sb[:, kt:kt + 1], nb_sb[:, kt:kt + 1],
                        op0=ALU.mult, op1=ALU.add)
                S["ve"].inc(inst)

            order = list(range(8, 16)) + list(range(0, 8)) + list(range(16, NCH))
            xnb_ready = S["ve"].n
            sc_ready = S["sc"].n
            for idx, ct in enumerate(order):
                @blk.tensor
                def _(te, ct=ct, idx=idx):
                    S["ve"].wait_at(te, xnb_ready)
                    S["sc"].wait_at(te, sc_ready)
                    for g0 in pr_guard[idx % 2]:
                        S[g0[0]].wait_at(te, g0[1])
                    for kt in range(8):
                        inst = te.matmul(p_pr2[idx % 2][:, :], w_sb[:, kt, ct * P:(ct + 1) * P],
                                         xnb[:, kt, :], start=(kt == 0), stop=(kt == 7))
                    S["te"].inc(inst)

                if 8 <= ct < 16:
                    @blk.scalar
                    def _(sc, ct=ct, idx=idx):
                        S["te"].wait(sc)
                        inst = sc.activation(sig_cg[:, ct - 8, :], p_pr2[idx % 2][:, :], AF.Sigmoid,
                                             bias=bcat_sb[:, ct:ct + 1])
                        S["sc"].inc(inst)
                    pr_guard[idx % 2] = (("sc", S["sc"].n),)
                elif ct < 8:
                    @blk.vector
                    def _(ve, ct=ct, idx=idx):
                        S["te"].wait(ve)
                        S["st"].wait_at(ve, cp_guard[0])
                        inst = ve.scalar_tensor_tensor(
                            out=cp_st[:, :], in0=p_pr2[idx % 2][:, :], scalar=bcat_sb[:, ct:ct + 1],
                            in1=sig_cg[:, ct, :], op0=ALU.add, op1=ALU.mult)
                        S["ve"].inc(inst)
                    pr_guard[idx % 2] = (("ve", S["ve"].n),)

                    @blk.sync
                    def _(sy, ct=ct, t0=t0):
                        S["ve"].wait(sy)
                        S["st"].dma(sy.dma_start(
                            out=cp_o[ct * P:(ct + 1) * P, t0:t0 + nt], in_=cp_st[:, :]))
                    cp_guard[0] = S["st"].n
                elif 16 <= ct < 24:
                    @blk.vector
                    def _(ve, ct=ct, idx=idx):
                        S["te"].wait(ve)
                        inst = ve.tensor_copy(sp_st[:, ct - 16, :], p_pr2[idx % 2][:, :])
                        S["ve"].inc(inst)
                    pr_guard[idx % 2] = (("ve", S["ve"].n),)
                elif 24 <= ct < 32:
                    @blk.scalar
                    def _(sc, ct=ct, idx=idx):
                        S["te"].wait(sc)
                        S["st"].wait_at(sc, gv_guard[0])
                        inst = sc.activation(gv_st[:, :], p_pr2[idx % 2][:, :], AF.Sigmoid,
                                             bias=bcat_sb[:, ct:ct + 1])
                        S["sc"].inc(inst)
                    pr_guard[idx % 2] = (("sc", S["sc"].n),)

                    @blk.sync
                    def _(sy, ct=ct, t0=t0):
                        S["sc"].wait(sy)
                        S["st"].dma(sy.dma_start(
                            out=gv_o[(ct - 24) * P:(ct - 23) * P, t0:t0 + nt], in_=gv_st[:, :]))
                    gv_guard[0] = S["st"].n
                elif 32 <= ct < 36:
                    @blk.scalar
                    def _(sc, ct=ct, idx=idx):
                        S["te"].wait(sc)
                        inst = sc.activation(fg_st[:, ct - 32, :], p_pr2[idx % 2][:, :], AF.Sigmoid,
                                             bias=bcat_sb[:, ct:ct + 1])
                        S["sc"].inc(inst)
                    pr_guard[idx % 2] = (("sc", S["sc"].n),)
                else:
                    @blk.scalar
                    def _(sc, ct=ct, idx=idx):
                        S["te"].wait(sc)
                        inst = sc.activation(misc_st[:, :], p_pr2[idx % 2][:, :], AF.Sigmoid,
                                             bias=bcat_sb[:, ct:ct + 1])
                        S["sc"].inc(inst)
                    pr_guard[idx % 2] = (("sc", S["sc"].n),)

            for j in range(4):
                @blk.vector
                def _(ve, j=j):
                    S["ve"].wait(ve)
                    S["sc"].wait(ve)
                    S["st"].wait(ve)
                    inst = ve.tensor_mul(u_st[:, :], sp_st[:, j, :], fg_st[:, j, :])
                    S["ve"].inc(inst)

                @blk.sync
                def _(sy, j=j, t0=t0):
                    S["ve"].wait(sy)
                    S["st"].dma(sy.dma_start(
                        out=ubf_loc[j * P:(j + 1) * P, t0:t0 + nt], in_=u_st[:, :]))

                @blk.vector
                def _(ve, j=j):
                    S["st"].wait(ve)
                    inst = ve.tensor_mul(u_st[:, :], sp_st[:, 4 + j, :], fg_st[:, j, :])
                    S["ve"].inc(inst)

                @blk.sync
                def _(sy, j=j, t0=t0):
                    S["ve"].wait(sy)
                    S["st"].dma(sy.dma_start(
                        out=ubf_loc[512 + j * P:512 + (j + 1) * P, t0:t0 + nt], in_=u_st[:, :]))

            @blk.sync
            def _(sy, t0=t0):
                S["ve"].wait(sy)
                S["sc"].wait(sy)
                S["st"].dma(sy.dma_start(out=fm_ap(x3_o, D, t0, nt), in_=x3c[:, :, :]))
                S["st"].dma(sy.dma_start(out=fm_ap(z_o, D, t0, nt), in_=zc[:, :, :]))
                S["st"].dma(sy.dma_start(out=tb_loc[0:4, t0:t0 + nt], in_=misc_st[0:4, :]))
                S["st"].dma(sy.dma_start(out=tb_loc[4:5, t0:t0 + nt], in_=misc_st[4:5, :]))


N_FFT = 16384
R_F = 128


def fft_host_consts():
    k = np.arange(R_F)
    aa, bb = np.meshgrid(k, k, indexing="ij")
    w128 = np.exp(-2j * np.pi * aa * bb / R_F)
    bgrid, kgrid = np.meshgrid(k, k, indexing="ij")
    tw = np.exp(-2j * np.pi * bgrid * kgrid / N_FFT)
    twc = np.conj(tw) / N_FFT
    w128c = np.conj(w128)

    def bf(x):
        return np.ascontiguousarray(x.astype(np.float32)).astype(BF)

    t4 = np.tile(tw, (1, 4))
    tc4 = np.tile(twc, (1, 4))
    return {
        "f_re": bf(w128.real), "f_im": bf(w128.imag), "f_imN": bf(-w128.imag),
        "fc_re": bf(w128c.real), "fc_im": bf(w128c.imag), "fc_imN": bf(-w128c.imag),
        "t_re": bf(t4.real), "t_im": bf(t4.imag),
        "tc_re": bf(tc4.real), "tc_im": bf(tc4.imag),
    }


def emit_fft4(S, te_c, xre, xim, C, ps, sb, out_re, out_im, zero_pad=True):
    blk = S["blk"]
    KA = 64 if zero_pad else 128

    @blk.tensor
    def _(te):
        te_c(te)
        for c in range(4):
            sl = slice(128 * c, 128 * (c + 1))
            te.matmul(ps["pA_re"][:, sl], xre[:, c, :], C["f_re"][0:KA, :], start=True, stop=False)
            te.matmul(ps["pA_re"][:, sl], xim[:, c, :], C["f_imN"][0:KA, :], start=False, stop=True)
            te.matmul(ps["pA_im"][:, sl], xre[:, c, :], C["f_im"][0:KA, :], start=True, stop=False)
            inst = te.matmul(ps["pA_im"][:, sl], xim[:, c, :], C["f_re"][0:KA, :], start=False, stop=True)
        S["te"].inc(inst)

    @blk.vector
    def _(ve):
        S["te"].wait(ve)
        ve.tensor_copy(sb["d_re"][:, :], ps["pA_re"][:, :])
        ve.tensor_copy(sb["d_im"][:, :], ps["pA_im"][:, :])
        ve.tensor_mul(sb["e_re"][:, :], sb["d_re"][:, :], C["t_re"][:, :])
        ve.tensor_mul(sb["scr"][:, :], sb["d_im"][:, :], C["t_im"][:, :])
        ve.tensor_sub(sb["e_re"][:, :], sb["e_re"][:, :], sb["scr"][:, :])
        ve.tensor_mul(sb["e_im"][:, :], sb["d_re"][:, :], C["t_im"][:, :])
        ve.tensor_mul(sb["scr"][:, :], sb["d_im"][:, :], C["t_re"][:, :])
        inst = ve.tensor_add(sb["e_im"][:, :], sb["e_im"][:, :], sb["scr"][:, :])
        S["ve"].inc(inst)

    @blk.tensor
    def _(te):
        S["ve"].wait(te)
        te.matmul(ps["pB_re"][:, :], C["f_re"][:, :], sb["e_re"][:, :], start=True, stop=False)
        te.matmul(ps["pB_re"][:, :], C["f_imN"][:, :], sb["e_im"][:, :], start=False, stop=True)
        te.matmul(ps["pB_im"][:, :], C["f_im"][:, :], sb["e_re"][:, :], start=True, stop=False)
        inst = te.matmul(ps["pB_im"][:, :], C["f_re"][:, :], sb["e_im"][:, :], start=False, stop=True)
        S["te"].inc(inst)

    @blk.vector
    def _(ve):
        S["te"].wait(ve)
        ve.tensor_copy(out_re.rearrange("p c k -> p (c k)"), ps["pB_re"][:, :])
        inst = ve.tensor_copy(out_im.rearrange("p c k -> p (c k)"), ps["pB_im"][:, :])
        S["ve"].inc(inst)


def emit_ifft4(S, pr_re, pr_im, C, ps, sb, y_re, y_im):
    blk = S["blk"]

    @blk.tensor
    def _(te):
        S["ve"].wait(te)
        for c in range(4):
            sl = slice(128 * c, 128 * (c + 1))
            te.matmul(ps["pA_re"][:, sl], pr_re[:, c, :], C["fc_re"][:, :], start=True, stop=False)
            te.matmul(ps["pA_re"][:, sl], pr_im[:, c, :], C["fc_imN"][:, :], start=False, stop=True)
            te.matmul(ps["pA_im"][:, sl], pr_re[:, c, :], C["fc_im"][:, :], start=True, stop=False)
            inst = te.matmul(ps["pA_im"][:, sl], pr_im[:, c, :], C["fc_re"][:, :], start=False, stop=True)
        S["te"].inc(inst)

    @blk.vector
    def _(ve):
        S["te"].wait(ve)
        ve.tensor_copy(sb["d_re"][:, :], ps["pA_re"][:, :])
        ve.tensor_copy(sb["d_im"][:, :], ps["pA_im"][:, :])
        ve.tensor_mul(sb["e_re"][:, :], sb["d_re"][:, :], C["tc_re"][:, :])
        ve.tensor_mul(sb["scr"][:, :], sb["d_im"][:, :], C["tc_im"][:, :])
        ve.tensor_sub(sb["e_re"][:, :], sb["e_re"][:, :], sb["scr"][:, :])
        ve.tensor_mul(sb["e_im"][:, :], sb["d_re"][:, :], C["tc_im"][:, :])
        ve.tensor_mul(sb["scr"][:, :], sb["d_im"][:, :], C["tc_re"][:, :])
        inst = ve.tensor_add(sb["e_im"][:, :], sb["e_im"][:, :], sb["scr"][:, :])
        S["ve"].inc(inst)

    @blk.tensor
    def _(te):
        S["ve"].wait(te)
        te.matmul(ps["pB_re"][:, :], C["fc_re"][:, :], sb["e_re"][:, :], start=True, stop=False)
        te.matmul(ps["pB_re"][:, :], C["fc_imN"][:, :], sb["e_im"][:, :], start=False, stop=True)
        te.matmul(ps["pB_im"][:, :], C["fc_im"][:, :], sb["e_re"][:, :], start=True, stop=False)
        inst = te.matmul(ps["pB_im"][:, :], C["fc_re"][:, :], sb["e_im"][:, :], start=False, stop=True)
        S["te"].inc(inst)

    @blk.vector
    def _(ve):
        S["te"].wait(ve)
        ve.tensor_copy(y_re.rearrange("p c k -> p (c k)"), ps["pB_re"][0:64, :])
        inst = ve.tensor_copy(y_im.rearrange("p c k -> p (c k)"), ps["pB_im"][0:64, :])
        S["ve"].inc(inst)


def emit_cmul4(S, a_re, a_im, b_re, b_im, o_re, o_im, scr):
    blk = S["blk"]

    @blk.vector
    def _(ve):
        S["ve"].wait(ve)
        ar = a_re.rearrange("p c k -> p (c k)"); ai = a_im.rearrange("p c k -> p (c k)")
        br = b_re.rearrange("p c k -> p (c k)"); bi = b_im.rearrange("p c k -> p (c k)")
        orr = o_re.rearrange("p c k -> p (c k)"); oi = o_im.rearrange("p c k -> p (c k)")
        ve.tensor_mul(orr, ar, br)
        ve.tensor_mul(scr[:, :], ai, bi)
        ve.tensor_sub(orr, orr, scr[:, :])
        ve.tensor_mul(oi, ar, bi)
        ve.tensor_mul(scr[:, :], ai, br)
        inst = ve.tensor_add(oi, oi, scr[:, :])
        S["ve"].inc(inst)


def emit_F(nc, S, ubf_g, tb_g, ufsel, tbsel, ld_loc, fr_loc, dt_loc, fi_roll,
           iota1, cst, u_re_s, u_im_s, y_loc, nt=512, cc_uf=None, cc_tb=None):
    CH = 256
    NJ = 2
    nch = SEQ // nt
    PI = float(np.pi)
    blk = S["blk"]
    k_re_d = nc.dram_tensor("f_k_re", [CH, SEQ], BF16, kind="Internal").ap()
    k_im_d = nc.dram_tensor("f_k_im", [CH, SEQ], BF16, kind="Internal").ap()
    dec_d0 = nc.dram_tensor("f_dec0", [P, 1], F32, kind="Internal").ap()
    dec_d1 = nc.dram_tensor("f_dec1", [P, 1], F32, kind="Internal").ap()
    bnd_d = nc.dram_tensor("f_bnd_d", [4, TPC], F32, kind="Internal").ap()
    with ExitStack() as ctx:
        E = ctx.enter_context
        C = {n: E(nc.sbuf_tensor("f_" + n, [128, 512 if n.startswith("t") else 128], BF16))
             for n in cst}
        ldr = E(nc.sbuf_tensor("f_ldr", [1, CH], F32))
        frr = E(nc.sbuf_tensor("f_frr", [1, CH], F32))
        dtr = E(nc.sbuf_tensor("f_dtr", [1, CH], F32))
        fir = E(nc.sbuf_tensor("f_fir", [1, 512], F32))
        fie = E(nc.sbuf_tensor("f_fie", [1, 512], F32))
        smx = E(nc.sbuf_tensor("f_smx", [1, 4], F32))
        decr = E(nc.sbuf_tensor("f_decr", [1, CH], F32))
        omr = E(nc.sbuf_tensor("f_omr", [1, CH], F32))
        iot_c = E(nc.sbuf_tensor("f_iot", [1, 512], F32))
        dec_col = E(nc.sbuf_tensor("f_decc", [P, NJ], F32))
        amag = E(nc.sbuf_tensor("f_amag", [P, SEQ], F32))
        rscan = E(nc.sbuf_tensor("f_rscan", [P, SEQ + 1], F32))
        ones_row = E(nc.sbuf_tensor("f_ones_r", [1, P], F32))
        ones_cl = E(nc.sbuf_tensor("f_ones_c", [P, 1], F32))
        phs = E(nc.sbuf_tensor("f_phs", [P, nt], F32))
        ki = E(nc.sbuf_tensor("f_ki", [P, nt], I32))
        kf = E(nc.sbuf_tensor("f_kf", [P, nt], F32))
        cu8 = E(nc.sbuf_tensor("f_cu8", [P, nt], U8))
        sinv = E(nc.sbuf_tensor("f_sinv", [P, nt], F32))
        cosv = E(nc.sbuf_tensor("f_cosv", [P, nt], F32))
        ktmp = E(nc.sbuf_tensor("f_ktmp", [P, nt], F32))
        kre_st = E(nc.sbuf_tensor("f_kre", [P, nt], BF16))
        kim_st = E(nc.sbuf_tensor("f_kim", [P, nt], BF16))
        selu = E(nc.sbuf_tensor("f_selu", [P, 8], I32))
        selt = E(nc.sbuf_tensor("f_selt", [4, 1], I32))
        g2 = E(nc.sbuf_tensor("f_g2", [P, TPC], BF16))
        rowbuf = E(nc.sbuf_tensor("f_rowbuf", [4, TPC], F32))
        bnd_c = E(nc.sbuf_tensor("f_bnd_c", [1, 512], F32))
        ts_c = E(nc.sbuf_tensor("f_ts_c", [1, 512], F32))
        xre = E(nc.sbuf_tensor("f_xre", [64, 4, 128], BF16))
        xim = E(nc.sbuf_tensor("f_xim", [64, 4, 128], BF16))
        kre = E(nc.sbuf_tensor("f_kre2", [64, 4, 128], BF16))
        kim = E(nc.sbuf_tensor("f_kim2", [64, 4, 128], BF16))
        su_re = E(nc.sbuf_tensor("f_su_re", [128, 4, 128], BF16))
        su_im = E(nc.sbuf_tensor("f_su_im", [128, 4, 128], BF16))
        sk_re = E(nc.sbuf_tensor("f_sk_re", [128, 4, 128], BF16))
        sk_im = E(nc.sbuf_tensor("f_sk_im", [128, 4, 128], BF16))
        pr_re = E(nc.sbuf_tensor("f_pr_re", [128, 4, 128], BF16))
        pr_im = E(nc.sbuf_tensor("f_pr_im", [128, 4, 128], BF16))
        yt_re = E(nc.sbuf_tensor("f_yt_re", [64, 4, 128], BF16))
        yt_im = E(nc.sbuf_tensor("f_yt_im", [64, 4, 128], BF16))
        sb = {n: E(nc.sbuf_tensor("f_" + n, [128, 512], BF16))
              for n in ["e_re", "e_im", "e_re2", "e_im2", "scr"]}
        ps = {n: E(nc.psum_tensor("f_" + n, [128, 512], F32))
              for n in ["pA_re", "pA_im", "pB_re", "pB_im"]}
        bpctx = ExitStack()
        p_b = bpctx.enter_context(nc.psum_tensor("f_p_b", [P, nt], F32))
        p_t = bpctx.enter_context(nc.psum_tensor("f_p_t", [P, nt], F32))
        p_ph = bpctx.enter_context(nc.psum_tensor("f_p_ph", [P, nt], F32))

        @blk.sync
        def _(sy):
            for n, ap in cst.items():
                S["ld"].dma(sy.dma_start(out=C[n][:, :], in_=ap))
            for t, ap in ((ldr, ld_loc), (frr, fr_loc), (dtr, dt_loc), (fir, fi_roll)):
                S["ld"].dma(sy.dma_start(out=t[:, :], in_=ap))
            S["ld"].dma(sy.dma_start(out=selu[:, :], in_=ufsel))
            S["ld"].dma(sy.dma_start(out=selt[:, :], in_=tbsel))

        @blk.gpsimd
        def _(gp):
            S["ld"].wait(gp)
            if cc_tb is not None:
                gp.wait_ge(cc_tb, 1)
            if cc_uf is not None:
                gp.wait_ge(cc_uf, 1)
            S["gi"].dma(gp.indirect_dma_start(
                out=rowbuf[:, :], out_offset=None, in_=tb_g,
                in_offset=IndirectOffsetOnAxis(ap=selt[:, 0:1], axis=0)))

        @blk.sync
        def _(sy):
            S["gi"].wait(sy)
            S["st"].dma(sy.dma_start(out=bnd_d, in_=rowbuf[:, :]))

        for t in range(8):
            plane, hh, j2 = t // 4, (t // 2) % 2, t % 2
            dst = (u_re_s if plane == 0 else u_im_s)

            @blk.gpsimd
            def _(gp, t=t):
                S["ld"].wait(gp)
                S["st"].wait(gp)
                S["gi"].dma(gp.indirect_dma_start(
                    out=g2[:, :], out_offset=None, in_=ubf_g,
                    in_offset=IndirectOffsetOnAxis(ap=selu[:, t:t + 1], axis=0)))

            @blk.sync
            def _(sy, dst=dst, hh=hh, j2=j2):
                S["gi"].wait(sy)
                S["st"].dma(sy.dma_start(
                    out=dst[j2 * P:(j2 + 1) * P, hh * TPC:(hh + 1) * TPC], in_=g2[:, :]))

        @blk.vector
        def _(ve):
            ve.memset(ones_row[:, :], 1.0)
            ve.memset(ones_cl[:, :], 1.0)
            ve.memset(rscan[:, 0:1], 1.0)
            S["ld"].wait(ve)
            inst = ve.tensor_reduce(smx[0:1, 0:1], fir[:, :], mybir.AxisListType.X, ALU.max)
            S["ve"].inc(inst)

        @blk.vector
        def _(ve):
            S["ve"].wait(ve)
            S["vv"].inc(ve.tensor_scalar_mul(smx[0:1, 1:2], smx[0:1, 0:1], -1.0))
            S["vv"].wait(ve)
            inst = ve.tensor_scalar_add(fie[:, :], fir[:, :], smx[0:1, 1:2])
            S["ve"].inc(inst)

        @blk.scalar
        def _(sc):
            S["ve"].wait(sc)
            inst = sc.activation(fie[:, :], fie[:, :], AF.Exp)
            S["sc"].inc(inst)

        @blk.vector
        def _(ve):
            S["sc"].wait(ve)
            S["vv"].inc(ve.tensor_reduce(smx[0:1, 2:3], fie[:, :], mybir.AxisListType.X, ALU.add))
            S["vv"].wait(ve)
            S["vv"].inc(ve.reciprocal(smx[0:1, 3:4], smx[0:1, 2:3]))
            S["vv"].wait(ve)
            S["vv"].inc(ve.tensor_scalar_mul(decr[:, :], fie[0:1, 0:CH], smx[0:1, 3:4]))
            S["vv"].wait(ve)
            inst = ve.tensor_scalar_mul(decr[:, :], decr[:, :], 512.0)
            S["ve"].inc(inst)

        @blk.scalar
        def _(sc):
            S["ve"].wait(sc)
            sc.activation(ldr[:, :], ldr[:, :], AF.Sigmoid)
            inst = sc.activation(dtr[:, :], dtr[:, :], AF.Sigmoid)
            S["sc"].inc(inst)

        @blk.vector
        def _(ve):
            S["sc"].wait(ve)
            ve.tensor_mul(decr[:, :], decr[:, :], ldr[:, :])
            S["vv"].inc(ve.tensor_mul(omr[:, :], frr[:, :], dtr[:, :]))
            S["vv"].wait(ve)
            inst = ve.tensor_scalar_mul(omr[:, :], omr[:, :], 0.2)
            S["ve"].inc(inst)

        @blk.sync
        def _(sy):
            S["ve"].wait(sy)
            S["st"].dma(sy.dma_start(out=dec_d0, in_=decr[0:1, 0:P]))
            S["st"].dma(sy.dma_start(out=dec_d1, in_=decr[0:1, P:2 * P]))

        @blk.sync
        def _(sy):
            S["st"].wait(sy)
            S["ld"].dma(sy.dma_start(out=dec_col[:, 0:1], in_=dec_d0))
            S["ld"].dma(sy.dma_start(out=dec_col[:, 1:2], in_=dec_d1))

        for j in range(NJ):
            @blk.vector
            def _(ve, j=j):
                S["ld"].wait(ve)
                S["te"].wait(ve)
                inst = ve.engine_nop()
                S["ve"].inc(inst)

            for ci in range(nch):
                t0 = ci * nt
                hh = t0 // TPC
                c0s = t0 % TPC

                @blk.sync
                def _(sy, j=j, hh=hh, c0s=c0s):
                    S["te"].wait(sy)
                    S["st"].wait(sy)
                    S["ld"].dma(sy.dma_start(
                        out=bnd_c[:, 0:nt], in_=bnd_d[2 * hh + j:2 * hh + j + 1, c0s:c0s + nt]))

                @blk.tensor
                def _(te, j=j, hh=hh, c0s=c0s):
                    S["ve"].wait(te)
                    S["ld"].wait(te)
                    inst = te.matmul(p_b[:, :], ones_row[:, :], bnd_c[0:1, 0:nt],
                                     start=True, stop=True)
                    S["te"].inc(inst)

                @blk.vector
                def _(ve, j=j, t0=t0):
                    S["te"].wait(ve)
                    inst = ve.tensor_single_scalar(amag[:, t0:t0 + nt], p_b[:, :],
                                                   dec_col[:, j:j + 1], op=ALU.mult)
                    S["ve"].inc(inst)

            @blk.vector
            def _(ve, j=j):
                S["ve"].wait(ve)
                inst = ve.tensor_tensor_scan(
                    out=rscan[:, 1:SEQ + 1], data0=amag[:, :],
                    data1=ones_cl[:, 0:1].to_broadcast([P, SEQ]),
                    initial=1.0, op0=ALU.mult, op1=ALU.mult)
                S["ve"].inc(inst)

            for ci in range(nch):
                t0 = ci * nt
                hh = t0 // TPC
                c0s = t0 % TPC

                @blk.sync
                def _(sy, t0=t0, hh=hh, c0s=c0s):
                    S["te"].wait(sy)
                    if cc_tb is not None:
                        sy.wait_ge(cc_tb, 1)
                    S["ld"].dma(sy.dma_start(out=iot_c[:, 0:nt], in_=iota1[0:1, t0:t0 + nt]))
                    S["ld"].dma(sy.dma_start(
                        out=ts_c[:, 0:nt], in_=tb_g[4 + 5 * hh:5 + 5 * hh, c0s:c0s + nt]))

                @blk.tensor
                def _(te, j=j, hh=hh, c0s=c0s):
                    S["ve"].wait(te)
                    S["sc"].wait(te)
                    S["ld"].wait(te)
                    te.matmul(p_ph[:, :], omr[0:1, j * P:(j + 1) * P], iot_c[0:1, 0:nt],
                              start=True, stop=True)
                    inst = te.matmul(p_t[:, :], ones_row[:, :], ts_c[0:1, 0:nt],
                                     start=True, stop=True)
                    S["te"].inc(inst)

                @blk.vector
                def _(ve):
                    S["te"].wait(ve)
                    S["vv"].inc(ve.tensor_single_scalar(kf[:, :], p_ph[:, :], 1.0 / (2 * PI), op=ALU.mult))
                    S["vv"].wait(ve)
                    S["vv"].inc(ve.tensor_copy(ki[:, :], kf[:, :]))
                    S["vv"].wait(ve)
                    S["vv"].inc(ve.tensor_copy(kf[:, :], ki[:, :]))
                    S["vv"].wait(ve)
                    S["vv"].inc(ve.scalar_tensor_tensor(out=phs[:, :], in0=kf[:, :],
                                                        scalar=-2 * PI, in1=p_ph[:, :],
                                                        op0=ALU.mult, op1=ALU.add))
                    S["vv"].wait(ve)
                    S["vv"].inc(ve.tensor_scalar(cu8[:, :], phs[:, :], PI, None, op0=ALU.is_gt))
                    S["vv"].wait(ve)
                    S["vv"].inc(ve.scalar_tensor_tensor(out=phs[:, :], in0=cu8[:, :],
                                                        scalar=-2 * PI, in1=phs[:, :],
                                                        op0=ALU.mult, op1=ALU.add))
                    S["vv"].wait(ve)
                    S["vv"].inc(ve.tensor_scalar(cu8[:, :], phs[:, :], -PI, None, op0=ALU.is_lt))
                    S["vv"].wait(ve)
                    inst = ve.scalar_tensor_tensor(out=phs[:, :], in0=cu8[:, :],
                                                   scalar=2 * PI, in1=phs[:, :],
                                                   op0=ALU.mult, op1=ALU.add)
                    S["ve"].inc(inst)

                @blk.scalar
                def _(sc):
                    S["ve"].wait(sc)
                    inst = sc.activation(sinv[:, :], phs[:, :], AF.Sin)
                    S["sc"].inc(inst)

                @blk.vector
                def _(ve):
                    S["sc"].wait(ve)
                    S["vv"].inc(ve.tensor_scalar_add(phs[:, :], phs[:, :], PI / 2))
                    S["vv"].wait(ve)
                    S["vv"].inc(ve.tensor_scalar(cu8[:, :], phs[:, :], PI, None, op0=ALU.is_gt))
                    S["vv"].wait(ve)
                    inst = ve.scalar_tensor_tensor(out=phs[:, :], in0=cu8[:, :],
                                                   scalar=-2 * PI, in1=phs[:, :],
                                                   op0=ALU.mult, op1=ALU.add)
                    S["ve"].inc(inst)

                @blk.scalar
                def _(sc):
                    S["ve"].wait(sc)
                    inst = sc.activation(cosv[:, :], phs[:, :], AF.Sin)
                    S["sc"].inc(inst)

                @blk.vector
                def _(ve, t0=t0):
                    S["sc"].wait(ve)
                    S["st"].wait(ve)
                    S["vv"].inc(ve.tensor_mul(ktmp[:, :], rscan[:, t0:t0 + nt], p_t[:, :]))
                    S["vv"].wait(ve)
                    ve.tensor_mul(kre_st[:, :], ktmp[:, :], cosv[:, :])
                    inst = ve.tensor_mul(kim_st[:, :], ktmp[:, :], sinv[:, :])
                    S["ve"].inc(inst)

                @blk.sync
                def _(sy, j=j, t0=t0):
                    S["ve"].wait(sy)
                    S["st"].dma(sy.dma_start(
                        out=k_re_d[j * P:(j + 1) * P, t0:t0 + nt], in_=kre_st[:, :]))
                    S["st"].dma(sy.dma_start(
                        out=k_im_d[j * P:(j + 1) * P, t0:t0 + nt], in_=kim_st[:, :]))

        bpctx.close()
        psK = {n: E(nc.psum_tensor("f_k" + n, [128, 512], F32))
               for n in ["pA_re", "pA_im", "pB_re", "pB_im"]}

        def stageA(te, xr, xi, p, KA):
            for c in range(4):
                sl = slice(128 * c, 128 * (c + 1))
                te.matmul(p["pA_re"][:, sl], xr[:, c, :], C["f_re"][0:KA, :], start=True, stop=False)
                te.matmul(p["pA_re"][:, sl], xi[:, c, :], C["f_imN"][0:KA, :], start=False, stop=True)
                te.matmul(p["pA_im"][:, sl], xr[:, c, :], C["f_im"][0:KA, :], start=True, stop=False)
                inst = te.matmul(p["pA_im"][:, sl], xi[:, c, :], C["f_re"][0:KA, :], start=False, stop=True)
            return inst

        def twiddle(ve, p, er, ei, tre, tim):
            ve.tensor_mul(er[:, :], p["pA_re"][:, :], tre[:, :])
            ve.tensor_mul(sb["scr"][:, :], p["pA_im"][:, :], tim[:, :])
            ve.tensor_sub(er[:, :], er[:, :], sb["scr"][:, :])
            ve.tensor_mul(ei[:, :], p["pA_re"][:, :], tim[:, :])
            ve.tensor_mul(sb["scr"][:, :], p["pA_im"][:, :], tre[:, :])
            return ve.tensor_add(ei[:, :], ei[:, :], sb["scr"][:, :])

        def stageB(te, er, ei, p, fre, fim, fimN):
            te.matmul(p["pB_re"][:, :], fre[:, :], er[:, :], start=True, stop=False)
            te.matmul(p["pB_re"][:, :], fimN[:, :], ei[:, :], start=False, stop=True)
            te.matmul(p["pB_im"][:, :], fim[:, :], er[:, :], start=True, stop=False)
            return te.matmul(p["pB_im"][:, :], fre[:, :], ei[:, :], start=False, stop=True)

        for g in range(CH // 4):
            c0 = g * 4

            @blk.gpsimd
            def _(gp, c0=c0):
                S["ve"].wait(gp)
                S["st"].wait(gp)
                S["ld"].dma(gp.dma_start(
                    out=xre[:, :, :], in_=u_re_s[c0:c0 + 4, :].rearrange("c (a b) -> a c b", b=128)))
                S["ld"].dma(gp.dma_start(
                    out=xim[:, :, :], in_=u_im_s[c0:c0 + 4, :].rearrange("c (a b) -> a c b", b=128)))
                S["ld"].dma(gp.dma_start(
                    out=kre[:, :, :], in_=k_re_d[c0:c0 + 4, :].rearrange("c (a b) -> a c b", b=128)))
                S["ld"].dma(gp.dma_start(
                    out=kim[:, :, :], in_=k_im_d[c0:c0 + 4, :].rearrange("c (a b) -> a c b", b=128)))

            @blk.tensor
            def _(te):
                S["ld"].wait(te)
                S["ve"].wait(te)
                stageA(te, xre, xim, ps, 64)
                inst = stageA(te, kre, kim, psK, 64)
                S["te"].inc(inst)

            @blk.vector
            def _(ve):
                S["te"].wait(ve)
                inst = twiddle(ve, ps, sb["e_re"], sb["e_im"], C["t_re"], C["t_im"])
                S["ve"].inc(inst)

            @blk.tensor
            def _(te):
                S["ve"].wait(te)
                inst = stageB(te, sb["e_re"], sb["e_im"], ps, C["f_re"], C["f_im"], C["f_imN"])
                S["te"].inc(inst)

            @blk.vector
            def _(ve):
                inst = twiddle(ve, psK, sb["e_re2"], sb["e_im2"], C["t_re"], C["t_im"])
                S["ve"].inc(inst)

            @blk.tensor
            def _(te):
                S["ve"].wait(te)
                inst = stageB(te, sb["e_re2"], sb["e_im2"], psK, C["f_re"], C["f_im"], C["f_imN"])
                S["te"].inc(inst)

            @blk.vector
            def _(ve):
                S["te"].wait(ve)
                S["st"].wait(ve)
                ur = su_re.rearrange("p c k -> p (c k)"); ui = su_im.rearrange("p c k -> p (c k)")
                kr = sk_re.rearrange("p c k -> p (c k)"); ki2 = sk_im.rearrange("p c k -> p (c k)")
                prr = pr_re.rearrange("p c k -> p (c k)"); pri = pr_im.rearrange("p c k -> p (c k)")
                ve.tensor_copy(ur, ps["pB_re"][:, :])
                ve.tensor_copy(ui, ps["pB_im"][:, :])
                ve.tensor_copy(kr, psK["pB_re"][:, :])
                ve.tensor_copy(ki2, psK["pB_im"][:, :])
                ve.tensor_mul(prr, ur, kr)
                ve.tensor_mul(sb["scr"][:, :], ui, ki2)
                ve.tensor_sub(prr, prr, sb["scr"][:, :])
                ve.tensor_mul(pri, ur, ki2)
                ve.tensor_mul(sb["scr"][:, :], ui, kr)
                inst = ve.tensor_add(pri, pri, sb["scr"][:, :])
                S["ve"].inc(inst)

            @blk.tensor
            def _(te):
                S["ve"].wait(te)
                for c in range(4):
                    sl = slice(128 * c, 128 * (c + 1))
                    te.matmul(ps["pA_re"][:, sl], pr_re[:, c, :], C["fc_re"][:, :], start=True, stop=False)
                    te.matmul(ps["pA_re"][:, sl], pr_im[:, c, :], C["fc_imN"][:, :], start=False, stop=True)
                    te.matmul(ps["pA_im"][:, sl], pr_re[:, c, :], C["fc_im"][:, :], start=True, stop=False)
                    inst = te.matmul(ps["pA_im"][:, sl], pr_im[:, c, :], C["fc_re"][:, :], start=False, stop=True)
                S["te"].inc(inst)

            @blk.vector
            def _(ve):
                S["te"].wait(ve)
                inst = twiddle(ve, ps, sb["e_re"], sb["e_im"], C["tc_re"], C["tc_im"])
                S["ve"].inc(inst)

            @blk.tensor
            def _(te):
                S["ve"].wait(te)
                inst = stageB(te, sb["e_re"], sb["e_im"], ps, C["fc_re"], C["fc_im"], C["fc_imN"])
                S["te"].inc(inst)

            @blk.vector
            def _(ve):
                S["te"].wait(ve)
                ve.tensor_copy(yt_re.rearrange("p c k -> p (c k)"), ps["pB_re"][0:64, :])
                inst = ve.tensor_copy(yt_im.rearrange("p c k -> p (c k)"), ps["pB_im"][0:64, :])
                S["ve"].inc(inst)

            @blk.sync
            def _(sy, c0=c0):
                S["ve"].wait(sy)
                S["st"].dma(sy.dma_start(
                    out=y_loc[c0:c0 + 4, :].rearrange("c (a b) -> a c b", b=128),
                    in_=yt_re[0:32, :, :]))
                S["st"].dma(sy.dma_start(
                    out=y_loc[512 + c0:512 + c0 + 4, :].rearrange("c (a b) -> a c b", b=128),
                    in_=yt_re[32:64, :, :]))
                S["st"].dma(sy.dma_start(
                    out=y_loc[256 + c0:256 + c0 + 4, :].rearrange("c (a b) -> a c b", b=128),
                    in_=yt_im[0:32, :, :]))
                S["st"].dma(sy.dma_start(
                    out=y_loc[768 + c0:768 + c0 + 4, :].rearrange("c (a b) -> a c b", b=128),
                    in_=yt_im[32:64, :, :]))


def emit_G2a(nc, S, z_in, y_g, ysel, cp, gv, wfsT, osc, x4_o, nt=512, cc_gate=None):
    nch = TPC // nt
    blk = S["blk"]
    with ExitStack() as ctx:
        E = ctx.enter_context
        wfs_sb = E(nc.sbuf_tensor("h_wfs", [P, 8, D], BF16))
        osc_sb = E(nc.sbuf_tensor("h_osc", [1, 1], F32))
        osc_col = E(nc.sbuf_tensor("h_oscc", [P, 1], F32))
        ones_row = E(nc.sbuf_tensor("h_ones_r", [1, P], F32))
        sely = E(nc.sbuf_tensor("h_sely", [P, 8], I32))
        specc = E(nc.sbuf_tensor("h_spec", [P, 8, nt], BF16))
        zc = E(nc.sbuf_tensor("h_zc", [P, 8, nt], F32))
        cpc = E(nc.sbuf_tensor("h_cpc", [P, 8, nt], F32))
        gvc = E(nc.sbuf_tensor("h_gvc", [P, 8, nt], F32))
        x4c = E(nc.sbuf_tensor("h_x4c", [P, 8, nt], F32))
        scr = E(nc.sbuf_tensor("h_scr", [P, nt], F32))
        p_y = E(nc.psum_tensor("h_p_y", [P, nt], F32))
        p_o = E(nc.psum_tensor("h_p_o", [P, 1], F32))

        @blk.gpsimd
        def _(gp):
            S["ld"].dma(gp.dma_start(
                out=wfs_sb[:, :, :], in_=wfsT.rearrange("(kt p) d -> p kt d", p=P)))

        @blk.sync
        def _(sy):
            S["ld"].dma(sy.dma_start(out=osc_sb[:, :], in_=osc))
            S["ld"].dma(sy.dma_start(out=sely[:, :], in_=ysel))

        @blk.vector
        def _(ve):
            inst = ve.memset(ones_row[:, :], 1.0)
            S["ve"].inc(inst)

        @blk.tensor
        def _(te):
            S["ld"].wait(te)
            S["ve"].wait(te)
            inst = te.matmul(p_o[:, :], ones_row[:, :], osc_sb[:, :], start=True, stop=True)
            S["te"].inc(inst)

        @blk.vector
        def _(ve):
            S["te"].wait(ve)
            inst = ve.tensor_copy(osc_col[:, :], p_o[:, :])
            S["ve"].inc(inst)

        for ci in range(nch):
            t0 = ci * nt

            @blk.sync
            def _(sy, t0=t0):
                S["st"].wait(sy)
                S["ve"].wait(sy)
                S["te"].wait(sy)
                S["ld"].dma(sy.dma_start(out=zc[:, :, :], in_=fm_ap(z_in, D, t0, nt)))
                S["ld"].dma(sy.dma_start(out=cpc[:, :, :], in_=fm_ap(cp, D, t0, nt)))
                S["ld"].dma(sy.dma_start(out=gvc[:, :, :], in_=fm_ap(gv, D, t0, nt)))

            @blk.gpsimd
            def _(gp, t0=t0):
                S["ve"].wait(gp)
                S["te"].wait(gp)
                S["ld"].wait(gp)
                if cc_gate is not None:
                    gp.wait_ge(cc_gate, 1)
                for kt in range(8):
                    S["gi"].dma(gp.indirect_dma_start(
                        out=specc[:, kt, :], out_offset=None, in_=y_g, element_offset=t0,
                        in_offset=IndirectOffsetOnAxis(ap=sely[:, kt:kt + 1], axis=0)))

            for dt in range(8):
                @blk.tensor
                def _(te, dt=dt):
                    S["ld"].wait(te)
                    S["gi"].wait(te)
                    S["ve"].wait(te)
                    for kt in range(8):
                        inst = te.matmul(p_y[:, :], wfs_sb[:, kt, dt * P:(dt + 1) * P],
                                         specc[:, kt, :], start=(kt == 0), stop=(kt == 7))
                    S["te"].inc(inst)

                @blk.vector
                def _(ve, dt=dt):
                    S["te"].wait(ve)
                    ve.tensor_single_scalar(scr[:, :], p_y[:, :], osc_col[:, 0:1], op=ALU.mult)
                    ve.tensor_mul(scr[:, :], scr[:, :], gvc[:, dt, :])
                    ve.tensor_add(scr[:, :], scr[:, :], cpc[:, dt, :])
                    inst = ve.scalar_tensor_tensor(
                        out=x4c[:, dt, :], in0=scr[:, :], scalar=0.5,
                        in1=zc[:, dt, :], op0=ALU.mult, op1=ALU.add)
                    S["ve"].inc(inst)

            @blk.sync
            def _(sy, t0=t0):
                S["ve"].wait(sy)
                S["st"].dma(sy.dma_start(out=fm_ap(x4_o, D, t0, nt), in_=x4c[:, :, :]))


def emit_G2f(nc, S, pfx, x4, w1T, b1p, w2T, b2p, x5_o,
             x3=None, n2g=None, n2b=None, plain_ln=False, nt=256, share_scratch=False,
             **kwargs):
    F = 4096
    nch = TPC // nt
    blk = S["blk"]
    with ExitStack() as ctx:
        E = ctx.enter_context
        w1_pre = kwargs.get("w1_pre")
        w1_sb = w1_pre if w1_pre is not None else E(nc.sbuf_tensor(pfx + "w1", [P, 8, F], BF16))
        w2_sb = E(nc.sbuf_tensor(pfx + "w2", [P, 32, D], BF16))
        b1_sb = E(nc.sbuf_tensor(pfx + "b1", [P, 32], F32))
        b2_sb = E(nc.sbuf_tensor(pfx + "b2", [P, 8], F32))
        if not plain_ln:
            n2g_sb = E(nc.sbuf_tensor(pfx + "n2g", [P, 8], F32))
            n2b_sb = E(nc.sbuf_tensor(pfx + "n2b", [P, 8], F32))
        if x3 is not None:
            x3c = E(nc.sbuf_tensor(pfx + "x3c", [P, 8, nt], F32))
        x4c = E(nc.sbuf_tensor(pfx + "x4c", [P, 8, nt], F32))
        xnb = E(nc.sbuf_tensor(pfx + "xnb", [P, 8, nt], BF16))
        h_bf = E(nc.sbuf_tensor(pfx + "h", [P, 32, nt], BF16))
        x5c = E(nc.sbuf_tensor(pfx + "x5c", [P, 8, nt], F32))
        xsq = x5c if share_scratch else E(nc.sbuf_tensor(pfx + "xsq", [P, 8, nt], F32))
        L = mk_ln_bufs(nc, E, pfx, nt, slim=share_scratch)
        p_sum = E(nc.psum_tensor(pfx + "p_sum", [1, nt], F32))
        p_sq = E(nc.psum_tensor(pfx + "p_sq", [1, nt], F32))
        p_mu = E(nc.psum_tensor(pfx + "p_mu", [P, nt], F32))
        p_rstd = E(nc.psum_tensor(pfx + "p_rstd", [P, nt], F32))
        p_h2 = [E(nc.psum_tensor(pfx + "p_h0", [P, nt], F32)),
                E(nc.psum_tensor(pfx + "p_h1", [P, nt], F32))]
        p_o2 = [E(nc.psum_tensor(pfx + "p_o0", [P, nt], F32)),
                E(nc.psum_tensor(pfx + "p_o1", [P, nt], F32))]

        @blk.gpsimd
        def _(gp):
            if w1_pre is None:
                S["ld"].dma(gp.dma_start(
                    out=w1_sb[:, :, :], in_=w1T.rearrange("(kt p) f -> p kt f", p=P)))
            S["ld"].dma(gp.dma_start(
                out=w2_sb[:, :, :], in_=w2T.rearrange("(ft p) d -> p ft d", p=P)))

        @blk.sync
        def _(sy):
            S["ld"].dma(sy.dma_start(out=b1_sb[:, :], in_=b1p))
            S["ld"].dma(sy.dma_start(out=b2_sb[:, :], in_=b2p))
            if not plain_ln:
                S["ld"].dma(sy.dma_start(out=n2g_sb[:, :], in_=n2g))
                S["ld"].dma(sy.dma_start(out=n2b_sb[:, :], in_=n2b))

        @blk.vector
        def _(ve):
            ve.memset(L["ones_col"][:, :], 1.0)
            ve.memset(L["ones_row"][:, :], 1.0)
            inst = ve.memset(L["eps_ap"][:, :], LN_EPS)
            S["ve"].inc(inst)

        for ci in range(nch):
            t0 = ci * nt

            @blk.sync
            def _(sy, t0=t0):
                S["st"].wait(sy)
                S["ve"].wait(sy)
                S["sc"].wait(sy)
                S["te"].wait(sy)
                if x3 is not None:
                    S["ld"].dma(sy.dma_start(out=x3c[:, :, :], in_=fm_ap(x3, D, t0, nt)))
                S["ld"].dma(sy.dma_start(out=x4c[:, :, :], in_=fm_ap(x4, D, t0, nt)))

            emit_ln(nc, S, x4c, 8, nt, *ln_args(L, p_sum, p_sq, p_mu, p_rstd, xsq),
                    gates=("ld",), stats_to_psum=share_scratch)
            mub = p_mu if share_scratch else L["mu_b"]
            rsb = p_rstd if share_scratch else L["rstd_b"]

            @blk.vector
            def _(ve):
                S["ve"].wait(ve)
                if share_scratch:
                    S["te"].wait(ve)
                for kt in range(8):
                    ve.tensor_sub(xsq[:, kt, :], x4c[:, kt, :], mub[:, :])
                    if plain_ln:
                        ve.tensor_mul(xsq[:, kt, :], xsq[:, kt, :], rsb[:, :])
                        inst = ve.tensor_copy(xnb[:, kt, :], xsq[:, kt, :])
                    else:
                        ve.tensor_mul(xsq[:, kt, :], xsq[:, kt, :], rsb[:, :])
                        inst = ve.tensor_scalar(
                            xnb[:, kt, :], xsq[:, kt, :],
                            n2g_sb[:, kt:kt + 1], n2b_sb[:, kt:kt + 1],
                            op0=ALU.mult, op1=ALU.add)
                S["ve"].inc(inst)

            for ft in range(32):
                @blk.tensor
                def _(te, ft=ft):
                    S["ve"].wait(te)
                    S["sc"].wait(te, lag=1)
                    for kt in range(8):
                        inst = te.matmul(p_h2[ft % 2][:, :], w1_sb[:, kt, ft * P:(ft + 1) * P],
                                         xnb[:, kt, :], start=(kt == 0), stop=(kt == 7))
                    S["te"].inc(inst)

                @blk.scalar
                def _(sc, ft=ft):
                    S["te"].wait(sc)
                    inst = sc.activation(h_bf[:, ft, :], p_h2[ft % 2][:, :], AF.Gelu,
                                         bias=b1_sb[:, ft:ft + 1])
                    S["sc"].inc(inst)

            for dt in range(8):
                @blk.tensor
                def _(te, dt=dt):
                    S["sc"].wait(te)
                    S["ve"].wait(te, lag=1)
                    for ft in range(32):
                        inst = te.matmul(p_o2[dt % 2][:, :], w2_sb[:, ft, dt * P:(dt + 1) * P],
                                         h_bf[:, ft, :], start=(ft == 0), stop=(ft == 31))
                    S["te"].inc(inst)

                @blk.vector
                def _(ve, dt=dt):
                    S["te"].wait(ve)
                    inst = ve.scalar_tensor_tensor(
                        out=x5c[:, dt, :], in0=p_o2[dt % 2][:, :], scalar=b2_sb[:, dt:dt + 1],
                        in1=x4c[:, dt, :], op0=ALU.add, op1=ALU.add)
                    if x3 is not None:
                        inst = ve.tensor_add(x5c[:, dt, :], x5c[:, dt, :], x3c[:, dt, :])
                    S["ve"].inc(inst)

            @blk.sync
            def _(sy, t0=t0):
                S["ve"].wait(sy)
                S["st"].dma(sy.dma_start(out=fm_ap(x5_o, D, t0, nt), in_=x5c[:, :, :]))


def emit_BIN(nc, S, fw1T, fw2T, bw1T_o, bw2T_o):
    F = 4096
    blk = S["blk"]
    with ExitStack() as ctx:
        E = ctx.enter_context
        wtile = E(nc.sbuf_tensor("n_wtile", [P, F], F32))
        scr = E(nc.sbuf_tensor("n_scr", [P, F], F32))
        alpha = E(nc.sbuf_tensor("n_alpha", [1, F], F32))
        alpha_b = E(nc.sbuf_tensor("n_alpha_b", [P, F], F32))
        bw_st = E(nc.sbuf_tensor("n_bw", [P, F], BF16))
        ones_col = E(nc.sbuf_tensor("n_ones_c", [P, 1], F32))
        ones_row = E(nc.sbuf_tensor("n_ones_r", [1, P], F32))
        p_al = E(nc.psum_tensor("n_p_al", [1, 512], F32))
        p_alb = E(nc.psum_tensor("n_p_alb", [P, 512], F32))

        @blk.vector
        def _(ve):
            ve.memset(ones_col[:, :], 1.0)
            inst = ve.memset(ones_row[:, :], 1.0)
            S["ve"].inc(inst)

        def binarize(wT_ap, bw_ap, nkt, nfree, dtot):
            for f0 in range(0, nfree, 512):
                for kt in range(nkt):
                    @blk.sync
                    def _(sy, kt=kt, f0=f0):
                        S["sc"].wait(sy)
                        S["te"].wait(sy)
                        S["ld"].dma(sy.dma_start(
                            out=wtile[:, 0:512],
                            in_=wT_ap.rearrange("(kt p) f -> p kt f", p=P)[:, kt, f0:f0 + 512]))

                    @blk.scalar
                    def _(sc, kt=kt):
                        S["ld"].wait(sc)
                        inst = sc.activation(scr[:, 0:512], wtile[:, 0:512], AF.Abs)
                        S["sc"].inc(inst)

                    @blk.tensor
                    def _(te, kt=kt, f0=f0):
                        S["sc"].wait(te)
                        inst = te.matmul(p_al[:, 0:512], ones_col[:, :], scr[:, 0:512],
                                         start=(kt == 0), stop=(kt == nkt - 1))
                        S["te"].inc(inst)

                @blk.scalar
                def _(sc, f0=f0):
                    S["te"].wait(sc)
                    inst = sc.mul(alpha[:, f0:f0 + 512], p_al[:, 0:512], 1.0 / dtot)
                    S["sc"].inc(inst)

            for f0 in range(0, nfree, 512):
                @blk.tensor
                def _(te, f0=f0):
                    S["sc"].wait(te)
                    S["ve"].wait(te)
                    inst = te.matmul(p_alb[:, 0:512], ones_row[:, :], alpha[0:1, f0:f0 + 512],
                                     start=True, stop=True)
                    S["te"].inc(inst)

                @blk.vector
                def _(ve, f0=f0):
                    S["te"].wait(ve)
                    inst = ve.tensor_copy(alpha_b[:, f0:f0 + 512], p_alb[:, 0:512])
                    S["ve"].inc(inst)

            for kt in range(nkt):
                @blk.sync
                def _(sy, kt=kt):
                    S["ve"].wait(sy)
                    S["sc"].wait(sy)
                    S["st"].wait(sy)
                    S["ld"].dma(sy.dma_start(
                        out=wtile[:, 0:nfree],
                        in_=wT_ap.rearrange("(kt p) f -> p kt f", p=P)[:, kt, :]))

                @blk.scalar
                def _(sc, kt=kt):
                    S["ld"].wait(sc)
                    inst = sc.activation(scr[:, 0:nfree], wtile[:, 0:nfree], AF.Sign)
                    S["sc"].inc(inst)

                @blk.vector
                def _(ve, kt=kt):
                    S["sc"].wait(ve)
                    inst = ve.tensor_mul(bw_st[:, 0:nfree], scr[:, 0:nfree],
                                         alpha_b[:, 0:nfree])
                    S["ve"].inc(inst)

                @blk.sync
                def _(sy, kt=kt):
                    S["ve"].wait(sy)
                    S["st"].dma(sy.dma_start(
                        out=bw_ap.rearrange("(kt p) f -> p kt f", p=P)[:, kt, :],
                        in_=bw_st[:, 0:nfree]))

        binarize(fw1T, bw1T_o, 8, 4096, D)
        binarize(fw2T, bw2T_o, 32, D, 4096)


def build_fused():
    nc = bass.Bass("TRN2", target_bir_lowering=False)

    def inp(name, shape, dt=F32):
        return nc.dram_tensor(name, shape, dt, kind="ExternalInput").ap()

    def intl(name, shape, dt=F32):
        return nc.dram_tensor(name, shape, dt, kind="Internal").ap()

    x = inp("x", [D, TPC])
    whgT = inp("whgT", [D, 3072])
    bhg = inp("bhg", [P, 24])
    h0p = inp("h0p", [P, 6])
    absel = inp("absel", [P, 24], I32)
    woutT = inp("woutT", [1536, D])
    bout = inp("bout", [P, 8])
    lng = inp("lng", [P, 8])
    lnb = inp("lnb", [P, 8])
    centT = inp("centT", [D, 8])
    hsel = inp("hsel", [P, 12], I32)
    identb = inp("identb", [P, P], BF16)
    triu = inp("triu", [P, P])
    iota_f = inp("iota_f", [P, 256])
    iota_i = inp("iota_i", [P, 256], I32)
    tksel = inp("tksel", [P, 32], I32)
    w1e = inp("w1e", [D, 4096])
    b1p = inp("b1p", [P, 32])
    w2e = inp("w2e", [4096, D])
    b2p = inp("b2p", [P, 8])
    wcatT = inp("wcatT", [D, 37 * P])
    bcat = inp("bcat", [P, 37])
    ng = inp("ng", [P, 8])
    nb = inp("nb", [P, 8])
    ufsel = inp("ufsel", [P, 8], I32)
    tbsel = inp("tbsel", [4, 1], I32)
    ld_loc = inp("ld_loc", [1, 256])
    fr_loc = inp("fr_loc", [1, 256])
    dt_loc = inp("dt_loc", [1, 256])
    fi_roll = inp("fi_roll", [1, 512])
    iota1 = inp("iota1", [1, SEQ])
    cst = {n: inp(n, [128, 512 if n.startswith("t") else 128], BF16)
           for n in ["f_re", "f_im", "f_imN", "fc_re", "fc_im", "fc_imN",
                     "t_re", "t_im", "tc_re", "tc_im"]}
    ysel = inp("ysel", [P, 8], I32)
    wfsT = inp("wfsT", [D, D])
    osc = inp("osc", [1, 1])
    n2g = inp("n2g", [P, 8])
    n2b = inp("n2b", [P, 8])
    w1T = inp("w1T", [D, 4096])
    b1p2 = inp("b1p2", [P, 32])
    w2T = inp("w2T", [4096, D])
    b2p2 = inp("b2p2", [P, 8])
    fw1T = inp("fw1T", [D, 4096])
    fw2T = inp("fw2T", [4096, D])
    fb1p = inp("fb1p", [P, 32])
    fb2p = inp("fb2p", [P, 8])

    ab_loc = intl("ab_loc", [3072, TPC])
    ab_g = intl("ab_g", [6144, TPC])
    h_loc = intl("h_loc", [1536, TPC], BF16)
    h_g = intl("h_g", [3072, TPC], BF16)
    x2_d = intl("x2_d", [D, TPC])
    feat_d = intl("feat_d", [D, TPC])
    sc_d = intl("sc_d", [8, TPC])
    scG = nc.dram_tensor("scG", [64, TPC], F32, kind="Internal", addr_space="Shared").ap()
    normtm = intl("normtm", [TPC, D], BF16)
    normG = nc.dram_tensor("normG", [8 * TPC, D], BF16, kind="Internal", addr_space="Shared").ap()
    t2s_d = intl("t2s_d", [P, 256], I32)
    t2sf_d = intl("t2sf_d", [32768, 1], I32)
    s2t_d = intl("s2t_d", [32768, 1], I32)
    stag = intl("stag", [TPC, D], BF16)
    slabtm = intl("slabtm", [TPC, D], BF16)
    ffnG = nc.dram_tensor("ffnG", [8 * TPC, D], BF16, kind="Internal", addr_space="Shared").ap()
    stag2 = intl("stag2", [TPC, D], BF16)
    x3_d = intl("x3_d", [D, TPC])
    z_d = intl("z_d", [D, TPC])
    cp_d = intl("cp_d", [D, TPC])
    gv_d = intl("gv_d", [D, TPC])
    ubf_loc = intl("ubf_loc", [1024, TPC], BF16)
    ubf_g = intl("ubf_g", [2048, TPC], BF16)
    tb_loc = intl("tb_loc", [5, TPC])
    tb_g = intl("tb_g", [10, TPC])
    u_re_s = intl("u_re_s", [256, SEQ], BF16)
    u_im_s = intl("u_im_s", [256, SEQ], BF16)
    y_loc = intl("y_loc", [1024, TPC], BF16)
    y_g = intl("y_g", [2048, TPC], BF16)
    x4_d = intl("x4_d", [D, TPC])
    x5_d = intl("x5_d", [D, TPC])
    bw1T_d = intl("bw1T_d", [D, 4096], BF16)
    bw2T_d = intl("bw2T_d", [4096, D], BF16)
    xout = nc.dram_tensor("xout", [D, TPC], F32, kind="ExternalOutput").ap()

    with ExitStack() as ctx:
        E = ctx.enter_context
        blk = E(nc.Block())
        S = {"blk": blk}
        for k in SEMS:
            S[k] = Sem(ctx, nc, "s_" + k)
        cch = {}
        for k in ("ab", "h", "nrm", "sco", "slab", "uf", "tb", "y"):
            cch[k] = E(nc.semaphore("cc_" + k))

        def cc(name, groups, i, o):
            @blk.gpsimd
            def _(gp):
                S["st"].wait(gp)
                gp.collective_compute(
                    "AllGather", ALU.bypass, replica_groups=groups,
                    ins=[i], outs=[o]).then_inc(cch[name], 1)

        emit_A(nc, S, x, whgT, bhg, ab_loc)
        cc("ab", PG, ab_loc, ab_g)
        barrier(S)

        emit_BIN(nc, S, fw1T, fw2T, bw1T_d, bw2T_d)
        barrier(S, [cch["ab"]])

        emit_B(nc, S, ab_g, h0p, absel, h_loc)
        cc("h", PG, h_loc, h_g)
        barrier(S)

        emit_C(nc, S, x, h_g, hsel, woutT, bout, lng, lnb, centT, identb,
               x2_d, feat_d, sc_d, normtm, cc_gate=cch["h"])
        cc("sco", AG8, sc_d, scG)
        cc("nrm", AG8, normtm, normG)
        barrier(S, [cch["sco"]])

        ectx = ExitStack()
        wbufs = emit_E_weights(nc, S, ectx, w1e, b1p, w2e, b2p)
        emit_D(nc, S, scG, triu, iota_f, iota_i, t2s_d, t2sf_d, s2t_d)
        barrier(S, [cch["nrm"]])

        emit_E(nc, S, normG, s2t_d, tksel, wbufs, identb, stag, slabtm)
        ectx.close()
        cc("slab", AG8, slabtm, ffnG)
        barrier(S)

        emit_G1(nc, S, x2_d, feat_d, ffnG, t2sf_d, tksel, wcatT, bcat, ng, nb, stag2,
                x3_d, z_d, cp_d, gv_d, ubf_loc, tb_loc, cc_gate=cch["slab"])
        cc("uf", PG, ubf_loc, ubf_g)
        cc("tb", PG, tb_loc, tb_g)
        barrier(S)

        emit_F(nc, S, ubf_g, tb_g, ufsel, tbsel, ld_loc, fr_loc, dt_loc, fi_roll,
               iota1, cst, u_re_s, u_im_s, y_loc, cc_uf=cch["uf"], cc_tb=cch["tb"])
        cc("y", PG, y_loc, y_g)
        barrier(S)

        ictx = ExitStack()
        iw1 = ictx.enter_context(nc.sbuf_tensor("i_w1", [P, 8, 4096], BF16))

        @blk.gpsimd
        def _(gp):
            S["ld"].dma(gp.dma_start(
                out=iw1[:, :, :], in_=w1T.rearrange("(kt p) f -> p kt f", p=P)))

        emit_G2a(nc, S, z_d, y_g, ysel, cp_d, gv_d, wfsT, osc, x4_d, cc_gate=cch["y"])
        barrier(S)

        emit_G2f(nc, S, "i_", x4_d, w1T, b1p2, w2T, b2p2, x5_d,
                 x3=x3_d, n2g=n2g, n2b=n2b, w1_pre=iw1)
        ictx.close()
        barrier(S)

        emit_G2f(nc, S, "j_", x5_d, bw1T_d, fb1p, bw2T_d, fb2p, xout,
                 x3=None, plain_ln=True, nt=256, share_scratch=True)
        barrier(S)
    return nc


def _t(a):
    return np.ascontiguousarray(a.T)


def _pp(v, ntile):
    return np.ascontiguousarray(np.asarray(v).reshape(ntile, 128).T)


_NC_CACHE = []


def kernel(**ins):
    cores = list(range(8))
    if not _NC_CACHE:
        _NC_CACHE.append(build_fused())
    nc = _NC_CACHE[0]

    x = np.asarray(ins["L0_attn_x"])
    x_fm = [_t(x[b]) for b in range(4)]

    pcol = np.arange(P).reshape(P, 1)
    whgT = _t(ins["L0_attn_W_hg"])
    bhgp = _pp(ins["L0_attn_b_hg"], 24)
    h0 = np.asarray(ins["L0_attn_h0"])
    woutT = _t(ins["L0_attn_W_out"])
    boutp = _pp(ins["L0_attn_b_out"], 8)
    lngp = _pp(ins["L0_ffn_ln_g"], 8)
    lnbp = _pp(ins["L0_ffn_ln_b"], 8)
    centT = _t(ins["L0_ffn_centroids"])
    identb = np.eye(P, dtype=np.float32).astype(BF)
    triu = np.ascontiguousarray(np.triu(np.ones((P, P), np.float32), 1))
    pg, fg = np.meshgrid(np.arange(P), np.arange(256), indexing="ij")
    iota = (pg * 256 + fg)
    w1 = np.asarray(ins["L0_ffn_w1"]); b1 = np.asarray(ins["L0_ffn_b1"])
    w2 = np.asarray(ins["L0_ffn_w2"]); b2 = np.asarray(ins["L0_ffn_b2"])
    wcatT = np.concatenate([
        _t(ins["L1_attn_Wconcept"]), _t(ins["L1_attn_Wcg"]), _t(ins["L1_attn_Wsp"]),
        _t(ins["L1_attn_Wgate"]), _t(ins["L1_attn_Wfg"]), _t(ins["L1_attn_Wband"]),
        _t(ins["L1_attn_Wtg"]), np.zeros((D, 123), np.float32)], axis=1)
    bcatv = np.concatenate([
        np.asarray(ins["L1_attn_bconcept"]), np.asarray(ins["L1_attn_bcg"]),
        np.zeros(D, np.float32), np.asarray(ins["L1_attn_bgate"]),
        np.asarray(ins["L1_attn_bfg"]), np.asarray(ins["L1_attn_bband"]),
        np.asarray(ins["L1_attn_btg"]), np.zeros(123, np.float32)])
    bcatp = _pp(bcatv, 37)
    ngp = _pp(ins["L1_attn_norm_g"], 8)
    nbp = _pp(ins["L1_attn_norm_b"], 8)
    consts = fft_host_consts()
    ld_ = np.asarray(ins["L1_attn_log_decay"]); fi_ = np.asarray(ins["L1_attn_freq_importance"])
    fr_ = np.asarray(ins["L1_attn_frequencies"]); dt_ = np.asarray(ins["L1_attn_dt"])
    iota1 = np.arange(1, SEQ + 1, dtype=np.float32).reshape(1, SEQ)
    wfsT = _t(ins["L1_attn_Wfs"])
    oscv = np.ascontiguousarray(np.asarray(ins["L1_attn_output_scale"]).reshape(1, 1))
    n2gp = _pp(ins["L1_attn_norm2_g"], 8)
    n2bp = _pp(ins["L1_attn_norm2_b"], 8)
    w1Tm = _t(ins["L1_attn_W1"]); b1p2 = _pp(ins["L1_attn_b1"], 32)
    w2Tm = _t(ins["L1_attn_W2"]); b2p2 = _pp(ins["L1_attn_b2"], 8)
    fw1T = _t(ins["L1_ffn_w1"]); fb1p = _pp(ins["L1_ffn_b1"], 32)
    fw2T = _t(ins["L1_ffn_w2"]); fb2p = _pp(ins["L1_ffn_b2"], 8)

    im = []
    for c in cores:
        b, h = c // 2, c % 2
        fh = h
        absel = np.empty((P, 24), np.int32)
        for j in range(6):
            base = 768 * h + j * P
            absel[:, 4 * j + 0] = base + pcol[:, 0]
            absel[:, 4 * j + 1] = 3072 + base + pcol[:, 0]
            absel[:, 4 * j + 2] = 1536 + base + pcol[:, 0]
            absel[:, 4 * j + 3] = 4608 + base + pcol[:, 0]
        hsel = np.empty((P, 12), np.int32)
        for ct in range(12):
            hsel[:, ct] = 1536 * (ct // 6) + 768 * h + (ct % 6) * P + pcol[:, 0]
        tksel = np.empty((P, 32), np.int32)
        for r in range(32):
            tksel[:, r] = c * TPC + r * P + pcol[:, 0]
        ufsel = np.empty((P, 8), np.int32)
        for t in range(8):
            plane, hh, j2 = t // 4, (t // 2) % 2, t % 2
            ufsel[:, t] = hh * 1024 + plane * 512 + 256 * fh + j2 * P + pcol[:, 0]
        tbsel = np.array([[2 * fh], [2 * fh + 1], [5 + 2 * fh], [6 + 2 * fh]], np.int32)
        ysel = np.empty((P, 8), np.int32)
        ybase = [h * 512 + 0, h * 512 + 128, 1024 + h * 512 + 0, 1024 + h * 512 + 128,
                 h * 512 + 256, h * 512 + 384, 1024 + h * 512 + 256, 1024 + h * 512 + 384]
        for kt in range(8):
            ysel[:, kt] = ybase[kt] + pcol[:, 0]
        fs = slice(256 * fh, 256 * (fh + 1))
        d = {
            "x": np.ascontiguousarray(x_fm[b][:, h * TPC:(h + 1) * TPC]),
            "whgT": whgT, "bhg": bhgp,
            "h0p": np.ascontiguousarray(h0[768 * h:768 * (h + 1)].reshape(6, P).T),
            "absel": absel, "woutT": woutT, "bout": boutp, "lng": lngp, "lnb": lnbp,
            "centT": centT, "hsel": hsel, "identb": identb,
            "triu": triu, "iota_f": iota.astype(np.float32), "iota_i": iota.astype(np.int32),
            "tksel": tksel,
            "w1e": np.ascontiguousarray(w1[c]), "b1p": _pp(b1[c], 32),
            "w2e": np.ascontiguousarray(w2[c]), "b2p": _pp(b2[c], 8),
            "wcatT": np.ascontiguousarray(wcatT), "bcat": bcatp, "ng": ngp, "nb": nbp,
            "ufsel": ufsel, "tbsel": tbsel,
            "ld_loc": np.ascontiguousarray(ld_[fs].reshape(1, 256)),
            "fr_loc": np.ascontiguousarray(fr_[fs].reshape(1, 256)),
            "dt_loc": np.ascontiguousarray(dt_[fs].reshape(1, 256)),
            "fi_roll": np.ascontiguousarray(np.roll(fi_, -256 * fh).reshape(1, 512)),
            "iota1": iota1,
            "ysel": ysel, "wfsT": wfsT, "osc": oscv,
            "n2g": n2gp, "n2b": n2bp, "w1T": w1Tm, "b1p2": b1p2, "w2T": w2Tm, "b2p2": b2p2,
            "fw1T": fw1T, "fw2T": fw2T, "fb1p": fb1p, "fb2p": fb2p,
        }
        d.update(consts)
        im.append(d)

    r = run_bass_kernel_spmd(nc, im, core_ids=cores).results
    out = np.empty((4, SEQ, D), np.float32)
    for c in cores:
        b, h = c // 2, c % 2
        out[b, h * TPC:(h + 1) * TPC, :] = np.asarray(r[c]["xout"], np.float32).T
    return out
